# version 09
# baseline (speedup 1.0000x reference)
import math
import os
os.environ.setdefault("TRN_ALLOW_ICI_ALL_TO_ALL", "1")
import numpy as np

import concourse.bass as bass
import concourse.mybir as mybir
from concourse.bass_utils import run_bass_kernel_spmd
from concourse.library_config import all_libraries

F32 = mybir.dt.float32
U32 = mybir.dt.uint32
BF16 = mybir.dt.bfloat16
A = mybir.AluOpType
ACT = mybir.ActivationFunctionType
AX = mybir.AxisListType
LIBS = {l.name: l for l in all_libraries}

B, T, D = 4, 2048, 2048
NC = 8
TOK = 1024
CH = D // 128
DFF = 8192
LN2F = float(np.float32(np.log(2.0)))
NEG = -1.0e30


ENGINES = ("sync", "tensor", "vector", "scalar", "gpsimd")


class Plan:
    def __init__(self, nc):
        self.nc = nc
        self.ops = {e: [] for e in ENGINES}
        self.cnt = {}
        self.sems = {}

    def sem(self, name):
        if name not in self.sems:
            h = self.nc.semaphore(name).__enter__()
            self.sems[name] = h
            self.cnt[name] = 0
        return self.sems[name]

    def count(self, name):
        self.sem(name)
        return self.cnt[name]

    def emit(self, eng, fn, wait=(), inc=None, inc_by=1):
        for s, v in wait:
            self.sem(s)
        if inc:
            self.sem(inc)
            self.cnt[inc] += inc_by
        waits = tuple(wait)
        inc_name = inc
        n = inc_by

        def op(h, _fn=fn, _waits=waits, _inc=inc_name, _n=n):
            for s, v in _waits:
                h.wait_ge(self.sems[s], v)
            inst = _fn(h)
            if _inc is not None:
                inst.then_inc(self.sems[_inc], _n)
        self.ops[eng].append(op)
        return self.cnt[inc] if inc else None

    def wait_only(self, eng, wait):
        def op(h, _waits=tuple(wait)):
            for s, v in _waits:
                h.wait_ge(self.sems[s], v)
        self.ops[eng].append(op)

    def drain(self, eng):
        self.ops[eng].append(lambda h: h.drain())

    def realize(self):
        nc = self.nc
        with nc.Block() as block:
            @block.sync
            def _(e):
                for op in self.ops["sync"]:
                    op(e)

            @block.tensor
            def _(e):
                for op in self.ops["tensor"]:
                    op(e)

            @block.vector
            def _(e):
                for op in self.ops["vector"]:
                    op(e)

            @block.scalar
            def _(e):
                for op in self.ops["scalar"]:
                    op(e)

            @block.gpsimd
            def _(e):
                for op in self.ops["gpsimd"]:
                    op(e)



def build_nc(debug_taps=()):
    nc = bass.Bass("TRN2", target_bir_lowering=False, debug=False)

    di = {}

    def inp(name, shape, dtype=F32):
        di[name] = nc.dram_tensor(name, shape, dtype, kind="ExternalInput")
        return di[name]

    x_fm = inp("x_fm", [D, TOK])
    wzT = inp("wzT", [D, D])
    whT = inp("whT", [D, D])
    bz = inp("bz", [128, CH])
    bh = inp("bh", [128, CH])
    lnw = inp("lnw", [128, CH])
    lnb = inp("lnb", [128, CH])
    h0t = inp("h0t", [128, CH])
    pm0 = inp("pm0", [128, CH])
    cmask = inp("cmask", [128, NC])
    wsel = inp("wsel", [128, 1])
    ident = inp("ident", [128, 128])
    onesb_i = inp("onesb", [128, 128], BF16)
    tribT_i = inp("tribT", [128, 128], BF16)
    identb_i = inp("identb", [128, 128], BF16)

    out = nc.dram_tensor("out", [D, TOK], F32, kind="ExternalOutput")

    taps = {}

    def tap(name, shape):
        if name in debug_taps:
            taps[name] = nc.dram_tensor("tap_" + name, shape, F32,
                                        kind="ExternalOutput")
            return taps[name]
        return None

    ag4_in = nc.dram_tensor("ag4_in", [D], F32)
    ag4_out = nc.dram_tensor("ag4_out", [NC, D], F32)
    ar5_in = nc.dram_tensor("ar5_in", [8], F32)
    ar5_out = nc.dram_tensor("ar5_out", [8], F32)
    bc_scr = nc.dram_tensor("bc_scr", [4, TOK], F32)
    xnh_sp = nc.dram_tensor("xnh_sp", [D, TOK], BF16)
    xnl_sp = nc.dram_tensor("xnl_sp", [D, TOK], BF16)

    p = Plan(nc)
    import contextlib
    stack = contextlib.ExitStack()
    frees = {}

    def sb(name, shape, dtype=F32):
        cm = nc.sbuf_tensor(name, shape, dtype)
        h = cm.__enter__()
        frees[name] = cm
        return h

    def sbfree(name):
        frees.pop(name).__exit__(None, None, None)

    def ps(name, shape, dtype=F32):
        cm = nc.psum_tensor(name, shape, dtype)
        h = cm.__enter__()
        frees["PS:" + name] = cm
        return h

    def psfree(name):
        frees.pop("PS:" + name).__exit__(None, None, None)

    def free_back_to(name):
        keys = list(frees.keys())
        idx = keys.index(name)
        for k in reversed(keys[idx:]):
            frees.pop(k).__exit__(None, None, None)

    c_ident = sb("c_ident", [128, 128])
    c_identb = sb("c_identb", [128, 128], BF16)
    c_onesb = sb("c_onesb", [128, 128], BF16)
    c_triTb = sb("c_triTb", [128, 128], BF16)
    c_two = sb("c_two", [128, 1])
    c_eps = sb("c_eps", [128, 1])
    s_h0 = sb("s_h0", [128, CH])
    s_pm0 = sb("s_pm0", [128, CH])
    s_cmask = sb("s_cmask", [128, NC])
    s_wsel = sb("s_wsel", [128, 1])
    s_lnw = sb("s_lnw", [128, CH])
    s_lnb = sb("s_lnb", [128, CH])
    s_bz = sb("s_bz", [128, CH])
    s_bh = sb("s_bh", [128, CH])

    def ld(dst, src_ap, inc="L"):
        return p.emit("sync", lambda h, d=dst, s=src_ap: h.dma_start(out=d, in_=s), inc=inc, inc_by=16)

    ld(c_ident[:], ident.ap())
    ld(c_identb[:], identb_i.ap())
    ld(c_onesb[:], onesb_i.ap())
    ld(c_triTb[:], tribT_i.ap())
    ld(s_h0[:], h0t.ap())
    ld(s_pm0[:], pm0.ap())
    ld(s_cmask[:], cmask.ap())
    ld(s_wsel[:], wsel.ap())
    for name, dst in ((lnw, s_lnw), (lnb, s_lnb), (bz, s_bz), (bh, s_bh)):
        ld(dst[:], name.ap())
    w_const = p.cnt["L"]

    p.emit("vector", lambda h: h.memset(c_two[:], 2.0))
    p.emit("vector", lambda h: h.memset(c_eps[:], 1e-38))
    p.emit("vector", lambda h: h.nop(), wait=[("L", w_const)], inc="V")
    w_cconst = p.cnt["V"]

    s_sqb = sb("s_sqb", [128, 2 * 512], BF16)
    s_xb = sb("s_xb", [128, 2 * 512], BF16)
    s_sum = sb("s_sum", [1, TOK])
    s_ssq = sb("s_ssq", [1, TOK])
    s_rstd = sb("s_rstd", [1, TOK])
    s_mub = sb("s_mub", [128, TOK])
    s_rstdb = sb("s_rstdb", [128, TOK])
    c_eps5 = sb("c_eps5", [1, 1])

    s_xnh = sb("s_xnh", [128, CH * TOK], BF16)
    s_xnl = sb("s_xnl", [128, CH * TOK], BF16)

    s_x = sb("s_x", [128, CH * 512])

    def xc(k):
        return s_x[:, k * 512:(k + 1) * 512]


    def xnc(k):
        return s_xn[:, k * TOK:(k + 1) * TOK]

    ps_sum = ps("ps_sum", [1, 512])
    ps_ssq = ps("ps_ssq", [1, 512])
    s_mu = s_sum

    def emit_ln_fm(src_c, dst_c, wait_src, eps_col, gam_col=None, beta_col=None,
                   ncols=TOK):
        t_base = p.count("T")
        j = 0
        for th in range(ncols // 512):
            sl = slice(th * 512, (th + 1) * 512)
            for k in range(CH):
                pp = slice((j % 2) * 512, (j % 2) * 512 + 512)
                w = [wait_src] if (th == 0 and k == 0) else []
                w = w + [("T", t_base + j - 1 if j >= 2 else t_base)]
                p.emit("vector", lambda h, k=k, sl=sl, pp=pp: h.tensor_copy(
                    s_xb[:, pp], src_c(k)[:, sl]), wait=w, inc="V")
                p.emit("scalar", lambda h, k=k, sl=sl, pp=pp: h.activation(
                    s_sqb[:, pp], src_c(k)[:, sl], ACT.Square), wait=w, inc="S")
                p.emit("tensor", lambda h, k=k, pp=pp: h.matmul(
                    ps_sum[:], c_onesb[:, 0:1], s_xb[:, pp], start=(k == 0), stop=False),
                    wait=[("V", p.cnt["V"]), ("S", p.cnt["S"])])
                p.emit("tensor", lambda h, k=k, pp=pp, last=(k == CH - 1): h.matmul(
                    ps_ssq[:], c_onesb[:, 0:1], s_sqb[:, pp], start=(k == 0), stop=last),
                    inc="T")
                j += 1
            p.emit("vector", lambda h, sl=sl: h.tensor_copy(s_sum[:, sl], ps_sum[:]),
                   wait=[("T", p.cnt["T"])])
            p.emit("vector", lambda h, sl=sl: h.tensor_copy(s_ssq[:, sl], ps_ssq[:]), inc="V")
        p.drain("vector")
        p.emit("vector", lambda h: h.tensor_scalar(s_mu[:, 0:ncols], s_sum[:, 0:ncols], 1.0 / D, None, op0=A.mult))
        p.drain("vector")
        p.emit("vector", lambda h: h.tensor_tensor(
            s_rstd[:, 0:ncols], s_mu[:, 0:ncols], s_mu[:, 0:ncols], op=A.mult), inc="V")
        p.drain("vector")
        p.emit("vector", lambda h: h.scalar_tensor_tensor(
            s_rstd[:, 0:ncols], s_ssq[:, 0:ncols], 1.0 / D, s_rstd[:, 0:ncols],
            op0=A.mult, op1=A.subtract), inc="V")
        p.drain("vector")
        p.emit("scalar", lambda h: h.activation(s_rstd[:, 0:ncols], s_rstd[:, 0:ncols], ACT.Sqrt,
                                                bias=eps_col, scale=1.0),
               wait=[("V", p.cnt["V"])], inc="S")
        p.emit("vector", lambda h: h.reciprocal(s_rstd[:, 0:ncols], s_rstd[:, 0:ncols]),
               wait=[("S", p.cnt["S"])], inc="V")
        p.drain("vector")
        p.emit("gpsimd", lambda h: h.dma_start(out=bc_scr.ap()[0:1, 0:ncols], in_=s_mu[:, 0:ncols]),
               wait=[("V", p.cnt["V"])], inc="P", inc_by=16)
        p.emit("gpsimd", lambda h: h.dma_start(out=bc_scr.ap()[1:2, 0:ncols], in_=s_rstd[:, 0:ncols]),
               inc="P", inc_by=16)
        wg = p.cnt["P"]
        p.emit("gpsimd", lambda h: h.dma_start(
            out=s_mub[:, 0:ncols], in_=bc_scr.ap()[0:1, 0:ncols].broadcast_to([128, ncols])),
            wait=[("P", wg)], inc="P", inc_by=16)
        p.emit("gpsimd", lambda h: h.dma_start(
            out=s_rstdb[:, 0:ncols], in_=bc_scr.ap()[1:2, 0:ncols].broadcast_to([128, ncols])),
            inc="P", inc_by=16)
        wP = p.cnt["P"]
        for k in range(CH):
            p.emit("vector", lambda h, k=k: h.tensor_tensor(dst_c(k), src_c(k), s_mub[:, 0:ncols], op=A.subtract),
                   wait=[("P", wP)] if k == 0 else [])
        p.drain("vector")
        for k in range(CH):
            p.emit("vector", lambda h, k=k: h.tensor_tensor(dst_c(k), dst_c(k), s_rstdb[:, 0:ncols], op=A.mult))
        p.drain("vector")
        if gam_col is not None:
            for k in range(CH):
                p.emit("vector", lambda h, k=k: h.tensor_scalar(
                    dst_c(k), dst_c(k), gam_col(k), beta_col(k), op0=A.mult, op1=A.add))
            p.drain("vector")
        p.emit("vector", lambda h: h.nop(), inc="V")
        return p.cnt["V"]

    p.emit("vector", lambda h: h.memset(c_eps5[:], 1e-5), inc="V")
    p.drain("vector")

    def xnh(k):
        return s_xnh[:, k * TOK:(k + 1) * TOK]

    def xnl(k):
        return s_xnl[:, k * TOK:(k + 1) * TOK]

    for hf in range(2):
        hsl = slice(hf * 512, hf * 512 + 512)
        for k in range(CH):
            ld(xc(k), x_fm.ap()[k * 128:(k + 1) * 128, hsl],
               )
        w_xload = p.cnt["L"]
        w_half = emit_ln_fm(xc, xc, ("L", w_xload), c_eps5[0:1, 0:1], ncols=512)
        for k in range(CH):
            p.emit("vector", lambda h, k=k, hsl=hsl: h.tensor_copy(xnh(k)[:, hsl], xc(k)),
                   wait=[("V", w_half)] if k == 0 else [])
        p.drain("vector")
        for k in range(CH):
            p.emit("vector", lambda h, k=k, hsl=hsl: h.tensor_tensor(
                xnl(k)[:, hsl], xc(k), xnh(k)[:, hsl], op=A.subtract),
                inc="V" if k == CH - 1 else None)
        p.drain("vector")
        p.wait_only("sync", [("V", p.cnt["V"])])
    w_xnsplit = p.cnt["V"]
    p.emit("gpsimd", lambda h: h.dma_start(out=xnh_sp.ap(), in_=s_xnh[:]),
           wait=[("V", w_xnsplit)], inc="G", inc_by=16)
    p.emit("gpsimd", lambda h: h.dma_start(out=xnl_sp.ap(), in_=s_xnl[:]),
           inc="G", inc_by=16)
    w_xnspill = p.cnt["G"]
    sbfree("s_x")

    a_spill = nc.dram_tensor("a_spill", [D, TOK], F32)
    s_spill = nc.dram_tensor("s_spill", [D, TOK], F32)
    NTOT = float(B * (T + 1) * D)

    c_half = sb("c_half", [128, 1])
    c_one = sb("c_one", [128, 1])
    p.emit("vector", lambda h: h.memset(c_half[:], 0.5))
    p.emit("vector", lambda h: h.memset(c_one[:], 1.0), inc="V")
    p.drain("vector")

    def emit_logsig(src, dst, tmp, wait=()):
        p.emit("scalar", lambda h: h.activation(tmp, src, ACT.Abs), wait=list(wait), inc="S")
        p.drain("scalar")
        p.emit("scalar", lambda h: h.activation(tmp, tmp, ACT.Exp, scale=-1.0), inc="S")
        p.drain("scalar")
        p.emit("scalar", lambda h: h.activation(tmp, tmp, ACT.Ln, bias=c_one[:, 0:1], scale=1.0), inc="S")
        wS = p.cnt["S"]
        p.emit("vector", lambda h: h.tensor_scalar(dst, src, 0.0, None, op0=A.min),
               wait=[("S", wS)])
        p.drain("vector")
        p.emit("vector", lambda h: h.tensor_tensor(dst, dst, tmp, op=A.subtract), inc="V")
        p.drain("vector")
        return p.cnt["V"]

    s_u0 = sb("s_u0", [128, CH])
    s_gt1 = sb("s_gt1", [128, CH])
    w0 = emit_logsig(s_h0[:], s_u0[:], s_gt1[:],
                     wait=[("L", w_const), ("V", w_xnsplit)])
    p.emit("vector", lambda h: h.tensor_scalar(s_gt1[:], s_h0[:], 0.0, None, op0=A.max), inc="V")
    p.drain("vector")
    p.emit("scalar", lambda h: h.activation(s_gt1[:], s_gt1[:], ACT.Ln,
                                            bias=c_half[:, 0:1], scale=1.0),
           wait=[("V", p.cnt["V"])], inc="S")
    s_msk0 = sb("s_msk0", [128, CH])
    p.emit("vector", lambda h: h.tensor_scalar(s_msk0[:], s_h0[:], 0.0, None, op0=A.is_ge),
           wait=[("S", p.cnt["S"])])
    p.drain("vector")
    p.emit("vector", lambda h: h.tensor_tensor(s_gt1[:], s_gt1[:], s_u0[:], op=A.subtract))
    p.drain("vector")
    p.emit("vector", lambda h: h.tensor_tensor(s_gt1[:], s_gt1[:], s_msk0[:], op=A.mult))
    p.drain("vector")
    p.emit("vector", lambda h: h.tensor_tensor(s_u0[:], s_u0[:], s_gt1[:], op=A.add))
    p.drain("vector")
    p.emit("vector", lambda h: h.tensor_tensor(s_u0[:], s_u0[:], s_pm0[:], op=A.add), inc="V")
    p.drain("vector")
    w_u0 = p.cnt["V"]

    t_u0 = tap("u0", [128, CH])
    if t_u0 is not None:
        p.emit("gpsimd", lambda h: h.dma_start(out=t_u0.ap(), in_=s_u0[:]),
               wait=[("V", w_u0)], inc="G", inc_by=16)

    TOK1 = TOK + 1
    s_cl = sb("s_cl", [128, CH * TOK1])

    def clc(k):
        return s_cl[:, k * TOK1:(k + 1) * TOK1]

    pool1 = sb("pool1", [128, TOK1])
    pool2 = sb("pool2", [128, TOK1])
    s_kt = pool1[:, 0:TOK]
    s_m = pool1[:]
    s_tht = pool2[:, 0:TOK]
    s_de = pool2[:]
    s_lz = s_mub
    s_lc = s_rstdb
    s_lth = sb("s_lth", [128, TOK])
    s_tmp = sb("s_tmp", [128, TOK])
    s_tmp2 = pool1[:, 0:TOK]
    s_t1a = sb("s_t1a", [128, 1])
    s_t1b = sb("s_t1b", [128, 1])
    s_lch = sb("s_lch", [128, TOK], BF16)
    s_lcl = sb("s_lcl", [128, TOK], BF16)
    s_rth = s_sum
    s_rth_h = sb("s_rtl_h", [1, TOK], BF16)
    s_rth_l = sb("s_rtl_l", [1, TOK], BF16)
    s_u = sb("s_u", [128, TOK1])
    s_tot = sb("s_tot", [128, CH])

    s_wst = sb("s_wst", [128, CH * 128])
    s_wzh = sb("s_wzh", [128, CH * 128], BF16)
    s_wzl = sb("s_wzl", [128, CH * 128], BF16)
    s_whh = sb("s_whh", [128, CH * 128], BF16)
    s_whl = sb("s_whl", [128, CH * 128], BF16)

    ps_z0 = ps("ps_z0", [128, 512])
    ps_z1 = ps("ps_z1", [128, 512])
    ps_h0 = ps("ps_h0", [128, 512])
    ps_h1 = ps("ps_h1", [128, 512])
    ps_a0 = ps("ps_a0", [128, 512])
    ps_a1 = ps("ps_a1", [128, 512])
    ps_cs = ps_sum

    p.emit("vector", lambda h: h.memset(s_rth[:], 0.0))
    p.emit("vector", lambda h: h.memset(s_rth_h[:], 0.0))
    p.emit("vector", lambda h: h.memset(s_rth_l[:], 0.0), inc="V")
    p.drain("vector")

    for m in range(CH):
        for kk in range(CH):
            p.emit("sync", lambda h, kk=kk, m=m: h.dma_start(
                out=s_wst[:, kk * 128:(kk + 1) * 128],
                in_=wzT.ap()[kk * 128:(kk + 1) * 128, m * 128:(m + 1) * 128]),
                wait=([("V", p.cnt["V"])] if kk == 0 else []), inc="L", inc_by=16)
        wL = p.cnt["L"]
        p.emit("vector", lambda h: h.tensor_copy(s_wzh[:], s_wst[:]),
               wait=[("L", wL), ("T", p.count("T"))])
        p.drain("vector")
        p.emit("vector", lambda h: h.tensor_tensor(s_wzl[:], s_wst[:], s_wzh[:], op=A.subtract),
               inc="V")
        p.drain("vector")
        for kk in range(CH):
            p.emit("sync", lambda h, kk=kk, m=m: h.dma_start(
                out=s_wst[:, kk * 128:(kk + 1) * 128],
                in_=whT.ap()[kk * 128:(kk + 1) * 128, m * 128:(m + 1) * 128]),
                wait=([("V", p.cnt["V"])] if kk == 0 else []), inc="L", inc_by=16)
        wL2 = p.cnt["L"]
        p.emit("vector", lambda h: h.tensor_copy(s_whh[:], s_wst[:]),
               wait=[("L", wL2)])
        p.drain("vector")
        p.emit("vector", lambda h: h.tensor_tensor(s_whl[:], s_wst[:], s_whh[:], op=A.subtract),
               inc="V")
        p.drain("vector")
        wsplit = p.cnt["V"]

        for th in range(2):
            pz = ps_z0 if th == 0 else ps_z1
            ph = ps_h0 if th == 0 else ps_h1
            sl = slice(th * 512, (th + 1) * 512)
            first = True
            for (wh_, wl_, rh_, rl_, pst) in (
                (s_wzh, s_wzl, s_xnh, s_xnl, pz),
                (s_whh, s_whl, s_xnh, s_xnl, ph),
            ):
                for pi, (wop, rop) in enumerate((( wh_, rh_), (wh_, rl_), (wl_, rh_))):
                    for kk in range(CH):
                        st = (pi == 0 and kk == 0)
                        sp = (pi == 2 and kk == CH - 1)
                        wt = [("V", wsplit)] if first else []
                        first = False
                        if wop in (s_wzh, s_wzl):
                            lw = wop[:, kk * 128:(kk + 1) * 128]
                        else:
                            lw = wop[:, kk * 128:(kk + 1) * 128]
                        rr = (s_xnh if rop is s_xnh else s_xnl)[:, kk * TOK + th * 512: kk * TOK + th * 512 + 512]
                        p.emit("tensor", lambda h, lw=lw, rr=rr, pst=pst, st=st, sp=sp: h.matmul(
                            pst[:], lw, rr, start=st, stop=sp),
                            wait=wt, inc="T" if sp else None)
        wTproj = p.cnt["T"]

        p.emit("vector", lambda h, m=m: h.tensor_scalar(
            s_kt[:, 0:512], ps_z0[:], s_bz[:, m:m + 1], None, op0=A.add),
            wait=[("T", wTproj)])
        p.emit("vector", lambda h, m=m: h.tensor_scalar(
            s_kt[:, 512:1024], ps_z1[:], s_bz[:, m:m + 1], None, op0=A.add))
        p.emit("vector", lambda h, m=m: h.tensor_scalar(
            s_tht[:, 0:512], ps_h0[:], s_bh[:, m:m + 1], None, op0=A.add))
        p.emit("vector", lambda h, m=m: h.tensor_scalar(
            s_tht[:, 512:1024], ps_h1[:], s_bh[:, m:m + 1], None, op0=A.add), inc="V")
        p.drain("vector")

        emit_logsig(s_kt[:], s_lz[:], s_tmp[:], wait=[("V", p.count("V"))])
        p.emit("vector", lambda h: h.tensor_tensor(s_lc[:], s_lz[:], s_kt[:], op=A.subtract), inc="V")
        p.drain("vector")

        emit_logsig(s_tht[:], s_lth[:], s_tmp[:], wait=[("V", p.count("V"))])
        p.emit("vector", lambda h: h.tensor_scalar(s_tmp[:], s_tht[:], 0.0, None, op0=A.max), inc="V")
        p.drain("vector")
        p.emit("scalar", lambda h: h.activation(s_tmp[:], s_tmp[:], ACT.Ln,
                                                bias=c_half[:, 0:1], scale=1.0),
               wait=[("V", p.cnt["V"])], inc="S")
        p.emit("vector", lambda h: h.tensor_scalar(s_tmp2[:], s_tht[:], 0.0, None, op0=A.is_ge),
               wait=[("S", p.cnt["S"])])
        p.drain("vector")
        p.emit("vector", lambda h: h.tensor_tensor(s_tmp[:], s_tmp[:], s_lth[:], op=A.subtract))
        p.drain("vector")
        p.emit("vector", lambda h: h.tensor_tensor(s_tmp[:], s_tmp[:], s_tmp2, op=A.mult))
        p.drain("vector")
        p.emit("vector", lambda h: h.tensor_tensor(s_lth[:], s_lth[:], s_tmp[:], op=A.add), inc="V")
        p.drain("vector")

        p.emit("vector", lambda h: h.tensor_copy(s_lch[:], s_lc[:]))
        p.drain("vector")
        p.emit("vector", lambda h: h.tensor_tensor(s_lcl[:], s_lc[:], s_lch[:], op=A.subtract), inc="V")
        p.drain("vector")
        wlc = p.cnt["V"]

        for th in range(2):
            pa = ps_a0 if th == 0 else ps_a1
            sl = slice(th * 512, (th + 1) * 512)
            p.emit("tensor", lambda h, sl=sl, pa=pa: h.matmul(
                pa[:], c_triTb[:], s_lch[:, sl], start=True, stop=False),
                wait=[("V", wlc)])
            p.emit("tensor", lambda h, sl=sl, pa=pa: h.matmul(
                pa[:], c_triTb[:], s_lcl[:, sl], start=False, stop=False))
            p.emit("tensor", lambda h, sl=sl, pa=pa: h.matmul(
                pa[:], c_onesb[0:1, :], s_rth_h[:, sl], start=False, stop=False))
            p.emit("tensor", lambda h, sl=sl, pa=pa: h.matmul(
                pa[:], c_onesb[0:1, :], s_rth_l[:, sl], start=False, stop=True),
                inc="T")
            p.emit("tensor", lambda h, sl=sl: h.matmul(
                ps_cs[:], c_onesb[:, 0:1], s_lch[:, sl], start=True, stop=False),
                wait=[("V", p.count("V"))])
            p.emit("tensor", lambda h, sl=sl: h.matmul(
                ps_cs[:], c_onesb[:, 0:1], s_lcl[:, sl], start=False, stop=True),
                inc="T")
            wTa = p.cnt["T"]
            p.emit("vector", lambda h, sl=sl: h.tensor_tensor(
                s_rth[:, sl], s_rth[:, sl], ps_cs[:], op=A.add),
                wait=[("T", wTa)], inc="V")
        p.drain("vector")
        p.emit("vector", lambda h: h.tensor_copy(s_rth_h[:], s_rth[:]),
               wait=[("T", p.count("T"))])
        p.drain("vector")
        p.emit("vector", lambda h: h.tensor_tensor(s_rth_l[:], s_rth[:], s_rth_h[:], op=A.subtract),
               inc="V")
        p.drain("vector")

        p.emit("vector", lambda h, m=m: h.tensor_copy(s_u[:, 0:1], s_u0[:, m:m + 1]),
               wait=[("V", w_u0)] if m == 0 else [])
        p.emit("vector", lambda h: h.tensor_tensor(s_u[:, 1:TOK1], s_lz[:], s_lth[:], op=A.add))
        p.drain("vector")
        p.emit("vector", lambda h: h.tensor_tensor(s_u[:, 1:513], s_u[:, 1:513], ps_a0[:], op=A.subtract))
        p.emit("vector", lambda h: h.tensor_tensor(s_u[:, 513:1025], s_u[:, 513:1025], ps_a1[:], op=A.subtract))
        p.emit("vector", lambda h: h.tensor_copy(s_tmp[:, 0:512], ps_a0[:]))
        p.emit("vector", lambda h: h.tensor_copy(s_tmp[:, 512:1024], ps_a1[:]), inc="V")
        p.drain("vector")
        wu = p.cnt["V"]
        p.emit("gpsimd", lambda h, m=m: h.dma_start(
            out=a_spill.ap()[m * 128:(m + 1) * 128, :], in_=s_tmp[:]),
            wait=[("V", wu)], inc="G", inc_by=16)
        wGa = p.cnt["G"]
        p.wait_only("tensor", [("G", wGa)])
        p.wait_only("vector", [("G", wGa)])
        p.wait_only("scalar", [("G", wGa)])

        p.emit("vector", lambda h: h.tensor_tensor_scan(
            s_m[:], s_u[:], s_u[:], -1e30, op0=A.max, op1=A.bypass), wait=[("V", wu)])
        p.drain("vector")
        p.emit("vector", lambda h: h.memset(s_de[:, 0:1], 0.0))
        p.emit("vector", lambda h: h.tensor_tensor(s_de[:, 1:TOK1], s_m[:, 0:TOK1 - 1], s_m[:, 1:TOK1], op=A.subtract))
        p.drain("vector")
        p.emit("vector", lambda h: h.tensor_tensor(s_u[:], s_u[:], s_m[:], op=A.subtract), inc="V")
        p.drain("vector")
        wsc = p.cnt["V"]
        p.emit("scalar", lambda h: h.activation(s_de[:], s_de[:], ACT.Exp), wait=[("V", wsc)])
        p.drain("scalar")
        p.emit("scalar", lambda h: h.activation(s_u[:], s_u[:], ACT.Exp), inc="S")
        p.drain("scalar")
        p.emit("vector", lambda h: h.tensor_tensor_scan(
            s_de[:], s_de[:], s_u[:], 0.0, op0=A.mult, op1=A.add),
            wait=[("S", p.cnt["S"])], inc="V")
        p.drain("vector")
        p.emit("scalar", lambda h: h.activation(s_de[:], s_de[:], ACT.Ln),
               wait=[("V", p.cnt["V"])], inc="S")
        p.drain("scalar")
        p.emit("vector", lambda h, m=m: h.tensor_tensor(clc(m), s_m[:], s_de[:], op=A.add),
               wait=[("S", p.cnt["S"])], inc="V")
        p.drain("vector")
        p.emit("vector", lambda h, m=m: h.tensor_copy(s_tot[:, m:m + 1], clc(m)[:, TOK1 - 1:TOK1]),
               inc="V")
        p.drain("vector")
    w_scan = p.cnt["V"]

    t_cl = tap("clocal", [D, TOK1])
    if t_cl is not None:
        for k in range(CH):
            p.emit("gpsimd", lambda h, k=k: h.dma_start(
                out=t_cl.ap()[k * 128:(k + 1) * 128, :], in_=clc(k)),
                wait=[("V", w_scan)] if k == 0 else [], inc="G", inc_by=16)

    ag4_in2 = nc.dram_tensor("ag4_in2", [128, CH], F32)
    ag4_out2 = nc.dram_tensor("ag4_out2", [NC, 128, CH], F32)
    p.emit("gpsimd", lambda h: h.dma_start(out=ag4_in2.ap(), in_=s_tot[:]),
           wait=[("V", w_scan)], inc="G", inc_by=16)
    p.wait_only("gpsimd", [("G", p.cnt["G"])])
    p.emit("gpsimd", lambda h: h.collective_compute(
        "AllGather", A.bypass, replica_groups=[list(range(NC))],
        ins=[ag4_in2.ap().opt()], outs=[ag4_out2.ap().opt()]), inc="C")
    wC4 = p.cnt["C"]
    s_ctall = sb("s_ctall", [128, CH * NC])
    p.emit("gpsimd", lambda h: h.dma_start(
        out=s_ctall[:],
        in_=ag4_out2.ap().rearrange("r p c -> p r c"),
        allow_slow_non_contiguous=True),
        wait=[("C", wC4)], inc="G", inc_by=16)
    wG4 = p.cnt["G"]

    s_carry = sb("s_carry", [128, CH])
    s_z8 = sb("s_z8", [128, NC])
    s_m8 = sb("s_m8", [128, 1])
    s_m8n = sb("s_m8n", [128, 1])
    s_se = sb("s_se", [128, 1])
    for k in range(CH):
        p.emit("vector", lambda h, k=k: h.tensor_tensor(
            s_z8[:], s_ctall[:].rearrange("p (r c) -> p r c", c=CH)[:, :, k],
            s_cmask[:], op=A.add),
            wait=[("G", wG4)] if k == 0 else [])
        p.drain("vector")
        p.emit("vector", lambda h: h.tensor_reduce(s_m8[:], s_z8[:], axis=AX.X, op=A.max))
        p.drain("vector")
        p.emit("vector", lambda h: h.tensor_scalar(s_m8n[:], s_m8[:], -1.0, None, op0=A.mult), inc="V")
        p.drain("vector")
        p.emit("scalar", lambda h: h.activation(s_z8[:], s_z8[:], ACT.Exp,
                                                bias=s_m8n[:, 0:1], scale=1.0,
                                                accum_out=s_se[:, 0:1]),
               wait=[("V", p.cnt["V"])], inc="S")
        p.drain("scalar")
        p.emit("scalar", lambda h: h.activation(s_se[:], s_se[:], ACT.Ln), inc="S")
        p.drain("scalar")
        p.emit("vector", lambda h, k=k: h.tensor_tensor(s_carry[:, k:k + 1], s_m8[:], s_se[:], op=A.add),
               wait=[("S", p.cnt["S"])], inc="V")
        p.drain("vector")
    w_carry = p.cnt["V"]
    t_ca = tap("carry", [128, CH])
    if t_ca is not None:
        p.emit("gpsimd", lambda h: h.dma_start(out=t_ca.ap(), in_=s_carry[:]),
               wait=[("V", w_carry)], inc="G", inc_by=16)
    t_as = tap("astar", [D, TOK])
    if t_as is not None:
        p.emit("gpsimd", lambda h: h.dma_start(out=t_as.ap(), in_=a_spill.ap()),
               wait=[("V", w_carry)], inc="G", inc_by=16)

    s_ar0 = s_lth[:, 0:512]
    s_ar1 = s_lth[:, 512:1024]
    s_sx = sb("s_sx2", [128, CH])
    s_sq2 = sb("s_sq2", [128, CH])
    s_c0 = sb("s_c0", [128, 1])
    s_c0q = sb("s_c0q", [128, 1])
    s_sqs = pool2

    for k in range(CH):
        wlda = [("V", w_carry)] if k == 0 else []
        la = p.emit("sync", lambda h, k=k: h.dma_start(
            out=s_ar0, in_=a_spill.ap()[k * 128:(k + 1) * 128, 0:512]),
            wait=wlda + [("V", p.count("V"))], inc="L", inc_by=16)
        p.emit("sync", lambda h, k=k: h.dma_start(
            out=s_ar1, in_=a_spill.ap()[k * 128:(k + 1) * 128, 512:1024]),
            inc="L", inc_by=16)
        wLa = p.cnt["L"]
        p.emit("vector", lambda h, k=k: h.tensor_scalar(
            s_u[:], clc(k), s_carry[:, k:k + 1], None, op0=A.subtract),
            wait=[("V", w_carry)] if k == 0 else [])
        p.emit("vector", lambda h, k=k: h.tensor_scalar(
            s_m[:], clc(k), s_carry[:, k:k + 1], None, op0=A.max), inc="V")
        p.drain("vector")
        p.emit("scalar", lambda h: h.activation(s_u[:], s_u[:], ACT.Abs),
               wait=[("V", p.cnt["V"])], inc="S")
        p.drain("scalar")
        p.emit("scalar", lambda h: h.activation(s_u[:], s_u[:], ACT.Exp, scale=-1.0), inc="S")
        p.drain("scalar")
        p.emit("scalar", lambda h: h.activation(s_u[:], s_u[:], ACT.Ln,
                                                bias=c_one[:, 0:1], scale=1.0), inc="S")
        wS = p.cnt["S"]
        p.emit("vector", lambda h, k=k: h.tensor_tensor(clc(k), s_m[:], s_u[:], op=A.add),
               wait=[("S", wS), ("L", wLa)])
        p.drain("vector")
        p.emit("vector", lambda h, k=k: h.tensor_tensor(
            clc(k)[:, 1:513], clc(k)[:, 1:513], s_ar0, op=A.add))
        p.emit("vector", lambda h, k=k: h.tensor_tensor(
            clc(k)[:, 513:1025], clc(k)[:, 513:1025], s_ar1, op=A.add), inc="V")
        p.drain("vector")
        p.emit("scalar", lambda h, k=k: h.activation(
            s_sqs[:, 0:TOK], clc(k)[:, 1:TOK1], ACT.Square),
            wait=[("V", p.cnt["V"])], inc="S")
        wq_ = p.cnt["S"]
        for w_ in (512, 256, 128, 64, 32, 16, 8, 4, 2, 1):
            p.emit("vector", lambda h, w_=w_: h.tensor_tensor(
                s_sqs[:, 0:w_], s_sqs[:, 0:w_], s_sqs[:, w_:2 * w_], op=A.add),
                wait=[("S", wq_)] if w_ == 512 else [])
            p.drain("vector")
        p.emit("vector", lambda h, k=k: h.tensor_copy(s_sq2[:, k:k + 1], s_sqs[:, 0:1]), inc="V")
        p.drain("vector")
        p.emit("vector", lambda h, k=k: h.tensor_copy(s_sqs[:, 0:TOK], clc(k)[:, 1:TOK1]))
        p.drain("vector")
        for w_ in (512, 256, 128, 64, 32, 16, 8, 4, 2, 1):
            p.emit("vector", lambda h, w_=w_: h.tensor_tensor(
                s_sqs[:, 0:w_], s_sqs[:, 0:w_], s_sqs[:, w_:2 * w_], op=A.add))
            p.drain("vector")
        p.emit("vector", lambda h, k=k: h.tensor_copy(s_sx[:, k:k + 1], s_sqs[:, 0:1]), inc="V")
        p.drain("vector")
        p.emit("vector", lambda h, k=k: h.tensor_scalar(
            s_tmp[:, 0:1], clc(k)[:, 0:1], -1e5, None, op0=A.max), inc="V")
        p.drain("vector")
        if k == 0:
            p.emit("vector", lambda h: h.tensor_copy(s_c0[:], s_tmp[:, 0:1]))
            p.emit("vector", lambda h: h.tensor_tensor(s_c0q[:], s_tmp[:, 0:1], s_tmp[:, 0:1], op=A.mult),
                   inc="V")
        else:
            p.emit("vector", lambda h: h.tensor_tensor(s_c0[:], s_c0[:], s_tmp[:, 0:1], op=A.add))
            p.drain("vector")
            p.emit("vector", lambda h: h.tensor_tensor(s_t1b[:], s_tmp[:, 0:1], s_tmp[:, 0:1], op=A.mult))
            p.drain("vector")
            p.emit("vector", lambda h: h.tensor_tensor(s_c0q[:], s_c0q[:], s_t1b[:], op=A.add),
                   inc="V")
        p.drain("vector")
    p.emit("vector", lambda h: h.tensor_reduce(s_tmp[:, 0:1], s_sx[:], axis=AX.X, op=A.add),
           wait=[("S", p.cnt["S"])])
    p.emit("vector", lambda h: h.tensor_reduce(s_t1b[:], s_sq2[:], axis=AX.X, op=A.add))
    p.drain("vector")
    p.emit("vector", lambda h: h.tensor_tensor(s_c0[:], s_c0[:], s_wsel[:], op=A.mult))
    p.emit("vector", lambda h: h.tensor_tensor(s_c0q[:], s_c0q[:], s_wsel[:], op=A.mult))
    p.drain("vector")
    p.emit("vector", lambda h: h.tensor_tensor(s_tmp[:, 0:1], s_tmp[:, 0:1], s_c0[:], op=A.add))
    p.emit("vector", lambda h: h.tensor_tensor(s_t1b[:], s_t1b[:], s_c0q[:], op=A.add))
    p.drain("vector")
    s_st2 = sb("s_st2", [128, 2])
    s_st2h = sb("s_st2h", [128, 2], BF16)
    s_st2l = sb("s_st2l", [128, 2], BF16)
    p.emit("vector", lambda h: h.tensor_copy(s_st2[:, 0:1], s_tmp[:, 0:1]))
    p.emit("vector", lambda h: h.tensor_copy(s_st2[:, 1:2], s_t1b[:]))
    p.drain("vector")
    p.emit("vector", lambda h: h.tensor_copy(s_st2h[:], s_st2[:]))
    p.drain("vector")
    p.emit("vector", lambda h: h.tensor_tensor(s_st2l[:], s_st2[:], s_st2h[:], op=A.subtract),
           inc="V")
    p.drain("vector")
    ps_g = ps_ssq
    p.emit("tensor", lambda h: h.matmul(ps_g[0:1, 0:2], c_onesb[:, 0:1], s_st2h[:], start=True, stop=False),
           wait=[("V", p.cnt["V"])])
    p.emit("tensor", lambda h: h.matmul(ps_g[0:1, 0:2], c_onesb[:, 0:1], s_st2l[:], start=False, stop=True),
           inc="T")
    s_g8 = sb("s_g8", [1, 8])
    p.emit("vector", lambda h: h.memset(s_g8[:], 0.0), wait=[("T", p.cnt["T"])])
    p.drain("vector")
    p.emit("vector", lambda h: h.tensor_copy(s_g8[:, 0:2], ps_g[0:1, 0:2]), inc="V")
    p.drain("vector")
    ar5b_in = nc.dram_tensor("ar5b_in", [1, 8], F32)
    ar5b_out = nc.dram_tensor("ar5b_out", [1, 8], F32)
    p.emit("gpsimd", lambda h: h.dma_start(out=ar5b_in.ap(), in_=s_g8[:]),
           wait=[("V", p.cnt["V"])], inc="G", inc_by=16)
    p.wait_only("gpsimd", [("G", p.cnt["G"])])
    p.emit("gpsimd", lambda h: h.collective_compute(
        "AllReduce", A.add, replica_groups=[list(range(NC))],
        ins=[ar5b_in.ap().opt()], outs=[ar5b_out.ap().opt()]), inc="C")
    p.emit("gpsimd", lambda h: h.dma_start(out=s_g8[:], in_=ar5b_out.ap()),
           wait=[("C", p.cnt["C"])], inc="G", inc_by=16)
    wG5 = p.cnt["G"]
    t_g8 = tap("stats8", [1, 8])
    if t_g8 is not None:
        p.emit("gpsimd", lambda h: h.dma_start(out=t_g8.ap(), in_=s_g8[:]),
               wait=[("G", wG5)], inc="G", inc_by=16)
    s_mean = sb("s_mean", [1, 1])
    s_rst = sb("s_rst", [1, 1])
    s_ngb = sb("s_ngb", [1, 1])
    p.emit("vector", lambda h: h.tensor_scalar(s_mean[:], s_g8[:, 0:1], 1.0 / NTOT, None, op0=A.mult),
           wait=[("G", wG5)])
    p.drain("vector")
    p.emit("vector", lambda h: h.tensor_tensor(s_rst[:], s_g8[:, 0:1], s_mean[:], op=A.mult))
    p.drain("vector")
    p.emit("vector", lambda h: h.tensor_tensor(s_rst[:], s_g8[:, 1:2], s_rst[:], op=A.subtract))
    p.drain("vector")
    p.emit("vector", lambda h: h.tensor_scalar(s_rst[:], s_rst[:], 1.0 / (NTOT - 1.0), None, op0=A.mult),
           inc="V")
    p.drain("vector")
    p.emit("scalar", lambda h: h.activation(s_rst[:], s_rst[:], ACT.Sqrt),
           wait=[("V", p.cnt["V"])], inc="S")
    p.emit("vector", lambda h: h.reciprocal(s_rst[:], s_rst[:]),
           wait=[("S", p.cnt["S"])], inc="V")
    p.drain("vector")
    p.emit("vector", lambda h: h.tensor_tensor(s_ngb[:], s_mean[:], s_rst[:], op=A.mult))
    p.drain("vector")
    p.emit("vector", lambda h: h.tensor_scalar(s_ngb[:], s_ngb[:], -1.0, None, op0=A.mult),
           inc="V")
    p.drain("vector")
    s_rstb = sb("s_rstb", [128, 1])
    s_ngbb = sb("s_ngbb", [128, 1])
    p.emit("gpsimd", lambda h: h.dma_start(out=bc_scr.ap()[2:3, 0:1], in_=s_rst[:]),
           wait=[("V", p.cnt["V"])], inc="G", inc_by=16)
    p.emit("gpsimd", lambda h: h.dma_start(out=bc_scr.ap()[3:4, 0:1], in_=s_ngb[:]),
           inc="G", inc_by=16)
    p.wait_only("gpsimd", [("G", p.cnt["G"])])
    p.emit("gpsimd", lambda h: h.dma_start(
        out=s_rstb[:], in_=bc_scr.ap()[2:3, 0:1].broadcast_to([128, 1])), inc="G", inc_by=16)
    p.emit("gpsimd", lambda h: h.dma_start(
        out=s_ngbb[:], in_=bc_scr.ap()[3:4, 0:1].broadcast_to([128, 1])), inc="G", inc_by=16)
    wGb = p.cnt["G"]

    for k in range(CH):
        p.emit("scalar", lambda h, k=k: h.activation(
            clc(k)[:, 1:TOK1], clc(k)[:, 1:TOK1], ACT.Exp,
            bias=s_ngbb[:, 0:1], scale=s_rstb[:, 0:1]),
            wait=[("G", wGb)] if k == 0 else [], inc="S")
    p.drain("scalar")
    wS_h = p.cnt["S"]
    for k in range(CH):
        p.emit("vector", lambda h, k=k: h.tensor_tensor(s_tmp[:], xnh(k), xnl(k), op=A.add),
               wait=[("S", wS_h)] if k == 0 else [])
        p.drain("vector")
        p.emit("vector", lambda h, k=k: h.tensor_tensor(
            clc(k)[:, 1:TOK1], clc(k)[:, 1:TOK1], s_tmp[:], op=A.add), inc="V")
        p.drain("vector")
    w_xa = p.cnt["V"]

    w_s = emit_ln_fm(lambda k: clc(k)[:, 1:TOK1], lambda k: clc(k)[:, 1:TOK1],
                     ("V", w_xa), c_eps5[0:1, 0:1],
                     gam_col=lambda k: s_lnw[:, k:k + 1],
                     beta_col=lambda k: s_lnb[:, k:k + 1])

    t_s0 = tap("s0", [D, TOK])
    for k in range(CH):
        p.emit("gpsimd", lambda h, k=k: h.dma_start(
            out=s_spill.ap()[k * 128:(k + 1) * 128, :], in_=clc(k)[:, 1:TOK1]),
            wait=[("V", w_s)] if k == 0 else [], inc="G", inc_by=16)
        if t_s0 is not None:
            p.emit("gpsimd", lambda h, k=k: h.dma_start(
                out=t_s0.ap()[k * 128:(k + 1) * 128, :], in_=clc(k)[:, 1:TOK1]),
                inc="G", inc_by=16)

    w_sspill = p.cnt["G"]

    gwN = inp("gwN", [1024, D])
    upN = inp("upN", [1024, D])
    dwN = inp("dwN", [256, DFF])
    x1_spill = nc.dram_tensor("x1_spill", [D, TOK], F32)
    SLC = 1024 * D + 1024 * D + 256 * DFF
    GOFF, UOFF, DOFF = 0, 1024 * D, 2 * 1024 * D
    ag2_in = nc.dram_tensor("ag2_in", [2 * SLC], BF16)
    ag2_out = nc.dram_tensor("ag2_out", [NC, 2 * SLC], BF16)

    free_back_to("s_h0")
    _bar = [("V", p.count("V")), ("S", p.count("S")), ("T", p.count("T")),
            ("G", p.count("G")), ("L", p.count("L"))]
    for _e in ENGINES:
        p.wait_only(_e, _bar)

    s_swq = sb("s_swq", [128, 4])
    s_swm = sb("s_swm", [128, 4])
    dqx_sp = nc.dram_tensor("dqx_sp", [2, D, TOK], BF16)
    s_dqxh = sb("s_dqxh", [128, CH * 512], BF16)
    s_dqxl = sb("s_dqxl", [128, CH * 512], BF16)
    s_dacc = sb("s_dacc", [128, CH * 512])

    def dqxh(kk):
        return s_dqxh[:, kk * 512:(kk + 1) * 512]

    def dqxl(kk):
        return s_dqxl[:, kk * 512:(kk + 1) * 512]

    def dacc(o):
        return s_dacc[:, o * 512:(o + 1) * 512]

    QW = 1024
    qs_src = sb("qs_src", [128, 1024])
    qs_t = sb("qs_t", [128, QW])
    qs_sq = sb("qs_sq", [128, QW])
    qs_r = sb("qs_r", [128, QW])
    qs_bm = sb("qs_bm", [128, QW // 2])
    qs_sc = sb("qs_sc", [128, 2 * (QW // 32)])
    qs_ev = sb("qs_ev", [128, 128])
    ps_tr = ps("ps_tr", [128, 128])
    ps_gu = ps("ps_gu", [128, 512])
    ps_d0 = ps("ps_d0", [128, 512])
    ps_d1 = ps("ps_d1", [128, 512])

    def emit_mxq(src, dst, W, wait=()):
        nb = W // 32
        p.emit("scalar", lambda h: h.activation(qs_sq[:, 0:W], src, ACT.Abs),
               wait=list(wait), inc="S")
        p.emit("scalar", lambda h: h.activation(qs_r[:, 0:W], src, ACT.Sign),
               inc="S")
        wS0 = p.cnt["S"]
        a3 = qs_sq[:, 0:W].rearrange("p (b q) -> p b q", q=32)
        b3 = qs_bm[:, 0:nb * 16].rearrange("p (b q) -> p b q", q=16)
        p.emit("vector", lambda h: h.tensor_tensor(b3[:, :, 0:16], a3[:, :, 0:16], a3[:, :, 16:32], op=A.max),
               wait=[("S", wS0)])
        p.drain("vector")
        p.emit("vector", lambda h: h.tensor_tensor(b3[:, :, 0:8], b3[:, :, 0:8], b3[:, :, 8:16], op=A.max))
        p.drain("vector")
        p.emit("vector", lambda h: h.tensor_tensor(b3[:, :, 0:4], b3[:, :, 0:4], b3[:, :, 4:8], op=A.max))
        p.drain("vector")
        p.emit("vector", lambda h: h.tensor_tensor(b3[:, :, 0:2], b3[:, :, 0:2], b3[:, :, 2:4], op=A.max))
        p.drain("vector")
        p.emit("vector", lambda h: h.tensor_tensor(b3[:, :, 0:1], b3[:, :, 0:1], b3[:, :, 1:2], op=A.max))
        p.drain("vector")
        p.emit("vector", lambda h: h.tensor_scalar(
            qs_sc[:, 0:nb], b3[:, :, 0:1], 1.0 / 448.0, 1e-30, op0=A.mult, op1=A.max))
        p.drain("vector")
        scb = qs_sc[:, 0:nb].rearrange("p (b o) -> p b o", o=1).broadcast_to([128, nb, 32])
        rscb = qs_sc[:, nb:2 * nb].rearrange("p (b o) -> p b o", o=1).broadcast_to([128, nb, 32])
        r3 = qs_r[:, 0:W].rearrange("p (b q) -> p b q", q=32)
        p.emit("vector", lambda h: h.reciprocal(qs_sc[:, nb:2 * nb], qs_sc[:, 0:nb]))
        p.emit("vector", lambda h: h.tensor_tensor(r3, r3, scb, op=A.mult))
        p.drain("vector")
        p.emit("vector", lambda h: h.tensor_tensor(
            qs_sq[:, 0:W].rearrange("p (b q) -> p b q", q=32), a3, rscb, op=A.mult))
        p.drain("vector")
        p.emit("vector", lambda h: h.tensor_scalar(
            qs_t[:, 0:W].bitcast(U32), qs_sq[:, 0:W].bitcast(U32), 0x7F800000, None,
            op0=A.bitwise_and))
        p.drain("vector")
        p.emit("vector", lambda h: h.tensor_scalar(qs_t[:, 0:W], qs_t[:, 0:W], 2.0 ** 20, 2.0 ** 14,
                                                   op0=A.mult, op1=A.max))
        p.drain("vector")
        p.emit("vector", lambda h: h.tensor_tensor(
            qs_sq[:, 0:W], qs_sq[:, 0:W], qs_t[:, 0:W], op=A.add))
        p.drain("vector")
        p.emit("vector", lambda h: h.tensor_tensor(
            qs_sq[:, 0:W], qs_sq[:, 0:W], qs_t[:, 0:W], op=A.subtract))
        p.drain("vector")
        p.emit("vector", lambda h: h.tensor_tensor(dst, qs_sq[:, 0:W], qs_r[:, 0:W], op=A.mult),
               inc="V")
        p.drain("vector")
        return p.cnt["V"]

    qs_evb = sb("qs_evb", [128, 128], BF16)
    qs_evlb = sb("qs_evlb", [128, 128], BF16)

    def stage_weight(src_dram, rows, width, base_off, out_rowlen):
        ntile = rows // 128
        for ti in range(ntile):
            for sub in range(width // QW):
                p.emit("sync", lambda h, ti=ti, sub=sub: h.dma_start(
                    out=qs_src[:, 0:QW],
                    in_=src_dram.ap()[ti * 128:(ti + 1) * 128, sub * QW:(sub + 1) * QW]),
                    wait=[("V", p.count("V")), ("T", p.count("T"))], inc="L", inc_by=16)
                emit_mxq(qs_src[:, 0:QW], qs_src[:, 0:QW], QW,
                         wait=[("L", p.cnt["L"])])
                for tj in range(QW // 128):
                    p.emit("tensor", lambda h, tj=tj: h.transpose(
                        ps_tr[:], qs_src[:, tj * 128:(tj + 1) * 128], c_ident[:]),
                        wait=[("V", p.count("V")), ("S", p.count("S"))], inc="T")
                    p.emit("scalar", lambda h: h.copy(qs_evb[:], ps_tr[:]),
                           wait=[("T", p.cnt["T"]), ("G", p.count("G"))], inc="S")
                    p.emit("vector", lambda h: h.tensor_tensor(
                        qs_evlb[:], ps_tr[:], qs_evb[:], op=A.subtract),
                        wait=[("S", p.cnt["S"]), ("G", p.count("G"))], inc="V")
                    row = (base_off + (sub * QW + tj * 128) * out_rowlen) // out_rowlen
                    p.emit("gpsimd", lambda h, row=row, ti=ti, rl=out_rowlen: h.dma_start(
                        out=ag2_in.ap()[0:SLC].rearrange("(a b) -> a b", b=rl)[
                            row:row + 128, ti * 128:ti * 128 + 128],
                        in_=qs_evb[:]),
                        wait=[("S", p.cnt["S"])], inc="G", inc_by=16)
                    p.emit("gpsimd", lambda h, row=row, ti=ti, rl=out_rowlen: h.dma_start(
                        out=ag2_in.ap()[SLC:2 * SLC].rearrange("(a b) -> a b", b=rl)[
                            row:row + 128, ti * 128:ti * 128 + 128],
                        in_=qs_evlb[:]),
                        wait=[("V", p.cnt["V"])], inc="G", inc_by=16)

    stage_weight(gwN, 1024, D, GOFF, 1024)
    stage_weight(upN, 1024, D, UOFF, 1024)
    stage_weight(dwN, 256, DFF, DOFF, 256)
    wG2 = p.cnt["G"]
    p.wait_only("gpsimd", [("G", wG2)])
    p.emit("gpsimd", lambda h: h.collective_compute(
        "AllGather", A.bypass, replica_groups=[list(range(NC))],
        ins=[ag2_in.ap().opt()], outs=[ag2_out.ap().opt()]), inc="C")
    wC2 = p.cnt["C"]

    s_xtm = sb("s_xtm", [128, D])
    s_xnhb = sb("s_xnhb", [128, 2048], BF16)
    s_xnlb = sb("s_xnlb", [128, 2048], BF16)
    for tc in range(TOK // 128):
        p.emit("sync", lambda h, tc=tc: h.dma_start(
            out=s_xnhb[:], in_=xnh_sp.ap().rearrange("(p c) t -> p c t", p=128)[
                :, :, tc * 128:(tc + 1) * 128]),
            wait=[("T", p.count("T")), ("V", p.count("V")), ("S", p.count("S")),
                  ("G", p.count("G"))],
            inc="L", inc_by=16)
        p.emit("sync", lambda h, tc=tc: h.dma_start(
            out=s_xnlb[:], in_=xnl_sp.ap().rearrange("(p c) t -> p c t", p=128)[
                :, :, tc * 128:(tc + 1) * 128]),
            inc="L", inc_by=16)
        wLx = p.cnt["L"]
        for hf2 in range(2):
            p.emit("vector", lambda h, hf2=hf2: h.tensor_tensor(
                qs_src[:], s_xnhb[:, hf2 * 1024:(hf2 + 1) * 1024],
                s_xnlb[:, hf2 * 1024:(hf2 + 1) * 1024], op=A.add),
                wait=[("L", wLx), ("T", p.count("T"))], inc="V")
            p.drain("vector")
            for kk2 in range(8):
                kk = hf2 * 8 + kk2
                p.emit("tensor", lambda h, kk2=kk2: h.transpose(
                    ps_tr[:], qs_src[:, kk2 * 128:(kk2 + 1) * 128], c_ident[:]),
                    wait=[("V", p.count("V"))], inc="T")
                p.emit("vector", lambda h, kk=kk: h.tensor_copy(s_xtm[:, kk * 128:(kk + 1) * 128], ps_tr[:]),
                       wait=[("T", p.cnt["T"])], inc="V")
        for sub in range(D // QW):
            emit_mxq(s_xtm[:, sub * QW:(sub + 1) * QW], s_xtm[:, sub * QW:(sub + 1) * QW], QW,
                     wait=[("V", p.count("V"))])
        wq = p.cnt["V"]
        for kk in range(CH):
            p.emit("tensor", lambda h, kk=kk: h.transpose(
                ps_tr[:], s_xtm[:, kk * 128:(kk + 1) * 128], c_ident[:]),
                wait=[("V", p.count("V")), ("S", p.count("S")), ("G", p.count("G"))], inc="T")
            p.emit("scalar", lambda h, kk=kk: h.copy(s_xnhb[:, 0:128], ps_tr[:]),
                   wait=[("T", p.cnt["T"]), ("G", p.count("G"))], inc="S")
            p.emit("vector", lambda h, kk=kk: h.tensor_tensor(
                s_xnlb[:, 0:128], ps_tr[:], s_xnhb[:, 0:128], op=A.subtract),
                wait=[("S", p.cnt["S"]), ("G", p.count("G"))], inc="V")
            p.emit("gpsimd", lambda h, kk=kk, tc=tc: h.dma_start(
                out=dqx_sp.ap()[0, kk * 128:(kk + 1) * 128, tc * 128:(tc + 1) * 128],
                in_=s_xnhb[:, 0:128]),
                wait=[("S", p.cnt["S"])], inc="G", inc_by=16)
            p.emit("gpsimd", lambda h, kk=kk, tc=tc: h.dma_start(
                out=dqx_sp.ap()[1, kk * 128:(kk + 1) * 128, tc * 128:(tc + 1) * 128],
                in_=s_xnlb[:, 0:128]),
                inc="G", inc_by=16)
    w_dqx = p.cnt["G"]
    t_dqx = tap("dqx", [D, TOK])
    if t_dqx is not None:
        for kk in range(CH):
            p.emit("sync", lambda h, kk=kk: h.dma_start(
                out=s_xnhb[:, 0:TOK], in_=dqx_sp.ap()[0, kk * 128:(kk + 1) * 128, :]),
                wait=[("G", w_dqx), ("V", p.count("V"))], inc="L", inc_by=16)
            p.emit("sync", lambda h, kk=kk: h.dma_start(
                out=s_xnlb[:, 0:TOK], in_=dqx_sp.ap()[1, kk * 128:(kk + 1) * 128, :]),
                inc="L", inc_by=16)
            p.emit("vector", lambda h: h.tensor_tensor(
                s_xtm[:, 0:TOK], s_xnhb[:, 0:TOK], s_xnlb[:, 0:TOK], op=A.add),
                wait=[("L", p.cnt["L"])], inc="V")
            p.drain("vector")
            p.emit("gpsimd", lambda h, kk=kk: h.dma_start(
                out=t_dqx.ap()[kk * 128:(kk + 1) * 128, :], in_=s_xtm[:, 0:TOK]),
                wait=[("V", p.cnt["V"])], inc="G", inc_by=16)
            p.wait_only("vector", [("G", p.cnt["G"])])
    t_gw = tap("dqgw0", [128, 1024])
    if t_gw is not None:
        p.emit("sync", lambda h: h.dma_start(
            out=s_xnhb[:, 0:1024],
            in_=ag2_out.ap()[0, 0:128 * 1024].rearrange("(a b) -> a b", b=1024)),
            wait=[("C", wC2), ("G", p.count("G")), ("S", p.count("S")),
                  ("V", p.count("V"))], inc="L", inc_by=16)
        p.emit("sync", lambda h: h.dma_start(
            out=s_xnlb[:, 0:1024],
            in_=ag2_out.ap()[0, SLC:SLC + 128 * 1024].rearrange("(a b) -> a b", b=1024)),
            inc="L", inc_by=16)
        p.emit("vector", lambda h: h.tensor_tensor(
            s_xtm[:, 0:1024], s_xnhb[:, 0:1024], s_xnlb[:, 0:1024], op=A.add),
            wait=[("L", p.cnt["L"])], inc="V")
        p.drain("vector")
        p.emit("gpsimd", lambda h: h.dma_start(
            out=t_gw.ap(), in_=s_xtm[:, 0:1024]),
            wait=[("V", p.cnt["V"])], inc="G", inc_by=16)
        p.wait_only("vector", [("G", p.cnt["G"])])
    wqN = inp("wqN", [256, D])
    wkN = inp("wkN", [256, D])
    wvN = inp("wvN", [256, D])
    woN = inp("woN", [256, D])
    NW = 2048 * 256
    ag3_in = nc.dram_tensor("ag3_in", [4 * NW], BF16)
    ag3_out = nc.dram_tensor("ag3_out", [NC, 4 * NW], BF16)
    ar1_in = nc.dram_tensor("ar1_in", [1, 8], F32)
    ar1_out = nc.dram_tensor("ar1_out", [1, 8], F32)
    pp_w2k = sb("pp_w2k", [128, D])
    pp_a = sb("pp_a", [128, D])
    pp_b2x = sb("pp_b2x", [128, D])
    pp_c2x = sb("pp_c2x", [128, D])
    pp_tq = sb("pp_tq", [128, D], BF16)
    pp_ev = sb("pp_ev", [128, 128], BF16)
    pp_c8 = sb("pp_c8", [128, 16])
    pp_w8h = sb("pp_w8h", [128, 8], BF16)
    pp_w8l = sb("pp_w8l", [128, 8], BF16)
    pp_trb = ps("pp_trb", [128, 128], BF16)
    pp_sm = ps("pp_sm", [1, 8])
    for wi, wsrc in enumerate((wqN, wkN, wvN, woN)):
        for ti in range(2):
            p.emit("sync", lambda h, wsrc=wsrc, ti=ti: h.dma_start(
                out=pp_w2k[:], in_=wsrc.ap()[ti * 128:(ti + 1) * 128, :]),
                wait=[("S", p.count("S")), ("V", p.count("V"))], inc="L", inc_by=16)
            p.emit("scalar", lambda h: h.activation(pp_a[:], pp_w2k[:], ACT.Abs),
                   wait=[("L", p.cnt["L"])], inc="S")
            wS_ = p.cnt["S"]
            for w_ in (1024, 512, 256, 128, 64, 32, 16, 8, 4, 2, 1):
                p.emit("vector", lambda h, w_=w_: h.tensor_tensor(
                    pp_a[:, 0:w_], pp_a[:, 0:w_], pp_a[:, w_:2 * w_], op=A.add),
                    wait=[("S", wS_)] if w_ == 1024 else [])
                p.drain("vector")
            p.emit("vector", lambda h, wi=wi, ti=ti: h.tensor_copy(
                pp_c8[:, (wi * 2 + ti):(wi * 2 + ti) + 1], pp_a[:, 0:1]), inc="V")
            p.drain("vector")
    p.emit("vector", lambda h: h.tensor_copy(pp_w8h[:], pp_c8[:, 0:8]))
    p.drain("vector")
    p.emit("vector", lambda h: h.tensor_tensor(pp_w8l[:], pp_c8[:, 0:8], pp_w8h[:], op=A.subtract),
           inc="V")
    p.drain("vector")
    p.emit("tensor", lambda h: h.matmul(pp_sm[0:1, 0:8], c_onesb[:, 0:1], pp_w8h[:], start=True, stop=False),
           wait=[("V", p.cnt["V"])])
    p.emit("tensor", lambda h: h.matmul(pp_sm[0:1, 0:8], c_onesb[:, 0:1], pp_w8l[:], start=False, stop=True),
           inc="T")
    p.emit("vector", lambda h: h.tensor_copy(pp_a[0:1, 0:8], pp_sm[0:1, 0:8]),
           wait=[("T", p.cnt["T"])], inc="V")
    p.drain("vector")
    p.emit("vector", lambda h: h.tensor_tensor(
        pp_a[0:1, 8:12],
        pp_a[0:1, 0:8].rearrange("p (a b) -> p a b", b=2)[:, :, 0],
        pp_a[0:1, 0:8].rearrange("p (a b) -> p a b", b=2)[:, :, 1], op=A.add), inc="V")
    p.drain("vector")
    p.emit("vector", lambda h: h.tensor_copy(pp_a[0:1, 0:4], pp_a[0:1, 8:12]), inc="V")
    p.drain("vector")
    p.emit("gpsimd", lambda h: h.dma_start(out=ar1_in.ap()[0:1, 0:4], in_=pp_a[0:1, 0:4]),
           wait=[("V", p.cnt["V"])], inc="G", inc_by=16)
    p.emit("gpsimd", lambda h: h.dma_start(
        out=ar1_in.ap()[0:1, 4:8], in_=pp_a[0:1, 0:4]), inc="G", inc_by=16)
    p.wait_only("gpsimd", [("G", p.cnt["G"])])
    p.emit("gpsimd", lambda h: h.collective_compute(
        "AllReduce", A.add, replica_groups=[list(range(NC))],
        ins=[ar1_in.ap().opt()], outs=[ar1_out.ap().opt()]), inc="C")
    p.emit("gpsimd", lambda h: h.dma_start(out=pp_a[0:1, 0:8], in_=ar1_out.ap()),
           wait=[("C", p.cnt["C"])], inc="G", inc_by=16)
    wGr1 = p.cnt["G"]
    p.emit("vector", lambda h: h.tensor_scalar(
        pp_a[0:1, 0:4], pp_a[0:1, 0:4], 1.0 / (D * D), 1e-5, op0=A.mult, op1=A.max),
        wait=[("G", wGr1)], inc="V")
    p.drain("vector")
    p.emit("vector", lambda h: h.reciprocal(pp_a[0:1, 4:8], pp_a[0:1, 0:4]), inc="V")
    p.drain("vector")
    p.emit("gpsimd", lambda h: h.dma_start(out=bc_scr.ap()[0:1, 0:4], in_=pp_a[0:1, 4:8]),
           wait=[("V", p.cnt["V"])], inc="G", inc_by=16)
    p.emit("gpsimd", lambda h: h.dma_start(out=bc_scr.ap()[1:2, 0:4], in_=pp_a[0:1, 0:4]),
           inc="G", inc_by=16)
    p.wait_only("gpsimd", [("G", p.cnt["G"])])
    p.emit("gpsimd", lambda h: h.dma_start(
        out=s_swq[:], in_=bc_scr.ap()[0:1, 0:4].broadcast_to([128, 4])), inc="G", inc_by=16)
    p.emit("gpsimd", lambda h: h.dma_start(
        out=s_swm[:], in_=bc_scr.ap()[1:2, 0:4].broadcast_to([128, 4])), inc="G", inc_by=16)
    wGswq = p.cnt["G"]

    for wi, wsrc in enumerate((wqN, wkN, wvN, woN)):
        for ti in range(2):
            p.emit("sync", lambda h, wsrc=wsrc, ti=ti: h.dma_start(
                out=pp_w2k[:], in_=wsrc.ap()[ti * 128:(ti + 1) * 128, :]),
                wait=[("V", p.count("V")), ("T", p.count("T")), ("S", p.count("S"))],
                inc="L", inc_by=16)
            p.emit("vector", lambda h, wi=wi: h.tensor_scalar(
                pp_a[:], pp_w2k[:], s_swq[:, wi:wi + 1], None, op0=A.mult),
                wait=[("L", p.cnt["L"]), ("G", wGswq)])
            p.drain("vector")
            p.emit("vector", lambda h: h.tensor_scalar(pp_b2x[:], pp_a[:], 0.5, None, op0=A.is_gt))
            p.drain("vector")
            p.emit("vector", lambda h: h.tensor_scalar(pp_c2x[:], pp_a[:], -0.5, None, op0=A.is_lt))
            p.drain("vector")
            p.emit("vector", lambda h: h.tensor_tensor(pp_tq[:], pp_b2x[:], pp_c2x[:], op=A.subtract),
                   inc="V")
            p.drain("vector")
            for tj in range(CH):
                p.emit("tensor", lambda h, tj=tj: h.transpose(
                    pp_trb[:], pp_tq[:, tj * 128:(tj + 1) * 128], c_identb[:]),
                    wait=[("V", p.count("V")), ("G", p.count("G"))], inc="T")
                p.emit("vector", lambda h: h.tensor_copy(pp_ev[:], pp_trb[:]),
                       wait=[("T", p.cnt["T"])], inc="V")
                row = tj * 128
                p.emit("gpsimd", lambda h, wi=wi, row=row, ti=ti: h.dma_start(
                    out=ag3_in.ap()[wi * NW:(wi + 1) * NW].rearrange("(a b) -> a b", b=256)[
                        row:row + 128, ti * 128:ti * 128 + 128],
                    in_=pp_ev[:]),
                    wait=[("V", p.cnt["V"])], inc="G", inc_by=16)
    wG3 = p.cnt["G"]
    p.wait_only("gpsimd", [("G", wG3)])
    p.emit("gpsimd", lambda h: h.collective_compute(
        "AllGather", A.bypass, replica_groups=[list(range(NC))],
        ins=[ag3_in.ap().opt()], outs=[ag3_out.ap().opt()]), inc="C")
    wC3 = p.cnt["C"]
    psfree("pp_sm")
    psfree("pp_trb")
    for _n in ("pp_w8l", "pp_w8h", "pp_c8", "pp_ev", "pp_tq", "pp_c2x", "pp_b2x",
               "pp_a", "pp_w2k"):
        sbfree(_n)
    sbfree("s_xnlb")
    sbfree("s_xnhb")
    sbfree("s_xtm")
    _bar2 = [("V", p.count("V")), ("T", p.count("T")), ("L", p.count("L"))]
    for _e in ENGINES:
        p.wait_only(_e, _bar2)

    JG = 256
    NJG = DFF // JG
    s_wg = [sb("s_wg0", [128, CH * 2 * JG], BF16), sb("s_wg1", [128, CH * 2 * JG], BF16)]
    s_wgl = [sb("s_wgl0", [128, CH * 2 * JG], BF16), sb("s_wgl1", [128, CH * 2 * JG], BF16)]
    s_wd = [sb("s_wd0", [128, 2 * D], BF16), sb("s_wd1", [128, 2 * D], BF16)]
    s_wdl = [sb("s_wdl0", [128, 2 * D], BF16), sb("s_wdl1", [128, 2 * D], BF16)]
    s_hA = qs_src
    s_hB = sb("s_hB", [128, 1024])
    s_dqfh = sb("s_dqfh", [128, 2 * 512], BF16)
    s_dqfl = sb("s_dqfl", [128, 2 * 512], BF16)
    ps_gu2 = ps("ps_gu2", [128, 512])
    ps_gu3 = ps("ps_gu3", [128, 512])
    ps_gu4 = ps("ps_gu4", [128, 512])
    GUB = [ps_gu, ps_gu2, ps_gu3, ps_gu4]
    t_g0 = tap("g0", [D, TOK])
    t_x1 = tap("x1", [D, TOK])
    s_xs = sb("s_xs", [128, 512])
    s_ss = sb("s_ss", [128, 512])

    def emit_wload(jg, buf, first_wait):
        slot = (jg * JG) // 1024
        col = (jg * JG) % 1024
        for hl, (wgt, wdt) in ((0, (s_wg, s_wd)), (1, (s_wgl, s_wdl))):
            hoff = hl * SLC
            for gu, boff in ((0, GOFF), (1, UOFF)):
                src = ag2_out.ap()[slot, hoff + boff:hoff + boff + CH * 128 * 1024].rearrange(
                    "(k q c) -> q k c", q=128, c=1024)[:, :, col:col + JG]
                dst = wgt[buf][:, :].rearrange("q (k c) -> q k c", c=2 * JG)[
                    :, :, gu * JG:(gu + 1) * JG]
                p.emit("sync", lambda h, src=src, dst=dst: h.dma_start(out=dst, in_=src),
                       wait=(list(first_wait) if (hl == 0 and gu == 0) else []),
                       inc="L", inc_by=16)
            for jc in range(2):
                roff = hoff + DOFF + (jg * JG + jc * 128) * 256
                src = ag2_out.ap()[:, roff:roff + 128 * 256].rearrange(
                    "s (q c) -> q s c", c=256)
                dst = wdt[buf][:, jc * D:(jc + 1) * D].rearrange("q (o c) -> q o c", c=256)
                p.emit("sync", lambda h, src=src, dst=dst: h.dma_start(out=dst, in_=src),
                       inc="L", inc_by=16)
        return p.cnt["L"]

    p.wait_only("sync", [("C", wC2), ("G", w_dqx)])
    w_wld_map = {}
    for th in range(2):
        for kk in range(CH):
            p.emit("sync", lambda h, kk=kk, th=th: h.dma_start(
                out=dqxh(kk), in_=dqx_sp.ap()[0, kk * 128:(kk + 1) * 128, th * 512:(th + 1) * 512]),
                wait=[("T", p.count("T"))] if kk == 0 else [], inc="L", inc_by=16)
            p.emit("sync", lambda h, kk=kk, th=th: h.dma_start(
                out=dqxl(kk), in_=dqx_sp.ap()[1, kk * 128:(kk + 1) * 128, th * 512:(th + 1) * 512]),
                inc="L", inc_by=16)
        wLdq = p.cnt["L"]
        for o in range(CH):
            p.emit("vector", lambda h, o=o: h.memset(dacc(o), 0.0),
                   wait=[("S", p.count("S")), ("G", p.count("G"))] if o == 0 else [],
                   inc="V" if o == CH - 1 else None)
        p.drain("vector")

        def emit_gu_T(jg):
            w_wld = w_wld_map.get((th, jg), p.cnt["L"])
            buf = jg % 2
            stops = []
            for tci in range(4):
                pg = GUB[tci]
                first = True
                for pi, (xs_, ws_) in enumerate(((0, 0), (0, 1), (1, 0))):
                    for kk in range(CH):
                        st = (pi == 0 and kk == 0)
                        sp = (pi == 2 and kk == CH - 1)
                        xop = (dqxh if xs_ == 0 else dqxl)(kk)[:, tci * 128:(tci + 1) * 128]
                        wop = (s_wg[buf] if ws_ == 0 else s_wgl[buf])[:, kk * 2 * JG:(kk + 1) * 2 * JG]
                        wt = ([("L", w_wld), ("V", p.count("V")), ("S", p.count("S")), ("L", wLdq)]
                              if first else [])
                        first = False
                        p.emit("tensor", lambda h, xop=xop, wop=wop, st=st, sp=sp, pg=pg: h.matmul(
                            pg[:], xop, wop, start=st, stop=sp),
                            wait=wt, inc="T" if sp else None)
                stops.append(p.cnt["T"])
            return stops

        if th == 0:
            w_wld_map[(0, 0)] = emit_wload(0, 0, [])
            w_wld_map[(0, 1)] = emit_wload(1, 1, [])
        gu_stops = emit_gu_T(0)
        for jg in range(NJG):
            buf = jg % 2
            s_h = s_hA if jg % 2 == 0 else s_hB
            for tci in range(4):
                pg = GUB[tci]
                hsl = slice(tci * JG, (tci + 1) * JG)
                p.emit("scalar", lambda h, hsl=hsl, pg=pg, s_h=s_h: h.activation(
                    s_h[:, hsl], pg[:, 0:JG], ACT.Silu),
                    wait=[("T", gu_stops[tci])], inc="S")
                p.emit("vector", lambda h, hsl=hsl, pg=pg, s_h=s_h: h.tensor_tensor(
                    s_h[:, hsl], s_h[:, hsl], pg[:, JG:2 * JG], op=A.mult),
                    wait=[("S", p.cnt["S"])], inc="V")
            nstops = emit_gu_T(jg + 1) if jg + 1 < NJG else None
            emit_mxq(s_h[:], s_h[:], 4 * JG, wait=[("V", p.count("V"))])
            for tci in range(4):
                tcol = tci * 128
                for jc in range(2):
                    p.emit("tensor", lambda h, tci=tci, jc=jc, s_h=s_h: h.transpose(
                        ps_tr[:], s_h[:, tci * JG + jc * 128:tci * JG + (jc + 1) * 128], c_ident[:]),
                        wait=[("V", p.count("V")), ("S", p.count("S"))], inc="T")
                    p.emit("scalar", lambda h, jc=jc, tcol=tcol: h.copy(
                        s_dqfh[:, jc * 512 + tcol:jc * 512 + tcol + 128], ps_tr[:]),
                        wait=[("T", p.cnt["T"])], inc="S")
                    p.emit("vector", lambda h, jc=jc, tcol=tcol: h.tensor_tensor(
                        s_dqfl[:, jc * 512 + tcol:jc * 512 + tcol + 128], ps_tr[:],
                        s_dqfh[:, jc * 512 + tcol:jc * 512 + tcol + 128], op=A.subtract),
                        wait=[("S", p.cnt["S"])], inc="V")
            w_dqf = p.cnt["V"]
            w_dqfS = p.cnt["S"]
            for o in range(CH):
                pd = ps_d0 if o % 2 == 0 else ps_d1
                first = True
                for pi, (hs_, ws_) in enumerate(((0, 0), (0, 1), (1, 0))):
                    for jc in range(2):
                        st = (pi == 0 and jc == 0)
                        sp = (pi == 2 and jc == 1)
                        hop = (s_dqfh if hs_ == 0 else s_dqfl)[:, jc * 512:(jc + 1) * 512]
                        wop = (s_wd[buf] if ws_ == 0 else s_wdl[buf])[:, jc * D + o * 128:jc * D + (o + 1) * 128]
                        wt = ([("V", w_dqf), ("S", w_dqfS)] if first else [])
                        if first:
                            wt.append(("V", p.count("V")))
                        first = False
                        p.emit("tensor", lambda h, hop=hop, wop=wop, st=st, sp=sp, pd=pd: h.matmul(
                            pd[:], wop, hop, start=st, stop=sp),
                            wait=wt, inc="T" if sp else None)
                wTd = p.cnt["T"]
                p.emit("vector", lambda h, o=o, pd=pd: h.tensor_tensor(
                    dacc(o), dacc(o), pd[:], op=A.add),
                    wait=[("T", wTd)], inc="V")
            p.drain("vector")
            nxt = jg + 2
            if th == 0 and nxt < NJG:
                w_wld_map[(0, nxt)] = emit_wload(nxt, nxt % 2, [("T", p.count("T"))])
            elif th == 0 and nxt - NJG < 2:
                w_wld_map[(1, nxt - NJG)] = emit_wload(nxt - NJG, (nxt - NJG) % 2,
                                                       [("T", p.count("T"))])
            elif th == 1 and nxt < NJG:
                w_wld_map[(1, nxt)] = emit_wload(nxt, nxt % 2, [("T", p.count("T"))])
            gu_stops = nstops
        for k in range(CH):
            p.emit("scalar", lambda h, k=k: h.activation(dacc(k), dacc(k), ACT.Silu),
                   wait=[("V", p.count("V"))] if k == 0 else [], inc="S")
            if t_g0 is not None:
                p.emit("gpsimd", lambda h, k=k, th=th: h.dma_start(
                    out=t_g0.ap()[k * 128:(k + 1) * 128, th * 512:(th + 1) * 512], in_=dacc(k)),
                    wait=[("S", p.cnt["S"])], inc="G", inc_by=16)
            p.emit("sync", lambda h, k=k, th=th: h.dma_start(
                out=s_xs[:], in_=x_fm.ap()[k * 128:(k + 1) * 128, th * 512:(th + 1) * 512]),
                wait=[("V", p.count("V"))], inc="L", inc_by=16)
            p.emit("sync", lambda h, k=k, th=th: h.dma_start(
                out=s_ss[:], in_=s_spill.ap()[k * 128:(k + 1) * 128, th * 512:(th + 1) * 512]),
                wait=[("G", w_sspill), ("V", p.count("V"))], inc="L", inc_by=16)
            wLs = p.cnt["L"]
            p.emit("vector", lambda h, k=k: h.tensor_tensor(dacc(k), dacc(k), s_ss[:], op=A.mult),
                   wait=[("L", wLs), ("S", p.cnt["S"]), ("G", p.cnt["G"])])
            p.drain("vector")
            p.emit("vector", lambda h, k=k: h.tensor_tensor(dacc(k), dacc(k), s_xs[:], op=A.add),
                   inc="V")
            p.emit("gpsimd", lambda h, k=k, th=th: h.dma_start(
                out=x1_spill.ap()[k * 128:(k + 1) * 128, th * 512:(th + 1) * 512], in_=dacc(k)),
                wait=[("V", p.cnt["V"])], inc="G", inc_by=16)
            if t_x1 is not None:
                p.emit("gpsimd", lambda h, k=k, th=th: h.dma_start(
                    out=t_x1.ap()[k * 128:(k + 1) * 128, th * 512:(th + 1) * 512], in_=dacc(k)),
                    inc="G", inc_by=16)
    w_ffn = p.cnt["V"]

    free_back_to("s_wg0")
    _bar3 = [("V", p.count("V")), ("S", p.count("S")), ("T", p.count("T")),
             ("G", p.count("G")), ("L", p.count("L"))]
    for _e in ENGINES:
        p.wait_only(_e, _bar3)
    w_x1sp = p.cnt["G"]

    wbT = inp("wbT", [D, D])
    bb1 = inp("bb1", [128, CH])
    wrT = inp("wrT", [D, 8])
    aaT = inp("aaT", [D, 64])
    ball_i = inp("ball", [64, D])
    ropeC = inp("ropeC", [128, TOK])
    ropeS = inp("ropeS", [128, TOK])
    dmask = inp("dmask", [128, 4 * 512])
    g1_spill = nc.dram_tensor("g1_spill", [D, TOK], F32)
    a2a6_in = nc.dram_tensor("a2a6_in", [NC, 3, 2, 128, TOK], F32)
    a2a6_out = nc.dram_tensor("a2a6_out", [NC, 3, 2, 128, TOK], F32)
    a2a7_in = nc.dram_tensor("a2a7_in", [NC, 2, 128, TOK], F32)
    a2a7_out = nc.dram_tensor("a2a7_out", [NC, 2, 128, TOK], F32)
    rq_scr = nc.dram_tensor("rq_scr", [16, 128], F32)

    free_back_to("s_dqxh")
    _bar4 = [("V", p.count("V")), ("S", p.count("S")), ("T", p.count("T")),
             ("G", p.count("G")), ("L", p.count("L"))]
    for _e in ENGINES:
        p.wait_only(_e, _bar4)

    s_ifm = sb("s_ifm", [128, CH * TOK], BF16)

    def ifm(kk):
        return s_ifm[:, kk * TOK:(kk + 1) * TOK]

    s_t2k = sb("s_t2k", [128, D])
    s_w2k = sb("s_w2k", [128, D])
    s_a = sb("s_a", [128, D])
    s_b2 = sb("s_b2", [128, D])
    s_c2 = sb("s_c2", [128, D])
    s_cols = sb("s_cols", [128, 64])
    s_row = sb("s_row", [128, TOK])
    s_row2 = sb("s_row2", [128, TOK])
    s_row3 = sb("s_row3", [128, TOK])
    s_rowv = sb("s_rowv", [128, TOK])
    ps_tr2 = ps("ps_tr2", [128, 128])
    ps_m0 = ps("ps_m0", [128, 512])
    ps_m1 = ps("ps_m1", [128, 512])
    ps_sm = ps("ps_sm", [128, 64])
    ps_y = ps("ps_y", [128, 128])
    c_e5 = sb("c_e5", [128, 1])
    s_rsx = sb("s_rsx", [128, 8])
    s_pj = sb("s_pj", [128, TOK])
    s_pj2 = sb("s_pj2", [128, TOK])
    s_wqkv = sb("s_wqkv", [128, CH * 128], BF16)
    s_ropeC = sb("s_ropeC", [128, TOK])
    s_ropeS = sb("s_ropeS", [128, TOK])
    s_g1t = s_pj2
    s_x1t = s_rowv

    s_x1h = sb("s_x1h", [128, CH * TOK], BF16)
    s_x1l = sb("s_x1l", [128, CH * TOK], BF16)

    def x1h(kk):
        return s_x1h[:, kk * TOK:(kk + 1) * TOK]

    def x1l(kk):
        return s_x1l[:, kk * TOK:(kk + 1) * TOK]

    s_wbh = sb("s_wbh", [128, D], BF16)
    s_wbl = sb("s_wbl", [128, D], BF16)
    s_bb1 = sb("s_bb1", [128, CH])

    p.emit("vector", lambda h: h.memset(c_e5[:], 1e-5), inc="V")
    p.drain("vector")
    p.emit("sync", lambda h: h.dma_start(out=s_bb1[:], in_=bb1.ap()), inc="L", inc_by=16)
    w_bb1 = p.cnt["L"]


    def tern_tile(wi, kk, m):
        slot = m // 2
        return ag3_out.ap()[slot, wi * NW:(wi + 1) * NW].rearrange(
            "(a b) -> a b", b=256)[kk * 128:(kk + 1) * 128,
                                   (m % 2) * 128:(m % 2) * 128 + 128]


    def emit_rms_quant(tm_ap, rs_col, wait0):
        p.emit("scalar", lambda h: h.activation(s_a[:], tm_ap, ACT.Square),
               wait=list(wait0), inc="S")
        p.emit("vector", lambda h: h.tensor_reduce(s_cols[:, 8:9], s_a[:], axis=AX.X, op=A.add),
               wait=[("S", p.cnt["S"])], inc="V")
        p.drain("vector")
        p.emit("vector", lambda h: h.tensor_scalar(
            s_cols[:, 8:9], s_cols[:, 8:9], 1.0 / D, 1e-6, op0=A.mult, op1=A.add), inc="V")
        p.drain("vector")
        p.emit("scalar", lambda h: h.activation(s_cols[:, 9:10], s_cols[:, 8:9], ACT.Sqrt),
               wait=[("V", p.cnt["V"])], inc="S")
        p.emit("vector", lambda h: h.reciprocal(s_cols[:, 8:9], s_cols[:, 9:10]),
               wait=[("S", p.cnt["S"])], inc="V")
        p.drain("vector")
        p.emit("vector", lambda h: h.tensor_scalar(tm_ap, tm_ap, s_cols[:, 8:9], None, op0=A.mult),
               inc="V")
        p.drain("vector")
        p.emit("scalar", lambda h: h.activation(s_a[:], tm_ap, ACT.Abs),
               wait=[("V", p.cnt["V"])], inc="S")
        p.emit("vector", lambda h: h.tensor_reduce(s_cols[:, 9:10], s_a[:], axis=AX.X, op=A.max),
               wait=[("S", p.cnt["S"])], inc="V")
        p.drain("vector")
        p.emit("vector", lambda h: h.tensor_scalar(
            s_cols[:, 9:10], s_cols[:, 9:10], 1e-5, 1.0 / 127.0, op0=A.max, op1=A.mult), inc="V")
        p.drain("vector")
        p.emit("vector", lambda h, rs_col=rs_col: h.tensor_copy(rs_col, s_cols[:, 9:10]))
        p.emit("vector", lambda h: h.reciprocal(s_cols[:, 10:11], s_cols[:, 9:10]), inc="V")
        p.drain("vector")
        p.emit("vector", lambda h: h.tensor_scalar(tm_ap, tm_ap, s_cols[:, 10:11], None, op0=A.mult),
               inc="V")
        p.drain("vector")
        p.emit("scalar", lambda h: h.activation(s_a[:], tm_ap, ACT.Abs),
               wait=[("V", p.cnt["V"])], inc="S")
        p.emit("scalar", lambda h: h.activation(s_b2[:], tm_ap, ACT.Sign), inc="S")
        wS2 = p.cnt["S"]
        p.emit("vector", lambda h: h.tensor_copy(s_c2[:], s_a[:]), wait=[("S", wS2)])
        p.drain("vector")
        for w_ in (64.0, 32.0, 16.0, 8.0, 4.0, 2.0, 1.0):
            p.emit("vector", lambda h, w_=w_: h.tensor_scalar(tm_ap, s_a[:], w_, None, op0=A.is_ge))
            p.drain("vector")
            p.emit("vector", lambda h, w_=w_: h.scalar_tensor_tensor(
                s_a[:], tm_ap, -w_, s_a[:], op0=A.mult, op1=A.add))
            p.drain("vector")
        p.emit("vector", lambda h: h.tensor_scalar(tm_ap, s_a[:], 0.5, None, op0=A.is_ge))
        p.drain("vector")
        p.emit("vector", lambda h: h.tensor_tensor(s_c2[:], s_c2[:], s_a[:], op=A.subtract))
        p.drain("vector")
        p.emit("vector", lambda h: h.tensor_tensor(s_c2[:], s_c2[:], tm_ap, op=A.add))
        p.drain("vector")
        p.emit("vector", lambda h: h.tensor_tensor(tm_ap, s_c2[:], s_b2[:], op=A.mult), inc="V")
        p.drain("vector")
        return p.cnt["V"]

    for tc in range(TOK // 128):
        p.emit("sync", lambda h, tc=tc: h.dma_start(
            out=s_w2k[:], in_=x1_spill.ap().rearrange("(c p) t -> p c t", p=128)[
                :, :, tc * 128:(tc + 1) * 128]),
            wait=[("G", w_x1sp), ("V", p.count("V")), ("T", p.count("T"))], inc="L", inc_by=16)
        wLx1 = p.cnt["L"]
        for kk in range(CH):
            p.emit("tensor", lambda h, kk=kk: h.transpose(
                ps_tr2[:], s_w2k[:, kk * 128:(kk + 1) * 128], c_ident[:]),
                wait=([("L", wLx1)] if kk == 0 else []) + [("V", p.count("V"))], inc="T")
            p.emit("vector", lambda h, kk=kk: h.tensor_copy(s_t2k[:, kk * 128:(kk + 1) * 128], ps_tr2[:]),
                   wait=[("T", p.cnt["T"])], inc="V")
        p.drain("vector")
        p.emit("vector", lambda h: h.tensor_reduce(s_cols[:, 0:1], s_t2k[:], axis=AX.X, op=A.add),
               wait=[("V", p.cnt["V"])], inc="V")
        p.drain("vector")
        p.emit("vector", lambda h: h.tensor_scalar(s_cols[:, 0:1], s_cols[:, 0:1], 1.0 / D, None, op0=A.mult),
               inc="V")
        p.drain("vector")
        p.emit("scalar", lambda h: h.activation(s_a[:], s_t2k[:], ACT.Square),
               wait=[("V", p.cnt["V"])], inc="S")
        p.emit("vector", lambda h: h.tensor_reduce(s_cols[:, 1:2], s_a[:], axis=AX.X, op=A.add),
               wait=[("S", p.cnt["S"])], inc="V")
        p.drain("vector")
        p.emit("vector", lambda h: h.tensor_tensor(s_cols[:, 2:3], s_cols[:, 0:1], s_cols[:, 0:1], op=A.mult))
        p.drain("vector")
        p.emit("vector", lambda h: h.scalar_tensor_tensor(
            s_cols[:, 1:2], s_cols[:, 1:2], 1.0 / D, s_cols[:, 2:3], op0=A.mult, op1=A.subtract),
            inc="V")
        p.drain("vector")
        p.emit("scalar", lambda h: h.activation(s_cols[:, 1:2], s_cols[:, 1:2], ACT.Sqrt,
                                                bias=c_e5[:, 0:1], scale=1.0),
               wait=[("V", p.cnt["V"])], inc="S")
        p.emit("vector", lambda h: h.reciprocal(s_cols[:, 1:2], s_cols[:, 1:2]),
               wait=[("S", p.cnt["S"])], inc="V")
        p.drain("vector")
        p.emit("vector", lambda h: h.tensor_scalar(
            s_t2k[:], s_t2k[:], s_cols[:, 0:1], None, op0=A.subtract))
        p.drain("vector")
        p.emit("vector", lambda h: h.tensor_scalar(
            s_t2k[:], s_t2k[:], s_cols[:, 1:2], None, op0=A.mult), inc="V")
        p.drain("vector")
        for kk in range(CH):
            p.emit("tensor", lambda h, kk=kk: h.transpose(
                ps_tr2[:], s_t2k[:, kk * 128:(kk + 1) * 128], c_ident[:]),
                wait=[("V", p.count("V"))], inc="T")
            p.emit("vector", lambda h, kk=kk, tc=tc: h.tensor_copy(
                x1h(kk)[:, tc * 128:(tc + 1) * 128], ps_tr2[:]),
                wait=[("T", p.cnt["T"])])
            p.drain("vector")
            p.emit("vector", lambda h, kk=kk, tc=tc: h.tensor_tensor(
                x1l(kk)[:, tc * 128:(tc + 1) * 128], ps_tr2[:],
                x1h(kk)[:, tc * 128:(tc + 1) * 128], op=A.subtract), inc="V")
            p.drain("vector")
        emit_rms_quant(s_t2k[:], s_rsx[:, tc:tc + 1], [("V", p.cnt["V"])])
        for kk in range(CH):
            p.emit("tensor", lambda h, kk=kk: h.transpose(
                ps_tr2[:], s_t2k[:, kk * 128:(kk + 1) * 128], c_ident[:]),
                wait=[("V", p.count("V"))], inc="T")
            p.emit("vector", lambda h, kk=kk, tc=tc: h.tensor_copy(
                ifm(kk)[:, tc * 128:(tc + 1) * 128], ps_tr2[:]),
                wait=[("T", p.cnt["T"])], inc="V")
        p.drain("vector")
    w_ix = p.cnt["V"]

    s_wr8h = sb("s_wr8h", [128, CH * 8], BF16)
    s_wr8l = sb("s_wr8l", [128, CH * 8], BF16)
    s_aah = sb("s_aah", [128, CH * 64], BF16)
    s_aal = sb("s_aal", [128, CH * 64], BF16)
    for kk in range(CH):
        p.emit("sync", lambda h, kk=kk: h.dma_start(
            out=s_w2k[:, 0:8], in_=wrT.ap()[kk * 128:(kk + 1) * 128, :]),
            wait=[("V", p.count("V")), ("T", p.count("T"))], inc="L", inc_by=16)
        p.emit("sync", lambda h, kk=kk: h.dma_start(
            out=s_w2k[:, 64:128], in_=aaT.ap()[kk * 128:(kk + 1) * 128, :]),
            inc="L", inc_by=16)
        wLr = p.cnt["L"]
        p.emit("vector", lambda h, kk=kk: h.tensor_copy(s_wr8h[:, kk * 8:(kk + 1) * 8], s_w2k[:, 0:8]),
               wait=[("L", wLr)])
        p.emit("vector", lambda h, kk=kk: h.tensor_copy(s_aah[:, kk * 64:(kk + 1) * 64], s_w2k[:, 64:128]))
        p.drain("vector")
        p.emit("vector", lambda h, kk=kk: h.tensor_tensor(
            s_wr8l[:, kk * 8:(kk + 1) * 8], s_w2k[:, 0:8], s_wr8h[:, kk * 8:(kk + 1) * 8], op=A.subtract))
        p.emit("vector", lambda h, kk=kk: h.tensor_tensor(
            s_aal[:, kk * 64:(kk + 1) * 64], s_w2k[:, 64:128], s_aah[:, kk * 64:(kk + 1) * 64], op=A.subtract),
            inc="V")
        p.drain("vector")
    w_wra = p.cnt["V"]

    s_lg = sb("s_lg", [128, 8 * 8])
    s_ha = sb("s_ha", [128, 8 * 64])
    s_hsch = sb("s_hsch", [128, TOK], BF16)
    s_hscl = sb("s_hscl", [128, TOK], BF16)
    for tc in range(8):
        first = True
        for pi, (xs_, ws_) in enumerate(((0, 0), (0, 1), (1, 0))):
            for kk in range(CH):
                st = (pi == 0 and kk == 0)
                sp = (pi == 2 and kk == CH - 1)
                xop = (s_x1h if xs_ == 0 else s_x1l)[:, kk * TOK + tc * 128: kk * TOK + tc * 128 + 128]
                wop = (s_wr8h if ws_ == 0 else s_wr8l)[:, kk * 8:(kk + 1) * 8]
                aop = (s_aah if ws_ == 0 else s_aal)[:, kk * 64:(kk + 1) * 64]
                wt = [("V", p.count("V"))] if first else []
                first = False
                p.emit("tensor", lambda h, xop=xop, wop=wop, st=st, sp=sp: h.matmul(
                    ps_sm[:, 0:8], xop, wop, start=st, stop=sp),
                    wait=wt, inc="T" if sp else None)
                p.emit("tensor", lambda h, xop=xop, aop=aop, st=st, sp=sp: h.matmul(
                    ps_m0[:, 0:64], xop, aop, start=st, stop=sp),
                    inc="T" if sp else None)
        wTr = p.cnt["T"]
        p.emit("vector", lambda h, tc=tc: h.tensor_copy(s_lg[:, tc * 8:(tc + 1) * 8], ps_sm[:, 0:8]),
               wait=[("T", wTr)])
        p.emit("vector", lambda h, tc=tc: h.tensor_copy(s_ha[:, tc * 64:(tc + 1) * 64], ps_m0[:, 0:64]),
               inc="V")
        p.drain("vector")
    w_lg = p.cnt["V"]

    for tc in range(8):
        lg = s_lg[:, tc * 8:(tc + 1) * 8]
        p.emit("vector", lambda h, lg=lg: h.tensor_reduce(s_cols[:, 0:1], lg, axis=AX.X, op=A.max),
               wait=[("V", w_lg)] if tc == 0 else [])
        p.drain("vector")
        p.emit("vector", lambda h, lg=lg: h.tensor_scalar(s_a[:, 0:8], lg, s_cols[:, 0:1], None, op0=A.is_ge))
        p.drain("vector")
        p.emit("vector", lambda h: h.tensor_reduce(s_cols[:, 1:2], s_a[:, 0:8], axis=AX.X, op=A.add))
        p.drain("vector")
        p.emit("vector", lambda h: h.tensor_scalar(s_a[:, 0:8], s_a[:, 0:8], 1e30, None, op0=A.mult))
        p.drain("vector")
        p.emit("vector", lambda h, lg=lg: h.tensor_tensor(s_a[:, 8:16], lg, s_a[:, 0:8], op=A.subtract))
        p.drain("vector")
        p.emit("vector", lambda h: h.tensor_reduce(s_cols[:, 2:3], s_a[:, 8:16], axis=AX.X, op=A.max))
        p.drain("vector")
        p.emit("vector", lambda h: h.tensor_scalar(s_cols[:, 3:4], s_cols[:, 1:2], 2.0, None, op0=A.is_ge))
        p.drain("vector")
        p.emit("vector", lambda h: h.tensor_tensor(s_cols[:, 4:5], s_cols[:, 0:1], s_cols[:, 2:3], op=A.subtract))
        p.drain("vector")
        p.emit("vector", lambda h: h.tensor_tensor(s_cols[:, 4:5], s_cols[:, 4:5], s_cols[:, 3:4], op=A.mult))
        p.drain("vector")
        p.emit("vector", lambda h: h.tensor_tensor(s_cols[:, 2:3], s_cols[:, 2:3], s_cols[:, 4:5], op=A.add),
               inc="V")
        p.drain("vector")
        p.emit("vector", lambda h: h.tensor_tensor(s_cols[:, 5:6], s_cols[:, 2:3], s_cols[:, 0:1], op=A.subtract),
               inc="V")
        p.drain("vector")
        p.emit("scalar", lambda h: h.activation(s_cols[:, 5:6], s_cols[:, 5:6], ACT.Exp),
               wait=[("V", p.cnt["V"])], inc="S")
        p.emit("vector", lambda h: h.tensor_scalar(s_cols[:, 5:6], s_cols[:, 5:6], 1.0, None, op0=A.add),
               wait=[("S", p.cnt["S"])], inc="V")
        p.drain("vector")
        p.emit("vector", lambda h: h.reciprocal(s_cols[:, 6:7], s_cols[:, 5:6]), inc="V")
        p.drain("vector")
        p.emit("vector", lambda h: h.tensor_scalar(s_cols[:, 6:7], s_cols[:, 6:7], 2.0, None, op0=A.mult))
        p.drain("vector")
        p.emit("vector", lambda h, lg=lg: h.tensor_scalar(s_a[:, 0:8], lg, s_cols[:, 2:3], None, op0=A.is_ge))
        p.drain("vector")
        p.emit("vector", lambda h: h.tensor_scalar(s_cols[:, 7:8], s_cols[:, 0:1], -1.0, None, op0=A.mult),
               inc="V")
        p.drain("vector")
        p.emit("scalar", lambda h, lg=lg: h.activation(s_a[:, 8:16], lg, ACT.Exp,
                                                       bias=s_cols[:, 7:8], scale=1.0),
               wait=[("V", p.cnt["V"])], inc="S")
        p.emit("vector", lambda h: h.tensor_tensor(s_a[:, 0:8], s_a[:, 0:8], s_a[:, 8:16], op=A.mult),
               wait=[("S", p.cnt["S"])])
        p.drain("vector")
        p.emit("vector", lambda h: h.tensor_scalar(s_a[:, 0:8], s_a[:, 0:8], s_cols[:, 6:7], None, op0=A.mult),
               inc="V")
        p.drain("vector")
        p.emit("vector", lambda h, tc=tc: h.tensor_tensor(
            s_a[:, 64:128].rearrange("p (r e) -> p r e", e=8),
            s_ha[:, tc * 64:(tc + 1) * 64].rearrange("p (r e) -> p r e", e=8),
            s_a[:, 0:8].unsqueeze(1).broadcast_to([128, 8, 8]), op=A.mult), inc="V")
        p.drain("vector")
        p.emit("tensor", lambda h: h.transpose(ps_tr2[0:64, :], s_a[:, 64:128], c_ident[:]),
               wait=[("V", p.cnt["V"])], inc="T")
        p.emit("vector", lambda h, tc=tc: h.tensor_copy(
            s_hsch[0:64, tc * 128:(tc + 1) * 128], ps_tr2[0:64, :]),
            wait=[("T", p.cnt["T"])])
        p.drain("vector")
        p.emit("vector", lambda h, tc=tc: h.tensor_tensor(
            s_hscl[0:64, tc * 128:(tc + 1) * 128], ps_tr2[0:64, :],
            s_hsch[0:64, tc * 128:(tc + 1) * 128], op=A.subtract), inc="V")
        p.drain("vector")
    w_hsc = p.cnt["V"]

    s_blh = sb("s_blh", [128, D], BF16)
    s_bll = sb("s_bll", [128, D], BF16)
    p.emit("sync", lambda h: h.dma_start(out=s_w2k[0:64, 0:D], in_=ball_i.ap()),
           wait=[("V", p.count("V")), ("T", p.count("T"))], inc="L", inc_by=16)
    p.emit("vector", lambda h: h.tensor_copy(s_blh[0:64, :], s_w2k[0:64, 0:D]),
           wait=[("L", p.cnt["L"])])
    p.drain("vector")
    p.emit("vector", lambda h: h.tensor_tensor(s_bll[0:64, :], s_w2k[0:64, 0:D], s_blh[0:64, :], op=A.subtract),
           inc="V")
    p.drain("vector")
    w_bl = p.cnt["V"]

    t_g1 = tap("g1", [D, TOK])
    for m in range(CH):
        for kk in range(CH):
            p.emit("sync", lambda h, kk=kk, m=m: h.dma_start(
                out=s_w2k[:, kk * 128:(kk + 1) * 128],
                in_=wbT.ap()[kk * 128:(kk + 1) * 128, m * 128:(m + 1) * 128]),
                wait=([("V", p.count("V")), ("T", p.count("T"))] if kk == 0 else []),
                inc="L", inc_by=16)
        wLb = p.cnt["L"]
        p.emit("vector", lambda h: h.tensor_copy(s_wbh[:], s_w2k[:]), wait=[("L", wLb)])
        p.drain("vector")
        p.emit("vector", lambda h: h.tensor_tensor(s_wbl[:], s_w2k[:], s_wbh[:], op=A.subtract),
               inc="V")
        p.drain("vector")
        wsp = p.cnt["V"]
        for th in range(2):
            pd = ps_m0 if th == 0 else ps_m1
            sl = slice(th * 512, (th + 1) * 512)
            first = True
            for pi, (xs_, ws_) in enumerate(((0, 0), (0, 1), (1, 0))):
                for kk in range(CH):
                    st = (pi == 0 and kk == 0)
                    xop = (s_wbh if ws_ == 0 else s_wbl)[:, kk * 128:(kk + 1) * 128]
                    rop = (s_x1h if xs_ == 0 else s_x1l)[:, kk * TOK + th * 512: kk * TOK + th * 512 + 512]
                    wt = [("V", wsp), ("V", p.count("V"))] if first else []
                    first = False
                    p.emit("tensor", lambda h, xop=xop, rop=rop, st=st, pd=pd: h.matmul(
                        pd[:], xop, rop, start=st, stop=False),
                        wait=wt)
            for pi, (hs_, ws_) in enumerate(((0, 0), (0, 1), (1, 0))):
                sp = (pi == 2)
                bop = (s_blh if ws_ == 0 else s_bll)[0:64, m * 128:(m + 1) * 128]
                hop = (s_hsch if hs_ == 0 else s_hscl)[0:64, sl]
                p.emit("tensor", lambda h, bop=bop, hop=hop, sp=sp, pd=pd: h.matmul(
                    pd[:], bop, hop, start=False, stop=sp),
                    wait=[("V", w_hsc), ("V", w_bl)] if pi == 0 else [], inc="T" if sp else None)
            wTb = p.cnt["T"]
            p.emit("scalar", lambda h, m=m, sl=sl, pd=pd: h.activation(
                s_c2[:, sl], pd[:], ACT.Silu, bias=s_bb1[:, m:m + 1], scale=1.0),
                wait=[("T", wTb), ("L", w_bb1), ("G", p.count("G"))], inc="S")
        wSg = p.cnt["S"]
        p.emit("gpsimd", lambda h, m=m: h.dma_start(
            out=g1_spill.ap()[m * 128:(m + 1) * 128, :], in_=s_c2[:, 0:TOK]),
            wait=[("S", wSg)], inc="G", inc_by=16)
        if t_g1 is not None:
            p.emit("gpsimd", lambda h, m=m: h.dma_start(
                out=t_g1.ap()[m * 128:(m + 1) * 128, :], in_=s_c2[:, 0:TOK]),
                inc="G", inc_by=16)
        p.wait_only("scalar", [("G", p.cnt["G"])])
    w_g1 = p.cnt["G"]
    free_back_to("s_x1h")
    _bar5 = [("V", p.count("V")), ("S", p.count("S")), ("T", p.count("T")),
             ("G", p.count("G")), ("L", p.count("L"))]
    for _e in ENGINES:
        p.wait_only(_e, _bar5)

    p.emit("gpsimd", lambda h: h.dma_start(
        out=rq_scr.ap()[0:8, :].transpose([1, 0]), in_=s_rsx[:],
        allow_slow_non_contiguous=True),
        wait=[("V", w_ix)], inc="G", inc_by=16)
    p.wait_only("gpsimd", [("G", p.cnt["G"])])
    for c_ in range(8):
        p.emit("gpsimd", lambda h, c_=c_: h.dma_start(
            out=s_row[:, c_ * 128:(c_ + 1) * 128],
            in_=rq_scr.ap()[c_:c_ + 1, :].broadcast_to([128, 128])), inc="G", inc_by=16)
    wGrs = p.cnt["G"]
    p.emit("vector", lambda h: h.tensor_scalar(
        s_row2[:], s_row[:], s_swm[:, 0:1], 1.0 / math.sqrt(128.0), op0=A.mult, op1=A.mult),
        wait=[("G", wGrs)])
    p.drain("vector")
    p.emit("vector", lambda h: h.tensor_scalar(s_row3[:], s_row[:], s_swm[:, 1:2], None, op0=A.mult),
           inc="V")
    p.drain("vector")
    p.emit("vector", lambda h: h.tensor_scalar(s_rowv[:], s_row[:], s_swm[:, 2:3], None, op0=A.mult),
           inc="V")
    p.drain("vector")
    w_rows = p.cnt["V"]

    p.emit("sync", lambda h: h.dma_start(out=s_ropeC[:], in_=ropeC.ap()), inc="L", inc_by=16)
    p.emit("sync", lambda h: h.dma_start(out=s_ropeS[:], in_=ropeS.ap()), inc="L", inc_by=16)
    w_ropeL = p.cnt["L"]
    p.wait_only("vector", [("L", w_ropeL)])
    for wi, rowt in ((0, s_row2), (1, s_row3), (2, s_rowv)):
        for m in range(CH):
            for kk in range(CH):
                p.emit("sync", lambda h, wi=wi, kk=kk, m=m: h.dma_start(
                    out=s_wqkv[:, kk * 128:(kk + 1) * 128], in_=tern_tile(wi, kk, m)),
                    wait=([("C", wC3), ("T", p.count("T"))] if kk == 0 else []),
                    inc="L", inc_by=16)
            wLq = p.cnt["L"]
            for th in range(2):
                pd = ps_m0 if th == 0 else ps_m1
                for kk in range(CH):
                    p.emit("tensor", lambda h, kk=kk, th=th, pd=pd: h.matmul(
                        pd[:], s_wqkv[:, kk * 128:(kk + 1) * 128],
                        s_ifm[:, kk * TOK + th * 512: kk * TOK + th * 512 + 512],
                        start=(kk == 0), stop=(kk == CH - 1)),
                        wait=([("L", wLq), ("V", p.count("V"))] if kk == 0 else []),
                        inc="T" if kk == CH - 1 else None)
            wTq = p.cnt["T"]
            p.emit("vector", lambda h, rowt=rowt: h.tensor_tensor(
                s_pj[:, 0:512], ps_m0[:], rowt[:, 0:512], op=A.mult),
                wait=[("T", wTq), ("V", w_rows)])
            p.emit("vector", lambda h, rowt=rowt: h.tensor_tensor(
                s_pj[:, 512:1024], ps_m1[:], rowt[:, 512:1024], op=A.mult), inc="V")
            p.drain("vector")
            if wi < 2:
                p.emit("vector", lambda h: h.stream_shuffle(
                    s_pj2[:], s_pj[:], [i ^ 1 for i in range(32)]), inc="V")
                p.drain("vector")
                p.emit("vector", lambda h: h.tensor_tensor(s_pj[:], s_pj[:], s_ropeC[:], op=A.mult))
                p.drain("vector")
                p.emit("vector", lambda h: h.tensor_tensor(s_pj2[:], s_pj2[:], s_ropeS[:], op=A.mult))
                p.drain("vector")
                p.emit("vector", lambda h: h.tensor_tensor(s_pj[:], s_pj[:], s_pj2[:], op=A.add),
                       inc="V")
                p.drain("vector")
            p.emit("gpsimd", lambda h, wi=wi, m=m: h.dma_start(
                out=a2a6_in.ap()[m // 2, wi, m % 2, :, :], in_=s_pj[:]),
                wait=[("V", p.cnt["V"])], inc="G", inc_by=16)
            p.wait_only("vector", [("G", p.cnt["G"])])
    p.wait_only("gpsimd", [("G", p.cnt["G"])])
    t_qr = tap("qrope", [D, TOK])
    t_kr = tap("krope", [D, TOK])
    t_vp = tap("vpre", [D, TOK])
    for tt, wi in ((t_qr, 0), (t_kr, 1), (t_vp, 2)):
        if tt is not None:
            for m in range(CH):
                p.emit("gpsimd", lambda h, tt=tt, wi=wi, m=m: h.dma_start(
                    out=tt.ap()[m * 128:(m + 1) * 128, :],
                    in_=a2a6_in.ap()[m // 2, wi, m % 2, :, :]), inc="G", inc_by=16)
    p.wait_only("gpsimd", [("G", p.cnt["G"])])
    p.emit("gpsimd", lambda h: h.collective_compute(
        "AllToAll", A.bypass, replica_groups=[list(range(NC))],
        ins=[a2a6_in.ap().opt()], outs=[a2a6_out.ap().opt()]), inc="C")
    wC6 = p.cnt["C"]

    s_qh = sb("s_qh", [128, 2048], BF16)
    s_ql = sb("s_ql", [128, 2048], BF16)
    s_kh = sb("s_kh", [128, 2048], BF16)
    s_kl = sb("s_kl", [128, 2048], BF16)
    s_vth = sb("s_vth", [128, 2048], BF16)
    s_vtl = sb("s_vtl", [128, 2048], BF16)
    s_P = sb("s_P", [128, 2048])
    s_PTh = sb("s_PTh", [128, 128], BF16)
    s_PTl = sb("s_PTl", [128, 128], BF16)
    s_y = sb("s_y", [128, 2048])
    s_dm = sb("s_dm", [128, 4 * 512])
    s_rq16 = sb("s_rq16", [128, 16])
    s_rowq = sb("s_rowq", [128, 2048])
    p.emit("sync", lambda h: h.dma_start(out=s_dm[:], in_=dmask.ap()), inc="L", inc_by=16)
    w_dm = p.cnt["L"]

    for jb in range(NC):
        bi, hs = jb // 2, jb % 2
        for o in range(2):
            p.emit("sync", lambda h, bi=bi, o=o, hs=hs: h.dma_start(
                out=s_pj[:], in_=a2a6_out.ap()[2 * bi + o, 0, hs, :, :]),
                wait=[("C", wC6), ("V", p.count("V")), ("T", p.count("T"))], inc="L", inc_by=16)
            p.emit("sync", lambda h, bi=bi, o=o, hs=hs: h.dma_start(
                out=s_pj2[:], in_=a2a6_out.ap()[2 * bi + o, 1, hs, :, :]),
                inc="L", inc_by=16)
            wLj = p.cnt["L"]
            sl = slice(o * TOK, (o + 1) * TOK)
            p.emit("vector", lambda h, sl=sl: h.tensor_copy(s_qh[:, sl], s_pj[:]),
                   wait=[("L", wLj)])
            p.emit("vector", lambda h, sl=sl: h.tensor_copy(s_kh[:, sl], s_pj2[:]))
            p.drain("vector")
            p.emit("vector", lambda h, sl=sl: h.tensor_tensor(s_ql[:, sl], s_pj[:], s_qh[:, sl], op=A.subtract))
            p.emit("vector", lambda h, sl=sl: h.tensor_tensor(s_kl[:, sl], s_pj2[:], s_kh[:, sl], op=A.subtract),
                   inc="V")
            p.drain("vector")
            p.emit("sync", lambda h, bi=bi, o=o, hs=hs: h.dma_start(
                out=s_pj[:], in_=a2a6_out.ap()[2 * bi + o, 2, hs, :, :]),
                wait=[("V", p.count("V"))], inc="L", inc_by=16)
            wLv = p.cnt["L"]
            for kt in range(8):
                p.emit("tensor", lambda h, kt=kt: h.transpose(
                    ps_tr2[:], s_pj[:, kt * 128:(kt + 1) * 128], c_ident[:]),
                    wait=([("L", wLv)] if kt == 0 else []) + [("V", p.count("V"))], inc="T")
                col = (o * 8 + kt) * 128
                p.emit("vector", lambda h, col=col: h.tensor_copy(
                    s_vth[:, col:col + 128], ps_tr2[:]),
                    wait=[("T", p.cnt["T"])])
                p.drain("vector")
                p.emit("vector", lambda h, col=col: h.tensor_tensor(
                    s_vtl[:, col:col + 128], ps_tr2[:], s_vth[:, col:col + 128], op=A.subtract),
                    inc="V")
                p.drain("vector")
        w_qkv = p.cnt["V"]
        for qt in range(16):
            ng = qt // 4 + 1
            al = qt % 4
            for g in range(ng):
                ksl = slice(g * 512, (g + 1) * 512)
                pd = ps_m0 if g % 2 == 0 else ps_m1
                for pi, (qs_, ks_) in enumerate(((0, 0), (0, 1), (1, 0))):
                    qop = (s_qh if qs_ == 0 else s_ql)[:, qt * 128:(qt + 1) * 128]
                    kop = (s_kh if ks_ == 0 else s_kl)[:, ksl]
                    p.emit("tensor", lambda h, qop=qop, kop=kop, pi=pi, pd=pd: h.matmul(
                        pd[:], qop, kop, start=(pi == 0), stop=(pi == 2)),
                        wait=([("V", p.count("V"))] if (pi == 0) else []),
                        inc="T" if pi == 2 else None)
                wTs = p.cnt["T"]
                if g == ng - 1:
                    p.emit("vector", lambda h, ksl=ksl, al=al, pd=pd: h.tensor_tensor(
                        s_P[:, ksl], pd[:], s_dm[:, al * 512:(al + 1) * 512], op=A.add),
                        wait=[("T", wTs), ("L", w_dm)], inc="V")
                else:
                    p.emit("vector", lambda h, ksl=ksl, pd=pd: h.tensor_copy(s_P[:, ksl], pd[:]),
                           wait=[("T", wTs)], inc="V")
            p.drain("vector")
            wid = ng * 512
            p.emit("vector", lambda h, wid=wid: h.tensor_reduce(
                s_cols[:, 0:1], s_P[:, 0:wid], axis=AX.X, op=A.max), inc="V")
            p.drain("vector")
            p.emit("vector", lambda h: h.tensor_scalar(s_cols[:, 1:2], s_cols[:, 0:1], -1.0, None, op0=A.mult),
                   inc="V")
            p.drain("vector")
            p.emit("scalar", lambda h, wid=wid: h.activation(
                s_P[:, 0:wid], s_P[:, 0:wid], ACT.Exp, bias=s_cols[:, 1:2], scale=1.0,
                accum_out=s_cols[:, 2:3]),
                wait=[("V", p.cnt["V"])], inc="S")
            p.emit("vector", lambda h: h.reciprocal(s_cols[:, 3:4], s_cols[:, 2:3]),
                   wait=[("S", p.cnt["S"])])
            p.drain("vector")
            p.emit("vector", lambda h, qt=qt: h.tensor_copy(s_rq16[:, qt:qt + 1], s_cols[:, 3:4]),
                   inc="V")
            p.drain("vector")
            nkc = ng * 4
            for kc in range(nkc):
                p.emit("tensor", lambda h, kc=kc: h.transpose(
                    ps_tr2[:], s_P[:, kc * 128:(kc + 1) * 128], c_ident[:]),
                    wait=[("V", p.count("V")), ("S", p.count("S"))], inc="T")
                p.emit("vector", lambda h: h.tensor_copy(s_PTh[:], ps_tr2[:]),
                       wait=[("T", p.cnt["T"])])
                p.drain("vector")
                p.emit("vector", lambda h: h.tensor_tensor(s_PTl[:], ps_tr2[:], s_PTh[:], op=A.subtract),
                       inc="V")
                p.drain("vector")
                wVpt = p.cnt["V"]
                for pi, (vs_, ps_2) in enumerate(((0, 0), (0, 1), (1, 0))):
                    vop = (s_vth if vs_ == 0 else s_vtl)[:, kc * 128:(kc + 1) * 128]
                    pop = s_PTh if ps_2 == 0 else s_PTl
                    p.emit("tensor", lambda h, vop=vop, pop=pop, st=(kc == 0 and pi == 0),
                           sp=(kc == nkc - 1 and pi == 2): h.matmul(
                        ps_y[:], vop, pop[:], start=st, stop=sp),
                        wait=[("V", wVpt)] if pi == 0 else [],
                        inc="T" if pi == 2 else None)
            wTy = p.cnt["T"]
            p.emit("vector", lambda h, qt=qt: h.tensor_copy(s_y[:, qt * 128:(qt + 1) * 128],
                                                            ps_y[:]),
                   wait=[("T", wTy)], inc="V")
            p.drain("vector")
        p.emit("gpsimd", lambda h, jb=jb: h.dma_start(
            out=rq_scr.ap().transpose([1, 0]), in_=s_rq16[:],
            allow_slow_non_contiguous=True),
            wait=[("V", p.cnt["V"])], inc="G", inc_by=16)
        p.wait_only("gpsimd", [("G", p.cnt["G"])])
        for c_ in range(16):
            p.emit("gpsimd", lambda h, c_=c_: h.dma_start(
                out=s_rowq[:, c_ * 128:(c_ + 1) * 128],
                in_=rq_scr.ap()[c_:c_ + 1, :].broadcast_to([128, 128])), inc="G", inc_by=16)
        wGq = p.cnt["G"]
        p.emit("vector", lambda h: h.tensor_tensor(s_y[:], s_y[:], s_rowq[:], op=A.mult),
               wait=[("G", wGq)], inc="V")
        p.drain("vector")
        for o in range(2):
            p.emit("gpsimd", lambda h, bi=bi, o=o, hs=hs: h.dma_start(
                out=a2a7_in.ap()[2 * bi + o, hs, :, :], in_=s_y[:, o * TOK:(o + 1) * TOK]),
                wait=[("V", p.cnt["V"])], inc="G", inc_by=16)
        p.wait_only("vector", [("G", p.cnt["G"])])
        p.wait_only("tensor", [("G", p.cnt["G"])])
    p.wait_only("gpsimd", [("G", p.cnt["G"])])
    p.emit("gpsimd", lambda h: h.collective_compute(
        "AllToAll", A.bypass, replica_groups=[list(range(NC))],
        ins=[a2a7_in.ap().opt()], outs=[a2a7_out.ap().opt()]), inc="C")
    wC7 = p.cnt["C"]

    t_ya = tap("yatt", [D, TOK])
    if t_ya is not None:
        for kk in range(CH):
            p.emit("gpsimd", lambda h, kk=kk: h.dma_start(
                out=t_ya.ap()[kk * 128:(kk + 1) * 128, :],
                in_=a2a7_out.ap()[kk // 2, kk % 2, :, :]),
                wait=[("C", wC7)] if kk == 0 else [], inc="G", inc_by=16)

    s_rsy = s_rsx
    for tc in range(TOK // 128):
        for kk in range(CH):
            p.emit("sync", lambda h, kk=kk, tc=tc: h.dma_start(
                out=s_w2k[:, kk * 128:(kk + 1) * 128],
                in_=a2a7_out.ap()[kk // 2, kk % 2, :, tc * 128:(tc + 1) * 128]),
                wait=([("C", wC7), ("V", p.count("V")), ("T", p.count("T"))] if kk == 0 else []),
                inc="L", inc_by=16)
        wLy = p.cnt["L"]
        for kk in range(CH):
            p.emit("tensor", lambda h, kk=kk: h.transpose(
                ps_tr2[:], s_w2k[:, kk * 128:(kk + 1) * 128], c_ident[:]),
                wait=([("L", wLy)] if kk == 0 else []) + [("V", p.count("V"))], inc="T")
            p.emit("vector", lambda h, kk=kk: h.tensor_copy(s_t2k[:, kk * 128:(kk + 1) * 128], ps_tr2[:]),
                   wait=[("T", p.cnt["T"])], inc="V")
        p.drain("vector")
        emit_rms_quant(s_t2k[:], s_rsy[:, tc:tc + 1], [("V", p.cnt["V"])])
        for kk in range(CH):
            p.emit("tensor", lambda h, kk=kk: h.transpose(
                ps_tr2[:], s_t2k[:, kk * 128:(kk + 1) * 128], c_ident[:]),
                wait=[("V", p.count("V"))], inc="T")
            p.emit("vector", lambda h, kk=kk, tc=tc: h.tensor_copy(
                ifm(kk)[:, tc * 128:(tc + 1) * 128], ps_tr2[:]),
                wait=[("T", p.cnt["T"])], inc="V")
        p.drain("vector")
    w_iy = p.cnt["V"]

    p.emit("gpsimd", lambda h: h.dma_start(
        out=rq_scr.ap()[0:8, :].transpose([1, 0]), in_=s_rsy[:],
        allow_slow_non_contiguous=True),
        wait=[("V", w_iy)], inc="G", inc_by=16)
    p.wait_only("gpsimd", [("G", p.cnt["G"])])
    for c_ in range(8):
        p.emit("gpsimd", lambda h, c_=c_: h.dma_start(
            out=s_row[:, c_ * 128:(c_ + 1) * 128],
            in_=rq_scr.ap()[c_:c_ + 1, :].broadcast_to([128, 128])), inc="G", inc_by=16)
    wGro = p.cnt["G"]
    p.emit("vector", lambda h: h.tensor_scalar(s_row2[:], s_row[:], s_swm[:, 3:4], None, op0=A.mult),
           wait=[("G", wGro)], inc="V")
    p.drain("vector")
    w_rowo = p.cnt["V"]

    t_s1 = tap("s1", [D, TOK])
    t_x2 = tap("x2", [D, TOK])
    for m in range(CH):
        for kk in range(CH):
            p.emit("sync", lambda h, kk=kk, m=m: h.dma_start(
                out=s_wqkv[:, kk * 128:(kk + 1) * 128], in_=tern_tile(3, kk, m)),
                wait=([("T", p.count("T")), ("V", p.count("V"))] if kk == 0 else []),
                inc="L", inc_by=16)
        wLo = p.cnt["L"]
        for th in range(2):
            pd = ps_m0 if th == 0 else ps_m1
            for kk in range(CH):
                p.emit("tensor", lambda h, kk=kk, th=th, pd=pd: h.matmul(
                    pd[:], s_wqkv[:, kk * 128:(kk + 1) * 128],
                    s_ifm[:, kk * TOK + th * 512: kk * TOK + th * 512 + 512],
                    start=(kk == 0), stop=(kk == CH - 1)),
                    wait=([("L", wLo), ("V", p.count("V"))] if kk == 0 else []),
                    inc="T" if kk == CH - 1 else None)
        wTo = p.cnt["T"]
        p.emit("sync", lambda h, m=m: h.dma_start(
            out=s_g1t[:], in_=g1_spill.ap()[m * 128:(m + 1) * 128, :]),
            wait=[("G", w_g1), ("V", p.count("V"))], inc="L", inc_by=16)
        p.emit("sync", lambda h, m=m: h.dma_start(
            out=s_x1t[:], in_=x1_spill.ap()[m * 128:(m + 1) * 128, :]),
            inc="L", inc_by=16)
        wLg = p.cnt["L"]
        p.emit("vector", lambda h: h.tensor_tensor(s_pj[:, 0:512], ps_m0[:], s_row2[:, 0:512], op=A.mult),
               wait=[("T", wTo), ("V", w_rowo), ("L", wLg)])
        p.emit("vector", lambda h: h.tensor_tensor(s_pj[:, 512:1024], ps_m1[:], s_row2[:, 512:1024], op=A.mult),
               inc="V")
        p.drain("vector")
        if t_s1 is not None:
            p.emit("gpsimd", lambda h, m=m: h.dma_start(
                out=t_s1.ap()[m * 128:(m + 1) * 128, :], in_=s_pj[:]),
                wait=[("V", p.cnt["V"])], inc="G", inc_by=16)
            p.wait_only("vector", [("G", p.cnt["G"])])
        p.emit("vector", lambda h: h.tensor_tensor(s_pj[:], s_pj[:], s_g1t[:], op=A.mult))
        p.drain("vector")
        p.emit("vector", lambda h: h.tensor_tensor(s_pj[:], s_pj[:], s_x1t[:], op=A.add), inc="V")
        p.drain("vector")
        p.emit("gpsimd", lambda h, m=m: h.dma_start(
            out=out.ap()[m * 128:(m + 1) * 128, :], in_=s_pj[:]),
            wait=[("V", p.cnt["V"])], inc="G", inc_by=16)
        if t_x2 is not None:
            p.emit("gpsimd", lambda h, m=m: h.dma_start(
                out=t_x2.ap()[m * 128:(m + 1) * 128, :], in_=s_pj[:]),
                inc="G", inc_by=16)
        p.wait_only("vector", [("G", p.cnt["G"])])
    p.wait_only("gpsimd", [("G", p.cnt["G"])])

    p.realize()
    stack.close()
    return nc, ["out"] + ["tap_" + t for t in debug_taps]



def prep_inputs(kw):
    import ml_dtypes
    x = np.asarray(kw["L0_attn_hidden_states"], dtype=np.float32)
    ident = np.eye(128, dtype=np.float32)
    onesb = np.ones((128, 128), dtype=ml_dtypes.bfloat16)
    tribT = np.triu(np.ones((128, 128))).astype(ml_dtypes.bfloat16)
    identb = np.eye(128, dtype=ml_dtypes.bfloat16)
    wzT = np.ascontiguousarray(np.asarray(kw["L0_attn_W_z"], np.float32).T)
    whT = np.ascontiguousarray(np.asarray(kw["L0_attn_W_h"], np.float32).T)
    def cmj(v):
        return np.ascontiguousarray(np.asarray(v, np.float32).reshape(CH, 128).T)
    h0t = cmj(kw["L0_attn_h0"])
    gw_full = np.asarray(kw["L0_ffn_gate_w"], np.float32)
    up_full = np.asarray(kw["L0_ffn_up_w"], np.float32)
    dw_full = np.asarray(kw["L0_ffn_down_w"], np.float32)
    gw_s = [np.ascontiguousarray(gw_full[c * 1024:(c + 1) * 1024]) for c in range(NC)]
    up_s = [np.ascontiguousarray(up_full[c * 1024:(c + 1) * 1024]) for c in range(NC)]
    dw_s = [np.ascontiguousarray(dw_full[c * 256:(c + 1) * 256]) for c in range(NC)]
    wq_f = np.asarray(kw["L1_attn_Wq"], np.float32)
    wk_f = np.asarray(kw["L1_attn_Wk"], np.float32)
    wv_f = np.asarray(kw["L1_attn_Wv"], np.float32)
    wo_f = np.asarray(kw["L1_attn_Wo"], np.float32)
    wq_s = [np.ascontiguousarray(wq_f[c * 256:(c + 1) * 256]) for c in range(NC)]
    wk_s = [np.ascontiguousarray(wk_f[c * 256:(c + 1) * 256]) for c in range(NC)]
    wv_s = [np.ascontiguousarray(wv_f[c * 256:(c + 1) * 256]) for c in range(NC)]
    wo_s = [np.ascontiguousarray(wo_f[c * 256:(c + 1) * 256]) for c in range(NC)]
    wbT = np.ascontiguousarray(np.asarray(kw["L1_ffn_W_base"], np.float32).T)
    bb1 = cmj(kw["L1_ffn_b_base"])
    wrT = np.ascontiguousarray(np.asarray(kw["L1_ffn_W_router"], np.float32).T)
    aaT = np.ascontiguousarray(
        np.asarray(kw["L1_ffn_A_w"], np.float32).transpose(1, 0, 2).reshape(64, D).T)
    ball = np.ascontiguousarray(
        np.asarray(kw["L1_ffn_B_w"], np.float32).transpose(2, 0, 1).reshape(64, D))
    freqs = (1.0 / 10000.0 ** (np.arange(0, 128, 2, dtype=np.float64) / 128.0))
    ii = np.arange(128)[:, None]
    jj = np.arange(512)[None, :]
    dmask = np.concatenate(
        [np.where(jj <= a * 128 + ii, 0.0, NEG) for a in range(4)], axis=1).astype(np.float32)

    in_maps = []
    for c in range(NC):
        b, hf = c // 2, c % 2
        xs = np.ascontiguousarray(x[b, hf * TOK:(hf + 1) * TOK, :].T)
        pos = np.arange(hf * TOK, (hf + 1) * TOK, dtype=np.float64)
        ang = pos[None, :] * freqs[:, None]
        ropeC = np.repeat(np.cos(ang), 2, axis=0).astype(np.float32)
        sgn = np.where(np.arange(128) % 2 == 0, -1.0, 1.0)[:, None]
        ropeS = (np.repeat(np.sin(ang), 2, axis=0) * sgn).astype(np.float32)
        pm0 = np.full((128, CH), 0.0 if hf == 0 else NEG, np.float32)
        cm = np.full((128, NC), NEG, np.float32)
        if hf == 1:
            cm[:, c - 1] = 0.0
        wsel = np.full((128, 1), 1.0 if hf == 0 else 0.0, np.float32)
        m = {
            "x_fm": xs, "wzT": wzT, "whT": whT,
            "gwN": gw_s[c], "upN": up_s[c], "dwN": dw_s[c],
            "wqN": wq_s[c], "wkN": wk_s[c], "wvN": wv_s[c], "woN": wo_s[c],
            "wbT": wbT, "bb1": bb1, "wrT": wrT, "aaT": aaT, "ball": ball,
            "ropeC": ropeC, "ropeS": ropeS, "dmask": dmask,
            "bz": cmj(kw["L0_attn_b_z"]),
            "bh": cmj(kw["L0_attn_b_h"]),
            "lnw": cmj(kw["L0_attn_ln_w"]),
            "lnb": cmj(kw["L0_attn_ln_b"]),
            "h0t": h0t, "pm0": pm0, "cmask": cm, "wsel": wsel,
            "ident": ident, "onesb": onesb, "tribT": tribT, "identb": identb,
        }
        in_maps.append(m)
    return in_maps


def kernel(**kw):
    in_maps = prep_inputs(kw)
    nc, outs = build_nc()
    res = run_bass_kernel_spmd(nc, in_maps, core_ids=list(range(NC)))
    parts = []
    for c in range(NC):
        parts.append(np.asarray(res.results[c]["out"]).T)
    full = np.stack(parts, 0).reshape(B, T, D)
    return full.astype(np.float32)


# version 10
# speedup vs baseline: 1.0138x; 1.0138x over previous
import math
import os
os.environ.setdefault("TRN_ALLOW_ICI_ALL_TO_ALL", "1")
import numpy as np

import concourse.bass as bass
import concourse.mybir as mybir
from concourse.bass_utils import run_bass_kernel_spmd
from concourse.library_config import all_libraries

F32 = mybir.dt.float32
U32 = mybir.dt.uint32
BF16 = mybir.dt.bfloat16
A = mybir.AluOpType
ACT = mybir.ActivationFunctionType
AX = mybir.AxisListType
LIBS = {l.name: l for l in all_libraries}

B, T, D = 4, 2048, 2048
NC = 8
TOK = 1024
CH = D // 128
DFF = 8192
LN2F = float(np.float32(np.log(2.0)))
NEG = -1.0e30


ENGINES = ("sync", "tensor", "vector", "scalar", "gpsimd")


class Plan:
    def __init__(self, nc):
        self.nc = nc
        self.ops = {e: [] for e in ENGINES}
        self.cnt = {}
        self.sems = {}

    def sem(self, name):
        if name not in self.sems:
            h = self.nc.semaphore(name).__enter__()
            self.sems[name] = h
            self.cnt[name] = 0
        return self.sems[name]

    def count(self, name):
        self.sem(name)
        return self.cnt[name]

    def emit(self, eng, fn, wait=(), inc=None, inc_by=1):
        for s, v in wait:
            self.sem(s)
        if inc:
            self.sem(inc)
            self.cnt[inc] += inc_by
        waits = tuple(wait)
        inc_name = inc
        n = inc_by

        def op(h, _fn=fn, _waits=waits, _inc=inc_name, _n=n):
            for s, v in _waits:
                h.wait_ge(self.sems[s], v)
            inst = _fn(h)
            if _inc is not None:
                inst.then_inc(self.sems[_inc], _n)
        self.ops[eng].append(op)
        return self.cnt[inc] if inc else None

    def wait_only(self, eng, wait):
        def op(h, _waits=tuple(wait)):
            for s, v in _waits:
                h.wait_ge(self.sems[s], v)
        self.ops[eng].append(op)

    def drain(self, eng):
        self.ops[eng].append(lambda h: h.drain())

    def realize(self):
        nc = self.nc
        with nc.Block() as block:
            @block.sync
            def _(e):
                for op in self.ops["sync"]:
                    op(e)

            @block.tensor
            def _(e):
                for op in self.ops["tensor"]:
                    op(e)

            @block.vector
            def _(e):
                for op in self.ops["vector"]:
                    op(e)

            @block.scalar
            def _(e):
                for op in self.ops["scalar"]:
                    op(e)

            @block.gpsimd
            def _(e):
                for op in self.ops["gpsimd"]:
                    op(e)



def build_nc(debug_taps=()):
    nc = bass.Bass("TRN2", target_bir_lowering=False, debug=False)

    di = {}

    def inp(name, shape, dtype=F32):
        di[name] = nc.dram_tensor(name, shape, dtype, kind="ExternalInput")
        return di[name]

    x_fm = inp("x_fm", [D, TOK])
    wzT = inp("wzT", [D, D])
    whT = inp("whT", [D, D])
    bz = inp("bz", [128, CH])
    bh = inp("bh", [128, CH])
    lnw = inp("lnw", [128, CH])
    lnb = inp("lnb", [128, CH])
    h0t = inp("h0t", [128, CH])
    pm0 = inp("pm0", [128, CH])
    cmask = inp("cmask", [128, NC])
    wsel = inp("wsel", [128, 1])
    ident = inp("ident", [128, 128])
    onesb_i = inp("onesb", [128, 128], BF16)
    tribT_i = inp("tribT", [128, 128], BF16)
    identb_i = inp("identb", [128, 128], BF16)

    out = nc.dram_tensor("out", [D, TOK], F32, kind="ExternalOutput")

    taps = {}

    def tap(name, shape):
        if name in debug_taps:
            taps[name] = nc.dram_tensor("tap_" + name, shape, F32,
                                        kind="ExternalOutput")
            return taps[name]
        return None

    ag4_in = nc.dram_tensor("ag4_in", [D], F32)
    ag4_out = nc.dram_tensor("ag4_out", [NC, D], F32)
    ar5_in = nc.dram_tensor("ar5_in", [8], F32)
    ar5_out = nc.dram_tensor("ar5_out", [8], F32)
    bc_scr = nc.dram_tensor("bc_scr", [4, TOK], F32)
    xnh_sp = nc.dram_tensor("xnh_sp", [D, TOK], BF16)
    xnl_sp = nc.dram_tensor("xnl_sp", [D, TOK], BF16)

    p = Plan(nc)
    import contextlib
    stack = contextlib.ExitStack()
    frees = {}

    def sb(name, shape, dtype=F32):
        cm = nc.sbuf_tensor(name, shape, dtype)
        h = cm.__enter__()
        frees[name] = cm
        return h

    def sbfree(name):
        frees.pop(name).__exit__(None, None, None)

    def ps(name, shape, dtype=F32):
        cm = nc.psum_tensor(name, shape, dtype)
        h = cm.__enter__()
        frees["PS:" + name] = cm
        return h

    def psfree(name):
        frees.pop("PS:" + name).__exit__(None, None, None)

    def free_back_to(name):
        keys = list(frees.keys())
        idx = keys.index(name)
        for k in reversed(keys[idx:]):
            frees.pop(k).__exit__(None, None, None)

    c_ident = sb("c_ident", [128, 128])
    c_identb = sb("c_identb", [128, 128], BF16)
    c_onesb = sb("c_onesb", [128, 128], BF16)
    c_triTb = sb("c_triTb", [128, 128], BF16)
    c_two = sb("c_two", [128, 1])
    c_eps = sb("c_eps", [128, 1])
    s_h0 = sb("s_h0", [128, CH])
    s_pm0 = sb("s_pm0", [128, CH])
    s_cmask = sb("s_cmask", [128, NC])
    s_wsel = sb("s_wsel", [128, 1])
    s_lnw = sb("s_lnw", [128, CH])
    s_lnb = sb("s_lnb", [128, CH])
    s_bz = sb("s_bz", [128, CH])
    s_bh = sb("s_bh", [128, CH])

    def ld(dst, src_ap, inc="L"):
        return p.emit("sync", lambda h, d=dst, s=src_ap: h.dma_start(out=d, in_=s), inc=inc, inc_by=16)

    ld(c_ident[:], ident.ap())
    ld(c_identb[:], identb_i.ap())
    ld(c_onesb[:], onesb_i.ap())
    ld(c_triTb[:], tribT_i.ap())
    ld(s_h0[:], h0t.ap())
    ld(s_pm0[:], pm0.ap())
    ld(s_cmask[:], cmask.ap())
    ld(s_wsel[:], wsel.ap())
    for name, dst in ((lnw, s_lnw), (lnb, s_lnb), (bz, s_bz), (bh, s_bh)):
        ld(dst[:], name.ap())
    w_const = p.cnt["L"]

    p.emit("vector", lambda h: h.memset(c_two[:], 2.0))
    p.emit("vector", lambda h: h.memset(c_eps[:], 1e-38))
    p.emit("vector", lambda h: h.nop(), wait=[("L", w_const)], inc="V")
    w_cconst = p.cnt["V"]

    s_sqb = sb("s_sqb", [128, 2 * 512], BF16)
    s_xb = sb("s_xb", [128, 2 * 512], BF16)
    s_sum = sb("s_sum", [1, TOK])
    s_ssq = sb("s_ssq", [1, TOK])
    s_rstd = sb("s_rstd", [1, TOK])
    s_mub = sb("s_mub", [128, TOK])
    s_rstdb = sb("s_rstdb", [128, TOK])
    c_eps5 = sb("c_eps5", [1, 1])

    s_xnh = sb("s_xnh", [128, CH * TOK], BF16)
    s_xnl = sb("s_xnl", [128, CH * TOK], BF16)

    s_x = sb("s_x", [128, CH * 512])

    def xc(k):
        return s_x[:, k * 512:(k + 1) * 512]


    def xnc(k):
        return s_xn[:, k * TOK:(k + 1) * TOK]

    ps_sum = ps("ps_sum", [1, 512])
    ps_ssq = ps("ps_ssq", [1, 512])
    s_mu = s_sum

    def emit_ln_fm(src_c, dst_c, wait_src, eps_col, gam_col=None, beta_col=None,
                   ncols=TOK):
        t_base = p.count("T")
        j = 0
        for th in range(ncols // 512):
            sl = slice(th * 512, (th + 1) * 512)
            for k in range(CH):
                pp = slice((j % 2) * 512, (j % 2) * 512 + 512)
                w = [wait_src] if (th == 0 and k == 0) else []
                w = w + [("T", t_base + j - 1 if j >= 2 else t_base)]
                p.emit("vector", lambda h, k=k, sl=sl, pp=pp: h.tensor_copy(
                    s_xb[:, pp], src_c(k)[:, sl]), wait=w, inc="V")
                p.emit("scalar", lambda h, k=k, sl=sl, pp=pp: h.activation(
                    s_sqb[:, pp], src_c(k)[:, sl], ACT.Square), wait=w, inc="S")
                p.emit("tensor", lambda h, k=k, pp=pp: h.matmul(
                    ps_sum[:], c_onesb[:, 0:1], s_xb[:, pp], start=(k == 0), stop=False),
                    wait=[("V", p.cnt["V"]), ("S", p.cnt["S"])])
                p.emit("tensor", lambda h, k=k, pp=pp, last=(k == CH - 1): h.matmul(
                    ps_ssq[:], c_onesb[:, 0:1], s_sqb[:, pp], start=(k == 0), stop=last),
                    inc="T")
                j += 1
            p.emit("vector", lambda h, sl=sl: h.tensor_copy(s_sum[:, sl], ps_sum[:]),
                   wait=[("T", p.cnt["T"])])
            p.emit("vector", lambda h, sl=sl: h.tensor_copy(s_ssq[:, sl], ps_ssq[:]), inc="V")
        p.drain("vector")
        p.emit("vector", lambda h: h.tensor_scalar(s_mu[:, 0:ncols], s_sum[:, 0:ncols], 1.0 / D, None, op0=A.mult))
        p.drain("vector")
        p.emit("vector", lambda h: h.tensor_tensor(
            s_rstd[:, 0:ncols], s_mu[:, 0:ncols], s_mu[:, 0:ncols], op=A.mult), inc="V")
        p.drain("vector")
        p.emit("vector", lambda h: h.scalar_tensor_tensor(
            s_rstd[:, 0:ncols], s_ssq[:, 0:ncols], 1.0 / D, s_rstd[:, 0:ncols],
            op0=A.mult, op1=A.subtract), inc="V")
        p.drain("vector")
        p.emit("scalar", lambda h: h.activation(s_rstd[:, 0:ncols], s_rstd[:, 0:ncols], ACT.Sqrt,
                                                bias=eps_col, scale=1.0),
               wait=[("V", p.cnt["V"])], inc="S")
        p.emit("vector", lambda h: h.reciprocal(s_rstd[:, 0:ncols], s_rstd[:, 0:ncols]),
               wait=[("S", p.cnt["S"])], inc="V")
        p.drain("vector")
        p.emit("gpsimd", lambda h: h.dma_start(out=bc_scr.ap()[0:1, 0:ncols], in_=s_mu[:, 0:ncols]),
               wait=[("V", p.cnt["V"])], inc="P", inc_by=16)
        p.emit("gpsimd", lambda h: h.dma_start(out=bc_scr.ap()[1:2, 0:ncols], in_=s_rstd[:, 0:ncols]),
               inc="P", inc_by=16)
        wg = p.cnt["P"]
        p.emit("gpsimd", lambda h: h.dma_start(
            out=s_mub[:, 0:ncols], in_=bc_scr.ap()[0:1, 0:ncols].broadcast_to([128, ncols])),
            wait=[("P", wg)], inc="P", inc_by=16)
        p.emit("gpsimd", lambda h: h.dma_start(
            out=s_rstdb[:, 0:ncols], in_=bc_scr.ap()[1:2, 0:ncols].broadcast_to([128, ncols])),
            inc="P", inc_by=16)
        wP = p.cnt["P"]
        for k in range(CH):
            p.emit("vector", lambda h, k=k: h.tensor_tensor(dst_c(k), src_c(k), s_mub[:, 0:ncols], op=A.subtract),
                   wait=[("P", wP)] if k == 0 else [])
        p.drain("vector")
        for k in range(CH):
            p.emit("vector", lambda h, k=k: h.tensor_tensor(dst_c(k), dst_c(k), s_rstdb[:, 0:ncols], op=A.mult))
        p.drain("vector")
        if gam_col is not None:
            for k in range(CH):
                p.emit("vector", lambda h, k=k: h.tensor_scalar(
                    dst_c(k), dst_c(k), gam_col(k), beta_col(k), op0=A.mult, op1=A.add))
            p.drain("vector")
        p.emit("vector", lambda h: h.nop(), inc="V")
        return p.cnt["V"]

    p.emit("vector", lambda h: h.memset(c_eps5[:], 1e-5), inc="V")
    p.drain("vector")

    def xnh(k):
        return s_xnh[:, k * TOK:(k + 1) * TOK]

    def xnl(k):
        return s_xnl[:, k * TOK:(k + 1) * TOK]

    for hf in range(2):
        hsl = slice(hf * 512, hf * 512 + 512)
        for k in range(CH):
            ld(xc(k), x_fm.ap()[k * 128:(k + 1) * 128, hsl],
               )
        w_xload = p.cnt["L"]
        w_half = emit_ln_fm(xc, xc, ("L", w_xload), c_eps5[0:1, 0:1], ncols=512)
        for k in range(CH):
            p.emit("vector", lambda h, k=k, hsl=hsl: h.tensor_copy(xnh(k)[:, hsl], xc(k)),
                   wait=[("V", w_half)] if k == 0 else [])
        p.drain("vector")
        for k in range(CH):
            p.emit("vector", lambda h, k=k, hsl=hsl: h.tensor_tensor(
                xnl(k)[:, hsl], xc(k), xnh(k)[:, hsl], op=A.subtract),
                inc="V" if k == CH - 1 else None)
        p.drain("vector")
        p.wait_only("sync", [("V", p.cnt["V"])])
    w_xnsplit = p.cnt["V"]
    p.emit("gpsimd", lambda h: h.dma_start(out=xnh_sp.ap(), in_=s_xnh[:]),
           wait=[("V", w_xnsplit)], inc="G", inc_by=16)
    p.emit("gpsimd", lambda h: h.dma_start(out=xnl_sp.ap(), in_=s_xnl[:]),
           inc="G", inc_by=16)
    w_xnspill = p.cnt["G"]
    sbfree("s_x")

    a_spill = nc.dram_tensor("a_spill", [D, TOK], F32)
    s_spill = nc.dram_tensor("s_spill", [D, TOK], F32)
    NTOT = float(B * (T + 1) * D)

    c_half = sb("c_half", [128, 1])
    c_one = sb("c_one", [128, 1])
    p.emit("vector", lambda h: h.memset(c_half[:], 0.5))
    p.emit("vector", lambda h: h.memset(c_one[:], 1.0), inc="V")
    p.drain("vector")

    def emit_logsig(src, dst, tmp, wait=()):
        p.emit("scalar", lambda h: h.activation(tmp, src, ACT.Abs), wait=list(wait), inc="S")
        p.drain("scalar")
        p.emit("scalar", lambda h: h.activation(tmp, tmp, ACT.Exp, scale=-1.0), inc="S")
        p.drain("scalar")
        p.emit("scalar", lambda h: h.activation(tmp, tmp, ACT.Ln, bias=c_one[:, 0:1], scale=1.0), inc="S")
        wS = p.cnt["S"]
        p.emit("vector", lambda h: h.tensor_scalar(dst, src, 0.0, None, op0=A.min),
               wait=[("S", wS)])
        p.drain("vector")
        p.emit("vector", lambda h: h.tensor_tensor(dst, dst, tmp, op=A.subtract), inc="V")
        p.drain("vector")
        return p.cnt["V"]

    s_u0 = sb("s_u0", [128, CH])
    s_gt1 = sb("s_gt1", [128, CH])
    w0 = emit_logsig(s_h0[:], s_u0[:], s_gt1[:],
                     wait=[("L", w_const), ("V", w_xnsplit)])
    p.emit("vector", lambda h: h.tensor_scalar(s_gt1[:], s_h0[:], 0.0, None, op0=A.max), inc="V")
    p.drain("vector")
    p.emit("scalar", lambda h: h.activation(s_gt1[:], s_gt1[:], ACT.Ln,
                                            bias=c_half[:, 0:1], scale=1.0),
           wait=[("V", p.cnt["V"])], inc="S")
    s_msk0 = sb("s_msk0", [128, CH])
    p.emit("vector", lambda h: h.tensor_scalar(s_msk0[:], s_h0[:], 0.0, None, op0=A.is_ge),
           wait=[("S", p.cnt["S"])])
    p.drain("vector")
    p.emit("vector", lambda h: h.tensor_tensor(s_gt1[:], s_gt1[:], s_u0[:], op=A.subtract))
    p.drain("vector")
    p.emit("vector", lambda h: h.tensor_tensor(s_gt1[:], s_gt1[:], s_msk0[:], op=A.mult))
    p.drain("vector")
    p.emit("vector", lambda h: h.tensor_tensor(s_u0[:], s_u0[:], s_gt1[:], op=A.add))
    p.drain("vector")
    p.emit("vector", lambda h: h.tensor_tensor(s_u0[:], s_u0[:], s_pm0[:], op=A.add), inc="V")
    p.drain("vector")
    w_u0 = p.cnt["V"]

    t_u0 = tap("u0", [128, CH])
    if t_u0 is not None:
        p.emit("gpsimd", lambda h: h.dma_start(out=t_u0.ap(), in_=s_u0[:]),
               wait=[("V", w_u0)], inc="G", inc_by=16)

    TOK1 = TOK + 1
    s_cl = sb("s_cl", [128, CH * TOK1])

    def clc(k):
        return s_cl[:, k * TOK1:(k + 1) * TOK1]

    pool1 = sb("pool1", [128, TOK1])
    pool2 = sb("pool2", [128, TOK1])
    s_kt = pool1[:, 0:TOK]
    s_m = pool1[:]
    s_tht = pool2[:, 0:TOK]
    s_de = pool2[:]
    s_lz = s_mub
    s_lc = s_rstdb
    s_lth = sb("s_lth", [128, TOK])
    s_tmp = sb("s_tmp", [128, TOK])
    s_tmp2 = pool1[:, 0:TOK]
    s_t1a = sb("s_t1a", [128, 1])
    s_t1b = sb("s_t1b", [128, 1])
    s_lch = sb("s_lch", [128, TOK], BF16)
    s_lcl = sb("s_lcl", [128, TOK], BF16)
    s_rth = s_sum
    s_rth_h = sb("s_rtl_h", [1, TOK], BF16)
    s_rth_l = sb("s_rtl_l", [1, TOK], BF16)
    s_u = sb("s_u", [128, TOK1])
    s_tot = sb("s_tot", [128, CH])

    s_wst = sb("s_wst", [128, CH * 128])
    s_wzh = sb("s_wzh", [128, CH * 128], BF16)
    s_wzl = sb("s_wzl", [128, CH * 128], BF16)
    s_whh = sb("s_whh", [128, CH * 128], BF16)
    s_whl = sb("s_whl", [128, CH * 128], BF16)

    ps_z0 = ps("ps_z0", [128, 512])
    ps_z1 = ps("ps_z1", [128, 512])
    ps_h0 = ps("ps_h0", [128, 512])
    ps_h1 = ps("ps_h1", [128, 512])
    ps_a0 = ps("ps_a0", [128, 512])
    ps_a1 = ps("ps_a1", [128, 512])
    ps_cs = ps_sum

    p.emit("vector", lambda h: h.memset(s_rth[:], 0.0))
    p.emit("vector", lambda h: h.memset(s_rth_h[:], 0.0))
    p.emit("vector", lambda h: h.memset(s_rth_l[:], 0.0), inc="V")
    p.drain("vector")

    for m in range(CH):
        for kk in range(CH):
            p.emit("sync", lambda h, kk=kk, m=m: h.dma_start(
                out=s_wst[:, kk * 128:(kk + 1) * 128],
                in_=wzT.ap()[kk * 128:(kk + 1) * 128, m * 128:(m + 1) * 128]),
                wait=([("V", p.cnt["V"])] if kk == 0 else []), inc="L", inc_by=16)
        wL = p.cnt["L"]
        p.emit("vector", lambda h: h.tensor_copy(s_wzh[:], s_wst[:]),
               wait=[("L", wL), ("T", p.count("T"))])
        p.drain("vector")
        p.emit("vector", lambda h: h.tensor_tensor(s_wzl[:], s_wst[:], s_wzh[:], op=A.subtract),
               inc="V")
        p.drain("vector")
        for kk in range(CH):
            p.emit("sync", lambda h, kk=kk, m=m: h.dma_start(
                out=s_wst[:, kk * 128:(kk + 1) * 128],
                in_=whT.ap()[kk * 128:(kk + 1) * 128, m * 128:(m + 1) * 128]),
                wait=([("V", p.cnt["V"])] if kk == 0 else []), inc="L", inc_by=16)
        wL2 = p.cnt["L"]
        p.emit("vector", lambda h: h.tensor_copy(s_whh[:], s_wst[:]),
               wait=[("L", wL2)])
        p.drain("vector")
        p.emit("vector", lambda h: h.tensor_tensor(s_whl[:], s_wst[:], s_whh[:], op=A.subtract),
               inc="V")
        p.drain("vector")
        wsplit = p.cnt["V"]

        for th in range(2):
            pz = ps_z0 if th == 0 else ps_z1
            ph = ps_h0 if th == 0 else ps_h1
            sl = slice(th * 512, (th + 1) * 512)
            first = True
            for (wh_, wl_, rh_, rl_, pst) in (
                (s_wzh, s_wzl, s_xnh, s_xnl, pz),
                (s_whh, s_whl, s_xnh, s_xnl, ph),
            ):
                for pi, (wop, rop) in enumerate((( wh_, rh_), (wh_, rl_), (wl_, rh_))):
                    for kk in range(CH):
                        st = (pi == 0 and kk == 0)
                        sp = (pi == 2 and kk == CH - 1)
                        wt = [("V", wsplit)] if first else []
                        first = False
                        if wop in (s_wzh, s_wzl):
                            lw = wop[:, kk * 128:(kk + 1) * 128]
                        else:
                            lw = wop[:, kk * 128:(kk + 1) * 128]
                        rr = (s_xnh if rop is s_xnh else s_xnl)[:, kk * TOK + th * 512: kk * TOK + th * 512 + 512]
                        p.emit("tensor", lambda h, lw=lw, rr=rr, pst=pst, st=st, sp=sp: h.matmul(
                            pst[:], lw, rr, start=st, stop=sp),
                            wait=wt, inc="T" if sp else None)
        wTproj = p.cnt["T"]

        p.emit("vector", lambda h, m=m: h.tensor_scalar(
            s_kt[:, 0:512], ps_z0[:], s_bz[:, m:m + 1], None, op0=A.add),
            wait=[("T", wTproj)])
        p.emit("vector", lambda h, m=m: h.tensor_scalar(
            s_kt[:, 512:1024], ps_z1[:], s_bz[:, m:m + 1], None, op0=A.add))
        p.emit("vector", lambda h, m=m: h.tensor_scalar(
            s_tht[:, 0:512], ps_h0[:], s_bh[:, m:m + 1], None, op0=A.add))
        p.emit("vector", lambda h, m=m: h.tensor_scalar(
            s_tht[:, 512:1024], ps_h1[:], s_bh[:, m:m + 1], None, op0=A.add), inc="V")
        p.drain("vector")

        emit_logsig(s_kt[:], s_lz[:], s_tmp[:], wait=[("V", p.count("V"))])
        p.emit("vector", lambda h: h.tensor_tensor(s_lc[:], s_lz[:], s_kt[:], op=A.subtract), inc="V")
        p.drain("vector")

        emit_logsig(s_tht[:], s_lth[:], s_tmp[:], wait=[("V", p.count("V"))])
        p.emit("vector", lambda h: h.tensor_scalar(s_tmp[:], s_tht[:], 0.0, None, op0=A.max), inc="V")
        p.drain("vector")
        p.emit("scalar", lambda h: h.activation(s_tmp[:], s_tmp[:], ACT.Ln,
                                                bias=c_half[:, 0:1], scale=1.0),
               wait=[("V", p.cnt["V"])], inc="S")
        p.emit("vector", lambda h: h.tensor_scalar(s_tmp2[:], s_tht[:], 0.0, None, op0=A.is_ge),
               wait=[("S", p.cnt["S"])])
        p.drain("vector")
        p.emit("vector", lambda h: h.tensor_tensor(s_tmp[:], s_tmp[:], s_lth[:], op=A.subtract))
        p.drain("vector")
        p.emit("vector", lambda h: h.tensor_tensor(s_tmp[:], s_tmp[:], s_tmp2, op=A.mult))
        p.drain("vector")
        p.emit("vector", lambda h: h.tensor_tensor(s_lth[:], s_lth[:], s_tmp[:], op=A.add), inc="V")
        p.drain("vector")

        p.emit("vector", lambda h: h.tensor_copy(s_lch[:], s_lc[:]))
        p.drain("vector")
        p.emit("vector", lambda h: h.tensor_tensor(s_lcl[:], s_lc[:], s_lch[:], op=A.subtract), inc="V")
        p.drain("vector")
        wlc = p.cnt["V"]

        for th in range(2):
            pa = ps_a0 if th == 0 else ps_a1
            sl = slice(th * 512, (th + 1) * 512)
            p.emit("tensor", lambda h, sl=sl, pa=pa: h.matmul(
                pa[:], c_triTb[:], s_lch[:, sl], start=True, stop=False),
                wait=[("V", wlc)])
            p.emit("tensor", lambda h, sl=sl, pa=pa: h.matmul(
                pa[:], c_triTb[:], s_lcl[:, sl], start=False, stop=False))
            p.emit("tensor", lambda h, sl=sl, pa=pa: h.matmul(
                pa[:], c_onesb[0:1, :], s_rth_h[:, sl], start=False, stop=False))
            p.emit("tensor", lambda h, sl=sl, pa=pa: h.matmul(
                pa[:], c_onesb[0:1, :], s_rth_l[:, sl], start=False, stop=True),
                inc="T")
            p.emit("tensor", lambda h, sl=sl: h.matmul(
                ps_cs[:], c_onesb[:, 0:1], s_lch[:, sl], start=True, stop=False),
                wait=[("V", p.count("V"))])
            p.emit("tensor", lambda h, sl=sl: h.matmul(
                ps_cs[:], c_onesb[:, 0:1], s_lcl[:, sl], start=False, stop=True),
                inc="T")
            wTa = p.cnt["T"]
            p.emit("vector", lambda h, sl=sl: h.tensor_tensor(
                s_rth[:, sl], s_rth[:, sl], ps_cs[:], op=A.add),
                wait=[("T", wTa)], inc="V")
        p.drain("vector")
        p.emit("vector", lambda h: h.tensor_copy(s_rth_h[:], s_rth[:]),
               wait=[("T", p.count("T"))])
        p.drain("vector")
        p.emit("vector", lambda h: h.tensor_tensor(s_rth_l[:], s_rth[:], s_rth_h[:], op=A.subtract),
               inc="V")
        p.drain("vector")

        p.emit("vector", lambda h, m=m: h.tensor_copy(s_u[:, 0:1], s_u0[:, m:m + 1]),
               wait=[("V", w_u0)] if m == 0 else [])
        p.emit("vector", lambda h: h.tensor_tensor(s_u[:, 1:TOK1], s_lz[:], s_lth[:], op=A.add))
        p.drain("vector")
        p.emit("vector", lambda h: h.tensor_tensor(s_u[:, 1:513], s_u[:, 1:513], ps_a0[:], op=A.subtract))
        p.emit("vector", lambda h: h.tensor_tensor(s_u[:, 513:1025], s_u[:, 513:1025], ps_a1[:], op=A.subtract))
        p.emit("vector", lambda h: h.tensor_copy(s_tmp[:, 0:512], ps_a0[:]))
        p.emit("vector", lambda h: h.tensor_copy(s_tmp[:, 512:1024], ps_a1[:]), inc="V")
        p.drain("vector")
        wu = p.cnt["V"]
        p.emit("gpsimd", lambda h, m=m: h.dma_start(
            out=a_spill.ap()[m * 128:(m + 1) * 128, :], in_=s_tmp[:]),
            wait=[("V", wu)], inc="G", inc_by=16)
        wGa = p.cnt["G"]
        p.wait_only("tensor", [("G", wGa)])
        p.wait_only("vector", [("G", wGa)])
        p.wait_only("scalar", [("G", wGa)])

        p.emit("vector", lambda h: h.tensor_tensor_scan(
            s_m[:], s_u[:], s_u[:], -1e30, op0=A.max, op1=A.bypass), wait=[("V", wu)])
        p.drain("vector")
        p.emit("vector", lambda h: h.memset(s_de[:, 0:1], 0.0))
        p.emit("vector", lambda h: h.tensor_tensor(s_de[:, 1:TOK1], s_m[:, 0:TOK1 - 1], s_m[:, 1:TOK1], op=A.subtract))
        p.drain("vector")
        p.emit("vector", lambda h: h.tensor_tensor(s_u[:], s_u[:], s_m[:], op=A.subtract), inc="V")
        p.drain("vector")
        wsc = p.cnt["V"]
        p.emit("scalar", lambda h: h.activation(s_de[:], s_de[:], ACT.Exp), wait=[("V", wsc)])
        p.drain("scalar")
        p.emit("scalar", lambda h: h.activation(s_u[:], s_u[:], ACT.Exp), inc="S")
        p.drain("scalar")
        p.emit("vector", lambda h: h.tensor_tensor_scan(
            s_de[:], s_de[:], s_u[:], 0.0, op0=A.mult, op1=A.add),
            wait=[("S", p.cnt["S"])], inc="V")
        p.drain("vector")
        p.emit("scalar", lambda h: h.activation(s_de[:], s_de[:], ACT.Ln),
               wait=[("V", p.cnt["V"])], inc="S")
        p.drain("scalar")
        p.emit("vector", lambda h, m=m: h.tensor_tensor(clc(m), s_m[:], s_de[:], op=A.add),
               wait=[("S", p.cnt["S"])], inc="V")
        p.drain("vector")
        p.emit("vector", lambda h, m=m: h.tensor_copy(s_tot[:, m:m + 1], clc(m)[:, TOK1 - 1:TOK1]),
               inc="V")
        p.drain("vector")
    w_scan = p.cnt["V"]

    t_cl = tap("clocal", [D, TOK1])
    if t_cl is not None:
        for k in range(CH):
            p.emit("gpsimd", lambda h, k=k: h.dma_start(
                out=t_cl.ap()[k * 128:(k + 1) * 128, :], in_=clc(k)),
                wait=[("V", w_scan)] if k == 0 else [], inc="G", inc_by=16)

    ag4_in2 = nc.dram_tensor("ag4_in2", [128, CH], F32)
    ag4_out2 = nc.dram_tensor("ag4_out2", [NC, 128, CH], F32)
    p.emit("gpsimd", lambda h: h.dma_start(out=ag4_in2.ap(), in_=s_tot[:]),
           wait=[("V", w_scan)], inc="G", inc_by=16)
    p.wait_only("gpsimd", [("G", p.cnt["G"])])
    p.emit("gpsimd", lambda h: h.collective_compute(
        "AllGather", A.bypass, replica_groups=[list(range(NC))],
        ins=[ag4_in2.ap().opt()], outs=[ag4_out2.ap().opt()]), inc="C")
    wC4 = p.cnt["C"]
    s_ctall = sb("s_ctall", [128, CH * NC])
    p.emit("gpsimd", lambda h: h.dma_start(
        out=s_ctall[:],
        in_=ag4_out2.ap().rearrange("r p c -> p r c"),
        allow_slow_non_contiguous=True),
        wait=[("C", wC4)], inc="G", inc_by=16)
    wG4 = p.cnt["G"]

    s_carry = sb("s_carry", [128, CH])
    s_z8 = sb("s_z8", [128, NC])
    s_m8 = sb("s_m8", [128, 1])
    s_m8n = sb("s_m8n", [128, 1])
    s_se = sb("s_se", [128, 1])
    for k in range(CH):
        p.emit("vector", lambda h, k=k: h.tensor_tensor(
            s_z8[:], s_ctall[:].rearrange("p (r c) -> p r c", c=CH)[:, :, k],
            s_cmask[:], op=A.add),
            wait=[("G", wG4)] if k == 0 else [])
        p.drain("vector")
        p.emit("vector", lambda h: h.tensor_reduce(s_m8[:], s_z8[:], axis=AX.X, op=A.max))
        p.drain("vector")
        p.emit("vector", lambda h: h.tensor_scalar(s_m8n[:], s_m8[:], -1.0, None, op0=A.mult), inc="V")
        p.drain("vector")
        p.emit("scalar", lambda h: h.activation(s_z8[:], s_z8[:], ACT.Exp,
                                                bias=s_m8n[:, 0:1], scale=1.0,
                                                accum_out=s_se[:, 0:1]),
               wait=[("V", p.cnt["V"])], inc="S")
        p.drain("scalar")
        p.emit("scalar", lambda h: h.activation(s_se[:], s_se[:], ACT.Ln), inc="S")
        p.drain("scalar")
        p.emit("vector", lambda h, k=k: h.tensor_tensor(s_carry[:, k:k + 1], s_m8[:], s_se[:], op=A.add),
               wait=[("S", p.cnt["S"])], inc="V")
        p.drain("vector")
    w_carry = p.cnt["V"]
    t_ca = tap("carry", [128, CH])
    if t_ca is not None:
        p.emit("gpsimd", lambda h: h.dma_start(out=t_ca.ap(), in_=s_carry[:]),
               wait=[("V", w_carry)], inc="G", inc_by=16)
    t_as = tap("astar", [D, TOK])
    if t_as is not None:
        p.emit("gpsimd", lambda h: h.dma_start(out=t_as.ap(), in_=a_spill.ap()),
               wait=[("V", w_carry)], inc="G", inc_by=16)

    s_ar0 = s_lth[:, 0:512]
    s_ar1 = s_lth[:, 512:1024]
    s_sx = sb("s_sx2", [128, CH])
    s_sq2 = sb("s_sq2", [128, CH])
    s_c0 = sb("s_c0", [128, 1])
    s_c0q = sb("s_c0q", [128, 1])
    s_sqs = pool2

    for k in range(CH):
        wlda = [("V", w_carry)] if k == 0 else []
        la = p.emit("sync", lambda h, k=k: h.dma_start(
            out=s_ar0, in_=a_spill.ap()[k * 128:(k + 1) * 128, 0:512]),
            wait=wlda + [("V", p.count("V"))], inc="L", inc_by=16)
        p.emit("sync", lambda h, k=k: h.dma_start(
            out=s_ar1, in_=a_spill.ap()[k * 128:(k + 1) * 128, 512:1024]),
            inc="L", inc_by=16)
        wLa = p.cnt["L"]
        p.emit("vector", lambda h, k=k: h.tensor_scalar(
            s_u[:], clc(k), s_carry[:, k:k + 1], None, op0=A.subtract),
            wait=[("V", w_carry)] if k == 0 else [])
        p.emit("vector", lambda h, k=k: h.tensor_scalar(
            s_m[:], clc(k), s_carry[:, k:k + 1], None, op0=A.max), inc="V")
        p.drain("vector")
        p.emit("scalar", lambda h: h.activation(s_u[:], s_u[:], ACT.Abs),
               wait=[("V", p.cnt["V"])], inc="S")
        p.drain("scalar")
        p.emit("scalar", lambda h: h.activation(s_u[:], s_u[:], ACT.Exp, scale=-1.0), inc="S")
        p.drain("scalar")
        p.emit("scalar", lambda h: h.activation(s_u[:], s_u[:], ACT.Ln,
                                                bias=c_one[:, 0:1], scale=1.0), inc="S")
        wS = p.cnt["S"]
        p.emit("vector", lambda h, k=k: h.tensor_tensor(clc(k), s_m[:], s_u[:], op=A.add),
               wait=[("S", wS), ("L", wLa)])
        p.drain("vector")
        p.emit("vector", lambda h, k=k: h.tensor_tensor(
            clc(k)[:, 1:513], clc(k)[:, 1:513], s_ar0, op=A.add))
        p.emit("vector", lambda h, k=k: h.tensor_tensor(
            clc(k)[:, 513:1025], clc(k)[:, 513:1025], s_ar1, op=A.add), inc="V")
        p.drain("vector")
        p.emit("scalar", lambda h, k=k: h.activation(
            s_sqs[:, 0:TOK], clc(k)[:, 1:TOK1], ACT.Square),
            wait=[("V", p.cnt["V"])], inc="S")
        wq_ = p.cnt["S"]
        for w_ in (512, 256, 128, 64, 32, 16, 8, 4, 2, 1):
            p.emit("vector", lambda h, w_=w_: h.tensor_tensor(
                s_sqs[:, 0:w_], s_sqs[:, 0:w_], s_sqs[:, w_:2 * w_], op=A.add),
                wait=[("S", wq_)] if w_ == 512 else [])
            p.drain("vector")
        p.emit("vector", lambda h, k=k: h.tensor_copy(s_sq2[:, k:k + 1], s_sqs[:, 0:1]), inc="V")
        p.drain("vector")
        p.emit("vector", lambda h, k=k: h.tensor_copy(s_sqs[:, 0:TOK], clc(k)[:, 1:TOK1]))
        p.drain("vector")
        for w_ in (512, 256, 128, 64, 32, 16, 8, 4, 2, 1):
            p.emit("vector", lambda h, w_=w_: h.tensor_tensor(
                s_sqs[:, 0:w_], s_sqs[:, 0:w_], s_sqs[:, w_:2 * w_], op=A.add))
            p.drain("vector")
        p.emit("vector", lambda h, k=k: h.tensor_copy(s_sx[:, k:k + 1], s_sqs[:, 0:1]), inc="V")
        p.drain("vector")
        p.emit("vector", lambda h, k=k: h.tensor_scalar(
            s_tmp[:, 0:1], clc(k)[:, 0:1], -1e5, None, op0=A.max), inc="V")
        p.drain("vector")
        if k == 0:
            p.emit("vector", lambda h: h.tensor_copy(s_c0[:], s_tmp[:, 0:1]))
            p.emit("vector", lambda h: h.tensor_tensor(s_c0q[:], s_tmp[:, 0:1], s_tmp[:, 0:1], op=A.mult),
                   inc="V")
        else:
            p.emit("vector", lambda h: h.tensor_tensor(s_c0[:], s_c0[:], s_tmp[:, 0:1], op=A.add))
            p.drain("vector")
            p.emit("vector", lambda h: h.tensor_tensor(s_t1b[:], s_tmp[:, 0:1], s_tmp[:, 0:1], op=A.mult))
            p.drain("vector")
            p.emit("vector", lambda h: h.tensor_tensor(s_c0q[:], s_c0q[:], s_t1b[:], op=A.add),
                   inc="V")
        p.drain("vector")
    p.emit("vector", lambda h: h.tensor_reduce(s_tmp[:, 0:1], s_sx[:], axis=AX.X, op=A.add),
           wait=[("S", p.cnt["S"])])
    p.emit("vector", lambda h: h.tensor_reduce(s_t1b[:], s_sq2[:], axis=AX.X, op=A.add))
    p.drain("vector")
    p.emit("vector", lambda h: h.tensor_tensor(s_c0[:], s_c0[:], s_wsel[:], op=A.mult))
    p.emit("vector", lambda h: h.tensor_tensor(s_c0q[:], s_c0q[:], s_wsel[:], op=A.mult))
    p.drain("vector")
    p.emit("vector", lambda h: h.tensor_tensor(s_tmp[:, 0:1], s_tmp[:, 0:1], s_c0[:], op=A.add))
    p.emit("vector", lambda h: h.tensor_tensor(s_t1b[:], s_t1b[:], s_c0q[:], op=A.add))
    p.drain("vector")
    s_st2 = sb("s_st2", [128, 2])
    s_st2h = sb("s_st2h", [128, 2], BF16)
    s_st2l = sb("s_st2l", [128, 2], BF16)
    p.emit("vector", lambda h: h.tensor_copy(s_st2[:, 0:1], s_tmp[:, 0:1]))
    p.emit("vector", lambda h: h.tensor_copy(s_st2[:, 1:2], s_t1b[:]))
    p.drain("vector")
    p.emit("vector", lambda h: h.tensor_copy(s_st2h[:], s_st2[:]))
    p.drain("vector")
    p.emit("vector", lambda h: h.tensor_tensor(s_st2l[:], s_st2[:], s_st2h[:], op=A.subtract),
           inc="V")
    p.drain("vector")
    ps_g = ps_ssq
    p.emit("tensor", lambda h: h.matmul(ps_g[0:1, 0:2], c_onesb[:, 0:1], s_st2h[:], start=True, stop=False),
           wait=[("V", p.cnt["V"])])
    p.emit("tensor", lambda h: h.matmul(ps_g[0:1, 0:2], c_onesb[:, 0:1], s_st2l[:], start=False, stop=True),
           inc="T")
    s_g8 = sb("s_g8", [1, 8])
    p.emit("vector", lambda h: h.memset(s_g8[:], 0.0), wait=[("T", p.cnt["T"])])
    p.drain("vector")
    p.emit("vector", lambda h: h.tensor_copy(s_g8[:, 0:2], ps_g[0:1, 0:2]), inc="V")
    p.drain("vector")
    ar5b_in = nc.dram_tensor("ar5b_in", [1, 8], F32)
    ar5b_out = nc.dram_tensor("ar5b_out", [1, 8], F32)
    p.emit("gpsimd", lambda h: h.dma_start(out=ar5b_in.ap(), in_=s_g8[:]),
           wait=[("V", p.cnt["V"])], inc="G", inc_by=16)
    p.wait_only("gpsimd", [("G", p.cnt["G"])])
    p.emit("gpsimd", lambda h: h.collective_compute(
        "AllReduce", A.add, replica_groups=[list(range(NC))],
        ins=[ar5b_in.ap().opt()], outs=[ar5b_out.ap().opt()]), inc="C")
    p.emit("gpsimd", lambda h: h.dma_start(out=s_g8[:], in_=ar5b_out.ap()),
           wait=[("C", p.cnt["C"])], inc="G", inc_by=16)
    wG5 = p.cnt["G"]
    t_g8 = tap("stats8", [1, 8])
    if t_g8 is not None:
        p.emit("gpsimd", lambda h: h.dma_start(out=t_g8.ap(), in_=s_g8[:]),
               wait=[("G", wG5)], inc="G", inc_by=16)
    s_mean = sb("s_mean", [1, 1])
    s_rst = sb("s_rst", [1, 1])
    s_ngb = sb("s_ngb", [1, 1])
    p.emit("vector", lambda h: h.tensor_scalar(s_mean[:], s_g8[:, 0:1], 1.0 / NTOT, None, op0=A.mult),
           wait=[("G", wG5)])
    p.drain("vector")
    p.emit("vector", lambda h: h.tensor_tensor(s_rst[:], s_g8[:, 0:1], s_mean[:], op=A.mult))
    p.drain("vector")
    p.emit("vector", lambda h: h.tensor_tensor(s_rst[:], s_g8[:, 1:2], s_rst[:], op=A.subtract))
    p.drain("vector")
    p.emit("vector", lambda h: h.tensor_scalar(s_rst[:], s_rst[:], 1.0 / (NTOT - 1.0), None, op0=A.mult),
           inc="V")
    p.drain("vector")
    p.emit("scalar", lambda h: h.activation(s_rst[:], s_rst[:], ACT.Sqrt),
           wait=[("V", p.cnt["V"])], inc="S")
    p.emit("vector", lambda h: h.reciprocal(s_rst[:], s_rst[:]),
           wait=[("S", p.cnt["S"])], inc="V")
    p.drain("vector")
    p.emit("vector", lambda h: h.tensor_tensor(s_ngb[:], s_mean[:], s_rst[:], op=A.mult))
    p.drain("vector")
    p.emit("vector", lambda h: h.tensor_scalar(s_ngb[:], s_ngb[:], -1.0, None, op0=A.mult),
           inc="V")
    p.drain("vector")
    s_rstb = sb("s_rstb", [128, 1])
    s_ngbb = sb("s_ngbb", [128, 1])
    p.emit("gpsimd", lambda h: h.dma_start(out=bc_scr.ap()[2:3, 0:1], in_=s_rst[:]),
           wait=[("V", p.cnt["V"])], inc="G", inc_by=16)
    p.emit("gpsimd", lambda h: h.dma_start(out=bc_scr.ap()[3:4, 0:1], in_=s_ngb[:]),
           inc="G", inc_by=16)
    p.wait_only("gpsimd", [("G", p.cnt["G"])])
    p.emit("gpsimd", lambda h: h.dma_start(
        out=s_rstb[:], in_=bc_scr.ap()[2:3, 0:1].broadcast_to([128, 1])), inc="G", inc_by=16)
    p.emit("gpsimd", lambda h: h.dma_start(
        out=s_ngbb[:], in_=bc_scr.ap()[3:4, 0:1].broadcast_to([128, 1])), inc="G", inc_by=16)
    wGb = p.cnt["G"]

    for k in range(CH):
        p.emit("scalar", lambda h, k=k: h.activation(
            clc(k)[:, 1:TOK1], clc(k)[:, 1:TOK1], ACT.Exp,
            bias=s_ngbb[:, 0:1], scale=s_rstb[:, 0:1]),
            wait=[("G", wGb)] if k == 0 else [], inc="S")
    p.drain("scalar")
    wS_h = p.cnt["S"]
    for k in range(CH):
        p.emit("vector", lambda h, k=k: h.tensor_tensor(s_tmp[:], xnh(k), xnl(k), op=A.add),
               wait=[("S", wS_h)] if k == 0 else [])
        p.drain("vector")
        p.emit("vector", lambda h, k=k: h.tensor_tensor(
            clc(k)[:, 1:TOK1], clc(k)[:, 1:TOK1], s_tmp[:], op=A.add), inc="V")
        p.drain("vector")
    w_xa = p.cnt["V"]

    w_s = emit_ln_fm(lambda k: clc(k)[:, 1:TOK1], lambda k: clc(k)[:, 1:TOK1],
                     ("V", w_xa), c_eps5[0:1, 0:1],
                     gam_col=lambda k: s_lnw[:, k:k + 1],
                     beta_col=lambda k: s_lnb[:, k:k + 1])

    t_s0 = tap("s0", [D, TOK])
    for k in range(CH):
        p.emit("gpsimd", lambda h, k=k: h.dma_start(
            out=s_spill.ap()[k * 128:(k + 1) * 128, :], in_=clc(k)[:, 1:TOK1]),
            wait=[("V", w_s)] if k == 0 else [], inc="G", inc_by=16)
        if t_s0 is not None:
            p.emit("gpsimd", lambda h, k=k: h.dma_start(
                out=t_s0.ap()[k * 128:(k + 1) * 128, :], in_=clc(k)[:, 1:TOK1]),
                inc="G", inc_by=16)

    w_sspill = p.cnt["G"]

    gwN = inp("gwN", [1024, D])
    upN = inp("upN", [1024, D])
    dwN = inp("dwN", [256, DFF])
    x1_spill = nc.dram_tensor("x1_spill", [D, TOK], F32)
    SLC = 1024 * D + 1024 * D + 256 * DFF
    GOFF, UOFF, DOFF = 0, 1024 * D, 2 * 1024 * D
    ag2_in = nc.dram_tensor("ag2_in", [2 * SLC], BF16)
    ag2_out = nc.dram_tensor("ag2_out", [NC, 2 * SLC], BF16)

    free_back_to("s_h0")
    _bar = [("V", p.count("V")), ("S", p.count("S")), ("T", p.count("T")),
            ("G", p.count("G")), ("L", p.count("L"))]
    for _e in ENGINES:
        p.wait_only(_e, _bar)

    s_swq = sb("s_swq", [128, 4])
    s_swm = sb("s_swm", [128, 4])
    dqx_sp = nc.dram_tensor("dqx_sp", [2, D, TOK], BF16)
    s_dqxh = sb("s_dqxh", [128, CH * 512], BF16)
    s_dqxl = sb("s_dqxl", [128, CH * 512], BF16)
    s_dacc = sb("s_dacc", [128, CH * 512])

    def dqxh(kk):
        return s_dqxh[:, kk * 512:(kk + 1) * 512]

    def dqxl(kk):
        return s_dqxl[:, kk * 512:(kk + 1) * 512]

    def dacc(o):
        return s_dacc[:, o * 512:(o + 1) * 512]

    QW = 1024
    qs_src = sb("qs_src", [128, 1024])
    qs_t = sb("qs_t", [128, QW])
    qs_sq = sb("qs_sq", [128, QW])
    qs_r = sb("qs_r", [128, QW])
    qs_bm = sb("qs_bm", [128, QW // 2])
    qs_sc = sb("qs_sc", [128, 2 * (QW // 32)])
    qs_ev = sb("qs_ev", [128, 128])
    ps_tr = ps("ps_tr", [128, 128])
    ps_gu = ps("ps_gu", [128, 512])
    ps_d0 = ps("ps_d0", [128, 512])
    ps_d1 = ps("ps_d1", [128, 512])

    def emit_mxq(src, dst, W, wait=()):
        nb = W // 32
        p.emit("scalar", lambda h: h.activation(qs_sq[:, 0:W], src, ACT.Abs),
               wait=list(wait), inc="S")
        p.emit("scalar", lambda h: h.activation(qs_r[:, 0:W], src, ACT.Sign),
               inc="S")
        wS0 = p.cnt["S"]
        a3 = qs_sq[:, 0:W].rearrange("p (b q) -> p b q", q=32)
        b3 = qs_bm[:, 0:nb * 16].rearrange("p (b q) -> p b q", q=16)
        p.emit("vector", lambda h: h.tensor_tensor(b3[:, :, 0:16], a3[:, :, 0:16], a3[:, :, 16:32], op=A.max),
               wait=[("S", wS0)])
        p.drain("vector")
        p.emit("vector", lambda h: h.tensor_tensor(b3[:, :, 0:8], b3[:, :, 0:8], b3[:, :, 8:16], op=A.max))
        p.drain("vector")
        p.emit("vector", lambda h: h.tensor_tensor(b3[:, :, 0:4], b3[:, :, 0:4], b3[:, :, 4:8], op=A.max))
        p.drain("vector")
        p.emit("vector", lambda h: h.tensor_tensor(b3[:, :, 0:2], b3[:, :, 0:2], b3[:, :, 2:4], op=A.max))
        p.drain("vector")
        p.emit("vector", lambda h: h.tensor_tensor(b3[:, :, 0:1], b3[:, :, 0:1], b3[:, :, 1:2], op=A.max))
        p.drain("vector")
        p.emit("vector", lambda h: h.tensor_scalar(
            qs_sc[:, 0:nb], b3[:, :, 0:1], 1.0 / 448.0, 1e-30, op0=A.mult, op1=A.max))
        p.drain("vector")
        scb = qs_sc[:, 0:nb].rearrange("p (b o) -> p b o", o=1).broadcast_to([128, nb, 32])
        rscb = qs_sc[:, nb:2 * nb].rearrange("p (b o) -> p b o", o=1).broadcast_to([128, nb, 32])
        r3 = qs_r[:, 0:W].rearrange("p (b q) -> p b q", q=32)
        p.emit("vector", lambda h: h.reciprocal(qs_sc[:, nb:2 * nb], qs_sc[:, 0:nb]))
        p.emit("vector", lambda h: h.tensor_tensor(r3, r3, scb, op=A.mult))
        p.drain("vector")
        p.emit("vector", lambda h: h.tensor_tensor(
            qs_sq[:, 0:W].rearrange("p (b q) -> p b q", q=32), a3, rscb, op=A.mult))
        p.drain("vector")
        p.emit("vector", lambda h: h.tensor_scalar(
            qs_t[:, 0:W].bitcast(U32), qs_sq[:, 0:W].bitcast(U32), 0x7F800000, None,
            op0=A.bitwise_and))
        p.drain("vector")
        p.emit("vector", lambda h: h.tensor_scalar(qs_t[:, 0:W], qs_t[:, 0:W], 2.0 ** 20, 2.0 ** 14,
                                                   op0=A.mult, op1=A.max))
        p.drain("vector")
        p.emit("vector", lambda h: h.tensor_tensor(
            qs_sq[:, 0:W], qs_sq[:, 0:W], qs_t[:, 0:W], op=A.add))
        p.drain("vector")
        p.emit("vector", lambda h: h.tensor_tensor(
            qs_sq[:, 0:W], qs_sq[:, 0:W], qs_t[:, 0:W], op=A.subtract))
        p.drain("vector")
        p.emit("vector", lambda h: h.tensor_tensor(dst, qs_sq[:, 0:W], qs_r[:, 0:W], op=A.mult),
               inc="V")
        p.drain("vector")
        return p.cnt["V"]

    qs_evb = sb("qs_evb", [128, 128], BF16)
    qs_evlb = sb("qs_evlb", [128, 128], BF16)

    def stage_weight(src_dram, rows, width, base_off, out_rowlen):
        ntile = rows // 128
        for ti in range(ntile):
            for sub in range(width // QW):
                p.emit("sync", lambda h, ti=ti, sub=sub: h.dma_start(
                    out=qs_src[:, 0:QW],
                    in_=src_dram.ap()[ti * 128:(ti + 1) * 128, sub * QW:(sub + 1) * QW]),
                    wait=[("V", p.count("V")), ("T", p.count("T"))], inc="L", inc_by=16)
                emit_mxq(qs_src[:, 0:QW], qs_src[:, 0:QW], QW,
                         wait=[("L", p.cnt["L"])])
                for tj in range(QW // 128):
                    p.emit("tensor", lambda h, tj=tj: h.transpose(
                        ps_tr[:], qs_src[:, tj * 128:(tj + 1) * 128], c_ident[:]),
                        wait=[("V", p.count("V")), ("S", p.count("S"))], inc="T")
                    p.emit("scalar", lambda h: h.copy(qs_evb[:], ps_tr[:]),
                           wait=[("T", p.cnt["T"]), ("G", p.count("G"))], inc="S")
                    p.emit("vector", lambda h: h.tensor_tensor(
                        qs_evlb[:], ps_tr[:], qs_evb[:], op=A.subtract),
                        wait=[("S", p.cnt["S"]), ("G", p.count("G"))], inc="V")
                    row = (base_off + (sub * QW + tj * 128) * out_rowlen) // out_rowlen
                    p.emit("gpsimd", lambda h, row=row, ti=ti, rl=out_rowlen: h.dma_start(
                        out=ag2_in.ap()[0:SLC].rearrange("(a b) -> a b", b=rl)[
                            row:row + 128, ti * 128:ti * 128 + 128],
                        in_=qs_evb[:]),
                        wait=[("S", p.cnt["S"])], inc="G", inc_by=16)
                    p.emit("gpsimd", lambda h, row=row, ti=ti, rl=out_rowlen: h.dma_start(
                        out=ag2_in.ap()[SLC:2 * SLC].rearrange("(a b) -> a b", b=rl)[
                            row:row + 128, ti * 128:ti * 128 + 128],
                        in_=qs_evlb[:]),
                        wait=[("V", p.cnt["V"])], inc="G", inc_by=16)

    stage_weight(gwN, 1024, D, GOFF, 1024)
    stage_weight(upN, 1024, D, UOFF, 1024)
    stage_weight(dwN, 256, DFF, DOFF, 256)
    wG2 = p.cnt["G"]
    p.wait_only("gpsimd", [("G", wG2)])
    p.emit("gpsimd", lambda h: h.collective_compute(
        "AllGather", A.bypass, replica_groups=[list(range(NC))],
        ins=[ag2_in.ap().opt()], outs=[ag2_out.ap().opt()]), inc="C")
    wC2 = p.cnt["C"]

    s_xtm = sb("s_xtm", [128, D])
    s_xnhb = sb("s_xnhb", [128, 2048], BF16)
    s_xnlb = sb("s_xnlb", [128, 2048], BF16)
    for tc in range(TOK // 128):
        p.emit("sync", lambda h, tc=tc: h.dma_start(
            out=s_xnhb[:], in_=xnh_sp.ap().rearrange("(p c) t -> p c t", p=128)[
                :, :, tc * 128:(tc + 1) * 128]),
            wait=[("T", p.count("T")), ("V", p.count("V")), ("S", p.count("S")),
                  ("G", p.count("G"))],
            inc="L", inc_by=16)
        p.emit("sync", lambda h, tc=tc: h.dma_start(
            out=s_xnlb[:], in_=xnl_sp.ap().rearrange("(p c) t -> p c t", p=128)[
                :, :, tc * 128:(tc + 1) * 128]),
            inc="L", inc_by=16)
        wLx = p.cnt["L"]
        for hf2 in range(2):
            p.emit("vector", lambda h, hf2=hf2: h.tensor_tensor(
                qs_src[:], s_xnhb[:, hf2 * 1024:(hf2 + 1) * 1024],
                s_xnlb[:, hf2 * 1024:(hf2 + 1) * 1024], op=A.add),
                wait=[("L", wLx), ("T", p.count("T"))], inc="V")
            p.drain("vector")
            for kk2 in range(8):
                kk = hf2 * 8 + kk2
                p.emit("tensor", lambda h, kk2=kk2: h.transpose(
                    ps_tr[:], qs_src[:, kk2 * 128:(kk2 + 1) * 128], c_ident[:]),
                    wait=[("V", p.count("V"))], inc="T")
                p.emit("vector", lambda h, kk=kk: h.tensor_copy(s_xtm[:, kk * 128:(kk + 1) * 128], ps_tr[:]),
                       wait=[("T", p.cnt["T"])], inc="V")
        for sub in range(D // QW):
            emit_mxq(s_xtm[:, sub * QW:(sub + 1) * QW], s_xtm[:, sub * QW:(sub + 1) * QW], QW,
                     wait=[("V", p.count("V"))])
        wq = p.cnt["V"]
        for kk in range(CH):
            p.emit("tensor", lambda h, kk=kk: h.transpose(
                ps_tr[:], s_xtm[:, kk * 128:(kk + 1) * 128], c_ident[:]),
                wait=[("V", p.count("V")), ("S", p.count("S")), ("G", p.count("G"))], inc="T")
            p.emit("scalar", lambda h, kk=kk: h.copy(s_xnhb[:, 0:128], ps_tr[:]),
                   wait=[("T", p.cnt["T"]), ("G", p.count("G"))], inc="S")
            p.emit("vector", lambda h, kk=kk: h.tensor_tensor(
                s_xnlb[:, 0:128], ps_tr[:], s_xnhb[:, 0:128], op=A.subtract),
                wait=[("S", p.cnt["S"]), ("G", p.count("G"))], inc="V")
            p.emit("gpsimd", lambda h, kk=kk, tc=tc: h.dma_start(
                out=dqx_sp.ap()[0, kk * 128:(kk + 1) * 128, tc * 128:(tc + 1) * 128],
                in_=s_xnhb[:, 0:128]),
                wait=[("S", p.cnt["S"])], inc="G", inc_by=16)
            p.emit("gpsimd", lambda h, kk=kk, tc=tc: h.dma_start(
                out=dqx_sp.ap()[1, kk * 128:(kk + 1) * 128, tc * 128:(tc + 1) * 128],
                in_=s_xnlb[:, 0:128]),
                inc="G", inc_by=16)
    w_dqx = p.cnt["G"]
    t_dqx = tap("dqx", [D, TOK])
    if t_dqx is not None:
        for kk in range(CH):
            p.emit("sync", lambda h, kk=kk: h.dma_start(
                out=s_xnhb[:, 0:TOK], in_=dqx_sp.ap()[0, kk * 128:(kk + 1) * 128, :]),
                wait=[("G", w_dqx), ("V", p.count("V"))], inc="L", inc_by=16)
            p.emit("sync", lambda h, kk=kk: h.dma_start(
                out=s_xnlb[:, 0:TOK], in_=dqx_sp.ap()[1, kk * 128:(kk + 1) * 128, :]),
                inc="L", inc_by=16)
            p.emit("vector", lambda h: h.tensor_tensor(
                s_xtm[:, 0:TOK], s_xnhb[:, 0:TOK], s_xnlb[:, 0:TOK], op=A.add),
                wait=[("L", p.cnt["L"])], inc="V")
            p.drain("vector")
            p.emit("gpsimd", lambda h, kk=kk: h.dma_start(
                out=t_dqx.ap()[kk * 128:(kk + 1) * 128, :], in_=s_xtm[:, 0:TOK]),
                wait=[("V", p.cnt["V"])], inc="G", inc_by=16)
            p.wait_only("vector", [("G", p.cnt["G"])])
    t_gw = tap("dqgw0", [128, 1024])
    if t_gw is not None:
        p.emit("sync", lambda h: h.dma_start(
            out=s_xnhb[:, 0:1024],
            in_=ag2_out.ap()[0, 0:128 * 1024].rearrange("(a b) -> a b", b=1024)),
            wait=[("C", wC2), ("G", p.count("G")), ("S", p.count("S")),
                  ("V", p.count("V"))], inc="L", inc_by=16)
        p.emit("sync", lambda h: h.dma_start(
            out=s_xnlb[:, 0:1024],
            in_=ag2_out.ap()[0, SLC:SLC + 128 * 1024].rearrange("(a b) -> a b", b=1024)),
            inc="L", inc_by=16)
        p.emit("vector", lambda h: h.tensor_tensor(
            s_xtm[:, 0:1024], s_xnhb[:, 0:1024], s_xnlb[:, 0:1024], op=A.add),
            wait=[("L", p.cnt["L"])], inc="V")
        p.drain("vector")
        p.emit("gpsimd", lambda h: h.dma_start(
            out=t_gw.ap(), in_=s_xtm[:, 0:1024]),
            wait=[("V", p.cnt["V"])], inc="G", inc_by=16)
        p.wait_only("vector", [("G", p.cnt["G"])])
    wqN = inp("wqN", [256, D])
    wkN = inp("wkN", [256, D])
    wvN = inp("wvN", [256, D])
    woN = inp("woN", [256, D])
    NW = 2048 * 256
    ag3_in = nc.dram_tensor("ag3_in", [4 * NW], BF16)
    ag3_out = nc.dram_tensor("ag3_out", [NC, 4 * NW], BF16)
    ar1_in = nc.dram_tensor("ar1_in", [1, 8], F32)
    ar1_out = nc.dram_tensor("ar1_out", [1, 8], F32)
    pp_w2k = sb("pp_w2k", [128, D])
    pp_a = sb("pp_a", [128, D])
    pp_b2x = sb("pp_b2x", [128, D])
    pp_c2x = sb("pp_c2x", [128, D])
    pp_tq = sb("pp_tq", [128, D], BF16)
    pp_ev = sb("pp_ev", [128, 128], BF16)
    pp_c8 = sb("pp_c8", [128, 16])
    pp_w8h = sb("pp_w8h", [128, 8], BF16)
    pp_w8l = sb("pp_w8l", [128, 8], BF16)
    pp_trb = ps("pp_trb", [128, 128], BF16)
    pp_sm = ps("pp_sm", [1, 8])
    for wi, wsrc in enumerate((wqN, wkN, wvN, woN)):
        for ti in range(2):
            p.emit("sync", lambda h, wsrc=wsrc, ti=ti: h.dma_start(
                out=pp_w2k[:], in_=wsrc.ap()[ti * 128:(ti + 1) * 128, :]),
                wait=[("S", p.count("S")), ("V", p.count("V"))], inc="L", inc_by=16)
            p.emit("scalar", lambda h: h.activation(pp_a[:], pp_w2k[:], ACT.Abs),
                   wait=[("L", p.cnt["L"])], inc="S")
            wS_ = p.cnt["S"]
            for w_ in (1024, 512, 256, 128, 64, 32, 16, 8, 4, 2, 1):
                p.emit("vector", lambda h, w_=w_: h.tensor_tensor(
                    pp_a[:, 0:w_], pp_a[:, 0:w_], pp_a[:, w_:2 * w_], op=A.add),
                    wait=[("S", wS_)] if w_ == 1024 else [])
                p.drain("vector")
            p.emit("vector", lambda h, wi=wi, ti=ti: h.tensor_copy(
                pp_c8[:, (wi * 2 + ti):(wi * 2 + ti) + 1], pp_a[:, 0:1]), inc="V")
            p.drain("vector")
    p.emit("vector", lambda h: h.tensor_copy(pp_w8h[:], pp_c8[:, 0:8]))
    p.drain("vector")
    p.emit("vector", lambda h: h.tensor_tensor(pp_w8l[:], pp_c8[:, 0:8], pp_w8h[:], op=A.subtract),
           inc="V")
    p.drain("vector")
    p.emit("tensor", lambda h: h.matmul(pp_sm[0:1, 0:8], c_onesb[:, 0:1], pp_w8h[:], start=True, stop=False),
           wait=[("V", p.cnt["V"])])
    p.emit("tensor", lambda h: h.matmul(pp_sm[0:1, 0:8], c_onesb[:, 0:1], pp_w8l[:], start=False, stop=True),
           inc="T")
    p.emit("vector", lambda h: h.tensor_copy(pp_a[0:1, 0:8], pp_sm[0:1, 0:8]),
           wait=[("T", p.cnt["T"])], inc="V")
    p.drain("vector")
    p.emit("vector", lambda h: h.tensor_tensor(
        pp_a[0:1, 8:12],
        pp_a[0:1, 0:8].rearrange("p (a b) -> p a b", b=2)[:, :, 0],
        pp_a[0:1, 0:8].rearrange("p (a b) -> p a b", b=2)[:, :, 1], op=A.add), inc="V")
    p.drain("vector")
    p.emit("vector", lambda h: h.tensor_copy(pp_a[0:1, 0:4], pp_a[0:1, 8:12]), inc="V")
    p.drain("vector")
    p.emit("gpsimd", lambda h: h.dma_start(out=ar1_in.ap()[0:1, 0:4], in_=pp_a[0:1, 0:4]),
           wait=[("V", p.cnt["V"])], inc="G", inc_by=16)
    p.emit("gpsimd", lambda h: h.dma_start(
        out=ar1_in.ap()[0:1, 4:8], in_=pp_a[0:1, 0:4]), inc="G", inc_by=16)
    p.wait_only("gpsimd", [("G", p.cnt["G"])])
    p.emit("gpsimd", lambda h: h.collective_compute(
        "AllReduce", A.add, replica_groups=[list(range(NC))],
        ins=[ar1_in.ap().opt()], outs=[ar1_out.ap().opt()]), inc="C")
    p.emit("gpsimd", lambda h: h.dma_start(out=pp_a[0:1, 0:8], in_=ar1_out.ap()),
           wait=[("C", p.cnt["C"])], inc="G", inc_by=16)
    wGr1 = p.cnt["G"]
    p.emit("vector", lambda h: h.tensor_scalar(
        pp_a[0:1, 0:4], pp_a[0:1, 0:4], 1.0 / (D * D), 1e-5, op0=A.mult, op1=A.max),
        wait=[("G", wGr1)], inc="V")
    p.drain("vector")
    p.emit("vector", lambda h: h.reciprocal(pp_a[0:1, 4:8], pp_a[0:1, 0:4]), inc="V")
    p.drain("vector")
    p.emit("gpsimd", lambda h: h.dma_start(out=bc_scr.ap()[0:1, 0:4], in_=pp_a[0:1, 4:8]),
           wait=[("V", p.cnt["V"])], inc="G", inc_by=16)
    p.emit("gpsimd", lambda h: h.dma_start(out=bc_scr.ap()[1:2, 0:4], in_=pp_a[0:1, 0:4]),
           inc="G", inc_by=16)
    p.wait_only("gpsimd", [("G", p.cnt["G"])])
    p.emit("gpsimd", lambda h: h.dma_start(
        out=s_swq[:], in_=bc_scr.ap()[0:1, 0:4].broadcast_to([128, 4])), inc="G", inc_by=16)
    p.emit("gpsimd", lambda h: h.dma_start(
        out=s_swm[:], in_=bc_scr.ap()[1:2, 0:4].broadcast_to([128, 4])), inc="G", inc_by=16)
    wGswq = p.cnt["G"]

    for wi, wsrc in enumerate((wqN, wkN, wvN, woN)):
        for ti in range(2):
            p.emit("sync", lambda h, wsrc=wsrc, ti=ti: h.dma_start(
                out=pp_w2k[:], in_=wsrc.ap()[ti * 128:(ti + 1) * 128, :]),
                wait=[("V", p.count("V")), ("T", p.count("T")), ("S", p.count("S"))],
                inc="L", inc_by=16)
            p.emit("vector", lambda h, wi=wi: h.tensor_scalar(
                pp_a[:], pp_w2k[:], s_swq[:, wi:wi + 1], None, op0=A.mult),
                wait=[("L", p.cnt["L"]), ("G", wGswq)])
            p.drain("vector")
            p.emit("vector", lambda h: h.tensor_scalar(pp_b2x[:], pp_a[:], 0.5, None, op0=A.is_gt))
            p.drain("vector")
            p.emit("vector", lambda h: h.tensor_scalar(pp_c2x[:], pp_a[:], -0.5, None, op0=A.is_lt))
            p.drain("vector")
            p.emit("vector", lambda h: h.tensor_tensor(pp_tq[:], pp_b2x[:], pp_c2x[:], op=A.subtract),
                   inc="V")
            p.drain("vector")
            for tj in range(CH):
                p.emit("tensor", lambda h, tj=tj: h.transpose(
                    pp_trb[:], pp_tq[:, tj * 128:(tj + 1) * 128], c_identb[:]),
                    wait=[("V", p.count("V")), ("G", p.count("G"))], inc="T")
                p.emit("vector", lambda h: h.tensor_copy(pp_ev[:], pp_trb[:]),
                       wait=[("T", p.cnt["T"])], inc="V")
                row = tj * 128
                p.emit("gpsimd", lambda h, wi=wi, row=row, ti=ti: h.dma_start(
                    out=ag3_in.ap()[wi * NW:(wi + 1) * NW].rearrange("(a b) -> a b", b=256)[
                        row:row + 128, ti * 128:ti * 128 + 128],
                    in_=pp_ev[:]),
                    wait=[("V", p.cnt["V"])], inc="G", inc_by=16)
    wG3 = p.cnt["G"]
    p.wait_only("gpsimd", [("G", wG3)])
    p.emit("gpsimd", lambda h: h.collective_compute(
        "AllGather", A.bypass, replica_groups=[list(range(NC))],
        ins=[ag3_in.ap().opt()], outs=[ag3_out.ap().opt()]), inc="C")
    wC3 = p.cnt["C"]
    psfree("pp_sm")
    psfree("pp_trb")
    for _n in ("pp_w8l", "pp_w8h", "pp_c8", "pp_ev", "pp_tq", "pp_c2x", "pp_b2x",
               "pp_a", "pp_w2k"):
        sbfree(_n)
    sbfree("s_xnlb")
    sbfree("s_xnhb")
    sbfree("s_xtm")
    _bar2 = [("V", p.count("V")), ("T", p.count("T")), ("L", p.count("L"))]
    for _e in ENGINES:
        p.wait_only(_e, _bar2)

    JG = 256
    NJG = DFF // JG
    s_wg = [sb("s_wg0", [128, CH * 2 * JG], BF16), sb("s_wg1", [128, CH * 2 * JG], BF16)]
    s_wgl = [sb("s_wgl0", [128, CH * 2 * JG], BF16), sb("s_wgl1", [128, CH * 2 * JG], BF16)]
    s_wd = [sb("s_wd0", [128, 2 * D], BF16), sb("s_wd1", [128, 2 * D], BF16)]
    s_wdl = [sb("s_wdl0", [128, 2 * D], BF16), sb("s_wdl1", [128, 2 * D], BF16)]
    s_hA = qs_src
    s_hB = sb("s_hB", [128, 1024])
    s_dqfh = sb("s_dqfh", [128, 2 * 512], BF16)
    s_dqfl = sb("s_dqfl", [128, 2 * 512], BF16)
    ps_gu2 = ps("ps_gu2", [128, 512])
    ps_gu3 = ps("ps_gu3", [128, 512])
    ps_gu4 = ps("ps_gu4", [128, 512])
    GUB = [ps_gu, ps_gu2, ps_gu3, ps_gu4]
    t_g0 = tap("g0", [D, TOK])
    t_x1 = tap("x1", [D, TOK])
    s_xs = sb("s_xs", [128, 512])
    s_ss = sb("s_ss", [128, 512])

    def emit_wload(jg, buf, first_wait):
        slot = (jg * JG) // 1024
        col = (jg * JG) % 1024
        for hl, (wgt, wdt) in ((0, (s_wg, s_wd)), (1, (s_wgl, s_wdl))):
            hoff = hl * SLC
            for gu, boff in ((0, GOFF), (1, UOFF)):
                src = ag2_out.ap()[slot, hoff + boff:hoff + boff + CH * 128 * 1024].rearrange(
                    "(k q c) -> q k c", q=128, c=1024)[:, :, col:col + JG]
                dst = wgt[buf][:, :].rearrange("q (k c) -> q k c", c=2 * JG)[
                    :, :, gu * JG:(gu + 1) * JG]
                p.emit("sync", lambda h, src=src, dst=dst: h.dma_start(out=dst, in_=src),
                       wait=(list(first_wait) if (hl == 0 and gu == 0) else []),
                       inc="L", inc_by=16)
            for jc in range(2):
                roff = hoff + DOFF + (jg * JG + jc * 128) * 256
                src = ag2_out.ap()[:, roff:roff + 128 * 256].rearrange(
                    "s (q c) -> q s c", c=256)
                dst = wdt[buf][:, jc * D:(jc + 1) * D].rearrange("q (o c) -> q o c", c=256)
                p.emit("sync", lambda h, src=src, dst=dst: h.dma_start(out=dst, in_=src),
                       inc="L", inc_by=16)
        return p.cnt["L"]

    w_wld_map = {}
    for th in range(2):
        for kk in range(CH):
            p.emit("sync", lambda h, kk=kk, th=th: h.dma_start(
                out=dqxh(kk), in_=dqx_sp.ap()[0, kk * 128:(kk + 1) * 128, th * 512:(th + 1) * 512]),
                wait=[("T", p.count("T")), ("G", w_dqx)] if kk == 0 else [],
                inc="L", inc_by=16)
            p.emit("sync", lambda h, kk=kk, th=th: h.dma_start(
                out=dqxl(kk), in_=dqx_sp.ap()[1, kk * 128:(kk + 1) * 128, th * 512:(th + 1) * 512]),
                inc="L", inc_by=16)
        wLdq = p.cnt["L"]
        for o in range(CH):
            p.emit("vector", lambda h, o=o: h.memset(dacc(o), 0.0),
                   wait=[("S", p.count("S")), ("G", p.count("G"))] if o == 0 else [],
                   inc="V" if o == CH - 1 else None)
        p.drain("vector")

        def emit_gu_T(jg):
            w_wld = w_wld_map.get((th, jg), p.cnt["L"])
            buf = jg % 2
            stops = []
            for tci in range(4):
                pg = GUB[tci]
                first = True
                for pi, (xs_, ws_) in enumerate(((0, 0), (0, 1), (1, 0))):
                    for kk in range(CH):
                        st = (pi == 0 and kk == 0)
                        sp = (pi == 2 and kk == CH - 1)
                        xop = (dqxh if xs_ == 0 else dqxl)(kk)[:, tci * 128:(tci + 1) * 128]
                        wop = (s_wg[buf] if ws_ == 0 else s_wgl[buf])[:, kk * 2 * JG:(kk + 1) * 2 * JG]
                        wt = ([("L", w_wld), ("V", p.count("V")), ("S", p.count("S")), ("L", wLdq)]
                              if first else [])
                        first = False
                        p.emit("tensor", lambda h, xop=xop, wop=wop, st=st, sp=sp, pg=pg: h.matmul(
                            pg[:], xop, wop, start=st, stop=sp),
                            wait=wt, inc="T" if sp else None)
                stops.append(p.cnt["T"])
            return stops

        if th == 0:
            w_wld_map[(0, 0)] = emit_wload(0, 0, [("C", wC2)])
            w_wld_map[(0, 1)] = emit_wload(1, 1, [])
        gu_stops = emit_gu_T(0)
        for jg in range(NJG):
            buf = jg % 2
            s_h = s_hA if jg % 2 == 0 else s_hB
            for tci in range(4):
                pg = GUB[tci]
                hsl = slice(tci * JG, (tci + 1) * JG)
                p.emit("scalar", lambda h, hsl=hsl, pg=pg, s_h=s_h: h.activation(
                    s_h[:, hsl], pg[:, 0:JG], ACT.Silu),
                    wait=[("T", gu_stops[tci])], inc="S")
                p.emit("vector", lambda h, hsl=hsl, pg=pg, s_h=s_h: h.tensor_tensor(
                    s_h[:, hsl], s_h[:, hsl], pg[:, JG:2 * JG], op=A.mult),
                    wait=[("S", p.cnt["S"])], inc="V")
            nstops = emit_gu_T(jg + 1) if jg + 1 < NJG else None
            emit_mxq(s_h[:], s_h[:], 4 * JG, wait=[("V", p.count("V"))])
            for tci in range(4):
                tcol = tci * 128
                for jc in range(2):
                    p.emit("tensor", lambda h, tci=tci, jc=jc, s_h=s_h: h.transpose(
                        ps_tr[:], s_h[:, tci * JG + jc * 128:tci * JG + (jc + 1) * 128], c_ident[:]),
                        wait=[("V", p.count("V")), ("S", p.count("S"))], inc="T")
                    p.emit("scalar", lambda h, jc=jc, tcol=tcol: h.copy(
                        s_dqfh[:, jc * 512 + tcol:jc * 512 + tcol + 128], ps_tr[:]),
                        wait=[("T", p.cnt["T"])], inc="S")
                    p.emit("vector", lambda h, jc=jc, tcol=tcol: h.tensor_tensor(
                        s_dqfl[:, jc * 512 + tcol:jc * 512 + tcol + 128], ps_tr[:],
                        s_dqfh[:, jc * 512 + tcol:jc * 512 + tcol + 128], op=A.subtract),
                        wait=[("S", p.cnt["S"])], inc="V")
            w_dqf = p.cnt["V"]
            w_dqfS = p.cnt["S"]
            for o in range(CH):
                pd = ps_d0 if o % 2 == 0 else ps_d1
                first = True
                for pi, (hs_, ws_) in enumerate(((0, 0), (0, 1), (1, 0))):
                    for jc in range(2):
                        st = (pi == 0 and jc == 0)
                        sp = (pi == 2 and jc == 1)
                        hop = (s_dqfh if hs_ == 0 else s_dqfl)[:, jc * 512:(jc + 1) * 512]
                        wop = (s_wd[buf] if ws_ == 0 else s_wdl[buf])[:, jc * D + o * 128:jc * D + (o + 1) * 128]
                        wt = ([("V", w_dqf), ("S", w_dqfS)] if first else [])
                        if first:
                            wt.append(("V", p.count("V")))
                        first = False
                        p.emit("tensor", lambda h, hop=hop, wop=wop, st=st, sp=sp, pd=pd: h.matmul(
                            pd[:], wop, hop, start=st, stop=sp),
                            wait=wt, inc="T" if sp else None)
                wTd = p.cnt["T"]
                p.emit("vector", lambda h, o=o, pd=pd: h.tensor_tensor(
                    dacc(o), dacc(o), pd[:], op=A.add),
                    wait=[("T", wTd)], inc="V")
            p.drain("vector")
            nxt = jg + 2
            if th == 0 and nxt < NJG:
                w_wld_map[(0, nxt)] = emit_wload(nxt, nxt % 2, [("T", p.count("T"))])
            elif th == 0 and nxt - NJG < 2:
                w_wld_map[(1, nxt - NJG)] = emit_wload(nxt - NJG, (nxt - NJG) % 2,
                                                       [("T", p.count("T"))])
            elif th == 1 and nxt < NJG:
                w_wld_map[(1, nxt)] = emit_wload(nxt, nxt % 2, [("T", p.count("T"))])
            gu_stops = nstops
        for k in range(CH):
            p.emit("scalar", lambda h, k=k: h.activation(dacc(k), dacc(k), ACT.Silu),
                   wait=[("V", p.count("V"))] if k == 0 else [], inc="S")
            if t_g0 is not None:
                p.emit("gpsimd", lambda h, k=k, th=th: h.dma_start(
                    out=t_g0.ap()[k * 128:(k + 1) * 128, th * 512:(th + 1) * 512], in_=dacc(k)),
                    wait=[("S", p.cnt["S"])], inc="G", inc_by=16)
            p.emit("sync", lambda h, k=k, th=th: h.dma_start(
                out=s_xs[:], in_=x_fm.ap()[k * 128:(k + 1) * 128, th * 512:(th + 1) * 512]),
                wait=[("V", p.count("V"))], inc="L", inc_by=16)
            p.emit("sync", lambda h, k=k, th=th: h.dma_start(
                out=s_ss[:], in_=s_spill.ap()[k * 128:(k + 1) * 128, th * 512:(th + 1) * 512]),
                wait=[("G", w_sspill), ("V", p.count("V"))], inc="L", inc_by=16)
            wLs = p.cnt["L"]
            p.emit("vector", lambda h, k=k: h.tensor_tensor(dacc(k), dacc(k), s_ss[:], op=A.mult),
                   wait=[("L", wLs), ("S", p.cnt["S"]), ("G", p.cnt["G"])])
            p.drain("vector")
            p.emit("vector", lambda h, k=k: h.tensor_tensor(dacc(k), dacc(k), s_xs[:], op=A.add),
                   inc="V")
            p.emit("gpsimd", lambda h, k=k, th=th: h.dma_start(
                out=x1_spill.ap()[k * 128:(k + 1) * 128, th * 512:(th + 1) * 512], in_=dacc(k)),
                wait=[("V", p.cnt["V"])], inc="G", inc_by=16)
            if t_x1 is not None:
                p.emit("gpsimd", lambda h, k=k, th=th: h.dma_start(
                    out=t_x1.ap()[k * 128:(k + 1) * 128, th * 512:(th + 1) * 512], in_=dacc(k)),
                    inc="G", inc_by=16)
    w_ffn = p.cnt["V"]

    free_back_to("s_wg0")
    _bar3 = [("V", p.count("V")), ("S", p.count("S")), ("T", p.count("T")),
             ("G", p.count("G")), ("L", p.count("L"))]
    for _e in ENGINES:
        p.wait_only(_e, _bar3)
    w_x1sp = p.cnt["G"]

    wbT = inp("wbT", [D, D])
    bb1 = inp("bb1", [128, CH])
    wrT = inp("wrT", [D, 8])
    aaT = inp("aaT", [D, 64])
    ball_i = inp("ball", [64, D])
    ropeC = inp("ropeC", [128, TOK])
    ropeS = inp("ropeS", [128, TOK])
    dmask = inp("dmask", [128, 4 * 512])
    g1_spill = nc.dram_tensor("g1_spill", [D, TOK], F32)
    a2a6_in = nc.dram_tensor("a2a6_in", [NC, 3, 2, 128, TOK], F32)
    a2a6_out = nc.dram_tensor("a2a6_out", [NC, 3, 2, 128, TOK], F32)
    a2a7_in = nc.dram_tensor("a2a7_in", [NC, 2, 128, TOK], F32)
    a2a7_out = nc.dram_tensor("a2a7_out", [NC, 2, 128, TOK], F32)
    rq_scr = nc.dram_tensor("rq_scr", [16, 128], F32)

    free_back_to("s_dqxh")
    _bar4 = [("V", p.count("V")), ("S", p.count("S")), ("T", p.count("T")),
             ("G", p.count("G")), ("L", p.count("L"))]
    for _e in ENGINES:
        p.wait_only(_e, _bar4)

    s_ifm = sb("s_ifm", [128, CH * TOK], BF16)

    def ifm(kk):
        return s_ifm[:, kk * TOK:(kk + 1) * TOK]

    s_t2k = sb("s_t2k", [128, D])
    s_w2k = sb("s_w2k", [128, D])
    s_a = sb("s_a", [128, D])
    s_b2 = sb("s_b2", [128, D])
    s_c2 = sb("s_c2", [128, D])
    s_cols = sb("s_cols", [128, 64])
    s_row = sb("s_row", [128, TOK])
    s_row2 = sb("s_row2", [128, TOK])
    s_row3 = sb("s_row3", [128, TOK])
    s_rowv = sb("s_rowv", [128, TOK])
    ps_tr2 = ps("ps_tr2", [128, 128])
    ps_m0 = ps("ps_m0", [128, 512])
    ps_m1 = ps("ps_m1", [128, 512])
    ps_sm = ps("ps_sm", [128, 64])
    ps_y = ps("ps_y", [128, 128])
    c_e5 = sb("c_e5", [128, 1])
    s_rsx = sb("s_rsx", [128, 8])
    s_pj = sb("s_pj", [128, TOK])
    s_pj2 = sb("s_pj2", [128, TOK])
    s_wqkv = sb("s_wqkv", [128, CH * 128], BF16)
    s_ropeC = sb("s_ropeC", [128, TOK])
    s_ropeS = sb("s_ropeS", [128, TOK])
    s_g1t = s_pj2
    s_x1t = s_rowv

    s_x1h = sb("s_x1h", [128, CH * TOK], BF16)
    s_x1l = sb("s_x1l", [128, CH * TOK], BF16)

    def x1h(kk):
        return s_x1h[:, kk * TOK:(kk + 1) * TOK]

    def x1l(kk):
        return s_x1l[:, kk * TOK:(kk + 1) * TOK]

    s_wbh = sb("s_wbh", [128, D], BF16)
    s_wbl = sb("s_wbl", [128, D], BF16)
    s_bb1 = sb("s_bb1", [128, CH])

    p.emit("vector", lambda h: h.memset(c_e5[:], 1e-5), inc="V")
    p.drain("vector")
    p.emit("sync", lambda h: h.dma_start(out=s_bb1[:], in_=bb1.ap()), inc="L", inc_by=16)
    w_bb1 = p.cnt["L"]


    def tern_tile(wi, kk, m):
        slot = m // 2
        return ag3_out.ap()[slot, wi * NW:(wi + 1) * NW].rearrange(
            "(a b) -> a b", b=256)[kk * 128:(kk + 1) * 128,
                                   (m % 2) * 128:(m % 2) * 128 + 128]


    def emit_rms_quant(tm_ap, rs_col, wait0):
        p.emit("scalar", lambda h: h.activation(s_a[:], tm_ap, ACT.Square),
               wait=list(wait0), inc="S")
        p.emit("vector", lambda h: h.tensor_reduce(s_cols[:, 8:9], s_a[:], axis=AX.X, op=A.add),
               wait=[("S", p.cnt["S"])], inc="V")
        p.drain("vector")
        p.emit("vector", lambda h: h.tensor_scalar(
            s_cols[:, 8:9], s_cols[:, 8:9], 1.0 / D, 1e-6, op0=A.mult, op1=A.add), inc="V")
        p.drain("vector")
        p.emit("scalar", lambda h: h.activation(s_cols[:, 9:10], s_cols[:, 8:9], ACT.Sqrt),
               wait=[("V", p.cnt["V"])], inc="S")
        p.emit("vector", lambda h: h.reciprocal(s_cols[:, 8:9], s_cols[:, 9:10]),
               wait=[("S", p.cnt["S"])], inc="V")
        p.drain("vector")
        p.emit("vector", lambda h: h.tensor_scalar(tm_ap, tm_ap, s_cols[:, 8:9], None, op0=A.mult),
               inc="V")
        p.drain("vector")
        p.emit("scalar", lambda h: h.activation(s_a[:], tm_ap, ACT.Abs),
               wait=[("V", p.cnt["V"])], inc="S")
        p.emit("vector", lambda h: h.tensor_reduce(s_cols[:, 9:10], s_a[:], axis=AX.X, op=A.max),
               wait=[("S", p.cnt["S"])], inc="V")
        p.drain("vector")
        p.emit("vector", lambda h: h.tensor_scalar(
            s_cols[:, 9:10], s_cols[:, 9:10], 1e-5, 1.0 / 127.0, op0=A.max, op1=A.mult), inc="V")
        p.drain("vector")
        p.emit("vector", lambda h, rs_col=rs_col: h.tensor_copy(rs_col, s_cols[:, 9:10]))
        p.emit("vector", lambda h: h.reciprocal(s_cols[:, 10:11], s_cols[:, 9:10]), inc="V")
        p.drain("vector")
        p.emit("vector", lambda h: h.tensor_scalar(tm_ap, tm_ap, s_cols[:, 10:11], None, op0=A.mult),
               inc="V")
        p.drain("vector")
        p.emit("scalar", lambda h: h.activation(s_a[:], tm_ap, ACT.Abs),
               wait=[("V", p.cnt["V"])], inc="S")
        p.emit("scalar", lambda h: h.activation(s_b2[:], tm_ap, ACT.Sign), inc="S")
        wS2 = p.cnt["S"]
        p.emit("vector", lambda h: h.tensor_copy(s_c2[:], s_a[:]), wait=[("S", wS2)])
        p.drain("vector")
        for w_ in (64.0, 32.0, 16.0, 8.0, 4.0, 2.0, 1.0):
            p.emit("vector", lambda h, w_=w_: h.tensor_scalar(tm_ap, s_a[:], w_, None, op0=A.is_ge))
            p.drain("vector")
            p.emit("vector", lambda h, w_=w_: h.scalar_tensor_tensor(
                s_a[:], tm_ap, -w_, s_a[:], op0=A.mult, op1=A.add))
            p.drain("vector")
        p.emit("vector", lambda h: h.tensor_scalar(tm_ap, s_a[:], 0.5, None, op0=A.is_ge))
        p.drain("vector")
        p.emit("vector", lambda h: h.tensor_tensor(s_c2[:], s_c2[:], s_a[:], op=A.subtract))
        p.drain("vector")
        p.emit("vector", lambda h: h.tensor_tensor(s_c2[:], s_c2[:], tm_ap, op=A.add))
        p.drain("vector")
        p.emit("vector", lambda h: h.tensor_tensor(tm_ap, s_c2[:], s_b2[:], op=A.mult), inc="V")
        p.drain("vector")
        return p.cnt["V"]

    for tc in range(TOK // 128):
        p.emit("sync", lambda h, tc=tc: h.dma_start(
            out=s_w2k[:], in_=x1_spill.ap().rearrange("(c p) t -> p c t", p=128)[
                :, :, tc * 128:(tc + 1) * 128]),
            wait=[("G", w_x1sp), ("V", p.count("V")), ("T", p.count("T"))], inc="L", inc_by=16)
        wLx1 = p.cnt["L"]
        for kk in range(CH):
            p.emit("tensor", lambda h, kk=kk: h.transpose(
                ps_tr2[:], s_w2k[:, kk * 128:(kk + 1) * 128], c_ident[:]),
                wait=([("L", wLx1)] if kk == 0 else []) + [("V", p.count("V"))], inc="T")
            p.emit("vector", lambda h, kk=kk: h.tensor_copy(s_t2k[:, kk * 128:(kk + 1) * 128], ps_tr2[:]),
                   wait=[("T", p.cnt["T"])], inc="V")
        p.drain("vector")
        p.emit("vector", lambda h: h.tensor_reduce(s_cols[:, 0:1], s_t2k[:], axis=AX.X, op=A.add),
               wait=[("V", p.cnt["V"])], inc="V")
        p.drain("vector")
        p.emit("vector", lambda h: h.tensor_scalar(s_cols[:, 0:1], s_cols[:, 0:1], 1.0 / D, None, op0=A.mult),
               inc="V")
        p.drain("vector")
        p.emit("scalar", lambda h: h.activation(s_a[:], s_t2k[:], ACT.Square),
               wait=[("V", p.cnt["V"])], inc="S")
        p.emit("vector", lambda h: h.tensor_reduce(s_cols[:, 1:2], s_a[:], axis=AX.X, op=A.add),
               wait=[("S", p.cnt["S"])], inc="V")
        p.drain("vector")
        p.emit("vector", lambda h: h.tensor_tensor(s_cols[:, 2:3], s_cols[:, 0:1], s_cols[:, 0:1], op=A.mult))
        p.drain("vector")
        p.emit("vector", lambda h: h.scalar_tensor_tensor(
            s_cols[:, 1:2], s_cols[:, 1:2], 1.0 / D, s_cols[:, 2:3], op0=A.mult, op1=A.subtract),
            inc="V")
        p.drain("vector")
        p.emit("scalar", lambda h: h.activation(s_cols[:, 1:2], s_cols[:, 1:2], ACT.Sqrt,
                                                bias=c_e5[:, 0:1], scale=1.0),
               wait=[("V", p.cnt["V"])], inc="S")
        p.emit("vector", lambda h: h.reciprocal(s_cols[:, 1:2], s_cols[:, 1:2]),
               wait=[("S", p.cnt["S"])], inc="V")
        p.drain("vector")
        p.emit("vector", lambda h: h.tensor_scalar(
            s_t2k[:], s_t2k[:], s_cols[:, 0:1], None, op0=A.subtract))
        p.drain("vector")
        p.emit("vector", lambda h: h.tensor_scalar(
            s_t2k[:], s_t2k[:], s_cols[:, 1:2], None, op0=A.mult), inc="V")
        p.drain("vector")
        for kk in range(CH):
            p.emit("tensor", lambda h, kk=kk: h.transpose(
                ps_tr2[:], s_t2k[:, kk * 128:(kk + 1) * 128], c_ident[:]),
                wait=[("V", p.count("V"))], inc="T")
            p.emit("vector", lambda h, kk=kk, tc=tc: h.tensor_copy(
                x1h(kk)[:, tc * 128:(tc + 1) * 128], ps_tr2[:]),
                wait=[("T", p.cnt["T"])])
            p.drain("vector")
            p.emit("vector", lambda h, kk=kk, tc=tc: h.tensor_tensor(
                x1l(kk)[:, tc * 128:(tc + 1) * 128], ps_tr2[:],
                x1h(kk)[:, tc * 128:(tc + 1) * 128], op=A.subtract), inc="V")
            p.drain("vector")
        emit_rms_quant(s_t2k[:], s_rsx[:, tc:tc + 1], [("V", p.cnt["V"])])
        for kk in range(CH):
            p.emit("tensor", lambda h, kk=kk: h.transpose(
                ps_tr2[:], s_t2k[:, kk * 128:(kk + 1) * 128], c_ident[:]),
                wait=[("V", p.count("V"))], inc="T")
            p.emit("vector", lambda h, kk=kk, tc=tc: h.tensor_copy(
                ifm(kk)[:, tc * 128:(tc + 1) * 128], ps_tr2[:]),
                wait=[("T", p.cnt["T"])], inc="V")
        p.drain("vector")
    w_ix = p.cnt["V"]

    s_wr8h = sb("s_wr8h", [128, CH * 8], BF16)
    s_wr8l = sb("s_wr8l", [128, CH * 8], BF16)
    s_aah = sb("s_aah", [128, CH * 64], BF16)
    s_aal = sb("s_aal", [128, CH * 64], BF16)
    for kk in range(CH):
        p.emit("sync", lambda h, kk=kk: h.dma_start(
            out=s_w2k[:, 0:8], in_=wrT.ap()[kk * 128:(kk + 1) * 128, :]),
            wait=[("V", p.count("V")), ("T", p.count("T"))], inc="L", inc_by=16)
        p.emit("sync", lambda h, kk=kk: h.dma_start(
            out=s_w2k[:, 64:128], in_=aaT.ap()[kk * 128:(kk + 1) * 128, :]),
            inc="L", inc_by=16)
        wLr = p.cnt["L"]
        p.emit("vector", lambda h, kk=kk: h.tensor_copy(s_wr8h[:, kk * 8:(kk + 1) * 8], s_w2k[:, 0:8]),
               wait=[("L", wLr)])
        p.emit("vector", lambda h, kk=kk: h.tensor_copy(s_aah[:, kk * 64:(kk + 1) * 64], s_w2k[:, 64:128]))
        p.drain("vector")
        p.emit("vector", lambda h, kk=kk: h.tensor_tensor(
            s_wr8l[:, kk * 8:(kk + 1) * 8], s_w2k[:, 0:8], s_wr8h[:, kk * 8:(kk + 1) * 8], op=A.subtract))
        p.emit("vector", lambda h, kk=kk: h.tensor_tensor(
            s_aal[:, kk * 64:(kk + 1) * 64], s_w2k[:, 64:128], s_aah[:, kk * 64:(kk + 1) * 64], op=A.subtract),
            inc="V")
        p.drain("vector")
    w_wra = p.cnt["V"]

    s_lg = sb("s_lg", [128, 8 * 8])
    s_ha = sb("s_ha", [128, 8 * 64])
    s_hsch = sb("s_hsch", [128, TOK], BF16)
    s_hscl = sb("s_hscl", [128, TOK], BF16)
    for tc in range(8):
        first = True
        for pi, (xs_, ws_) in enumerate(((0, 0), (0, 1), (1, 0))):
            for kk in range(CH):
                st = (pi == 0 and kk == 0)
                sp = (pi == 2 and kk == CH - 1)
                xop = (s_x1h if xs_ == 0 else s_x1l)[:, kk * TOK + tc * 128: kk * TOK + tc * 128 + 128]
                wop = (s_wr8h if ws_ == 0 else s_wr8l)[:, kk * 8:(kk + 1) * 8]
                aop = (s_aah if ws_ == 0 else s_aal)[:, kk * 64:(kk + 1) * 64]
                wt = [("V", p.count("V"))] if first else []
                first = False
                p.emit("tensor", lambda h, xop=xop, wop=wop, st=st, sp=sp: h.matmul(
                    ps_sm[:, 0:8], xop, wop, start=st, stop=sp),
                    wait=wt, inc="T" if sp else None)
                p.emit("tensor", lambda h, xop=xop, aop=aop, st=st, sp=sp: h.matmul(
                    ps_m0[:, 0:64], xop, aop, start=st, stop=sp),
                    inc="T" if sp else None)
        wTr = p.cnt["T"]
        p.emit("vector", lambda h, tc=tc: h.tensor_copy(s_lg[:, tc * 8:(tc + 1) * 8], ps_sm[:, 0:8]),
               wait=[("T", wTr)])
        p.emit("vector", lambda h, tc=tc: h.tensor_copy(s_ha[:, tc * 64:(tc + 1) * 64], ps_m0[:, 0:64]),
               inc="V")
        p.drain("vector")
    w_lg = p.cnt["V"]

    for tc in range(8):
        lg = s_lg[:, tc * 8:(tc + 1) * 8]
        p.emit("vector", lambda h, lg=lg: h.tensor_reduce(s_cols[:, 0:1], lg, axis=AX.X, op=A.max),
               wait=[("V", w_lg)] if tc == 0 else [])
        p.drain("vector")
        p.emit("vector", lambda h, lg=lg: h.tensor_scalar(s_a[:, 0:8], lg, s_cols[:, 0:1], None, op0=A.is_ge))
        p.drain("vector")
        p.emit("vector", lambda h: h.tensor_reduce(s_cols[:, 1:2], s_a[:, 0:8], axis=AX.X, op=A.add))
        p.drain("vector")
        p.emit("vector", lambda h: h.tensor_scalar(s_a[:, 0:8], s_a[:, 0:8], 1e30, None, op0=A.mult))
        p.drain("vector")
        p.emit("vector", lambda h, lg=lg: h.tensor_tensor(s_a[:, 8:16], lg, s_a[:, 0:8], op=A.subtract))
        p.drain("vector")
        p.emit("vector", lambda h: h.tensor_reduce(s_cols[:, 2:3], s_a[:, 8:16], axis=AX.X, op=A.max))
        p.drain("vector")
        p.emit("vector", lambda h: h.tensor_scalar(s_cols[:, 3:4], s_cols[:, 1:2], 2.0, None, op0=A.is_ge))
        p.drain("vector")
        p.emit("vector", lambda h: h.tensor_tensor(s_cols[:, 4:5], s_cols[:, 0:1], s_cols[:, 2:3], op=A.subtract))
        p.drain("vector")
        p.emit("vector", lambda h: h.tensor_tensor(s_cols[:, 4:5], s_cols[:, 4:5], s_cols[:, 3:4], op=A.mult))
        p.drain("vector")
        p.emit("vector", lambda h: h.tensor_tensor(s_cols[:, 2:3], s_cols[:, 2:3], s_cols[:, 4:5], op=A.add),
               inc="V")
        p.drain("vector")
        p.emit("vector", lambda h: h.tensor_tensor(s_cols[:, 5:6], s_cols[:, 2:3], s_cols[:, 0:1], op=A.subtract),
               inc="V")
        p.drain("vector")
        p.emit("scalar", lambda h: h.activation(s_cols[:, 5:6], s_cols[:, 5:6], ACT.Exp),
               wait=[("V", p.cnt["V"])], inc="S")
        p.emit("vector", lambda h: h.tensor_scalar(s_cols[:, 5:6], s_cols[:, 5:6], 1.0, None, op0=A.add),
               wait=[("S", p.cnt["S"])], inc="V")
        p.drain("vector")
        p.emit("vector", lambda h: h.reciprocal(s_cols[:, 6:7], s_cols[:, 5:6]), inc="V")
        p.drain("vector")
        p.emit("vector", lambda h: h.tensor_scalar(s_cols[:, 6:7], s_cols[:, 6:7], 2.0, None, op0=A.mult))
        p.drain("vector")
        p.emit("vector", lambda h, lg=lg: h.tensor_scalar(s_a[:, 0:8], lg, s_cols[:, 2:3], None, op0=A.is_ge))
        p.drain("vector")
        p.emit("vector", lambda h: h.tensor_scalar(s_cols[:, 7:8], s_cols[:, 0:1], -1.0, None, op0=A.mult),
               inc="V")
        p.drain("vector")
        p.emit("scalar", lambda h, lg=lg: h.activation(s_a[:, 8:16], lg, ACT.Exp,
                                                       bias=s_cols[:, 7:8], scale=1.0),
               wait=[("V", p.cnt["V"])], inc="S")
        p.emit("vector", lambda h: h.tensor_tensor(s_a[:, 0:8], s_a[:, 0:8], s_a[:, 8:16], op=A.mult),
               wait=[("S", p.cnt["S"])])
        p.drain("vector")
        p.emit("vector", lambda h: h.tensor_scalar(s_a[:, 0:8], s_a[:, 0:8], s_cols[:, 6:7], None, op0=A.mult),
               inc="V")
        p.drain("vector")
        p.emit("vector", lambda h, tc=tc: h.tensor_tensor(
            s_a[:, 64:128].rearrange("p (r e) -> p r e", e=8),
            s_ha[:, tc * 64:(tc + 1) * 64].rearrange("p (r e) -> p r e", e=8),
            s_a[:, 0:8].unsqueeze(1).broadcast_to([128, 8, 8]), op=A.mult), inc="V")
        p.drain("vector")
        p.emit("tensor", lambda h: h.transpose(ps_tr2[0:64, :], s_a[:, 64:128], c_ident[:]),
               wait=[("V", p.cnt["V"])], inc="T")
        p.emit("vector", lambda h, tc=tc: h.tensor_copy(
            s_hsch[0:64, tc * 128:(tc + 1) * 128], ps_tr2[0:64, :]),
            wait=[("T", p.cnt["T"])])
        p.drain("vector")
        p.emit("vector", lambda h, tc=tc: h.tensor_tensor(
            s_hscl[0:64, tc * 128:(tc + 1) * 128], ps_tr2[0:64, :],
            s_hsch[0:64, tc * 128:(tc + 1) * 128], op=A.subtract), inc="V")
        p.drain("vector")
    w_hsc = p.cnt["V"]

    s_blh = sb("s_blh", [128, D], BF16)
    s_bll = sb("s_bll", [128, D], BF16)
    p.emit("sync", lambda h: h.dma_start(out=s_w2k[0:64, 0:D], in_=ball_i.ap()),
           wait=[("V", p.count("V")), ("T", p.count("T"))], inc="L", inc_by=16)
    p.emit("vector", lambda h: h.tensor_copy(s_blh[0:64, :], s_w2k[0:64, 0:D]),
           wait=[("L", p.cnt["L"])])
    p.drain("vector")
    p.emit("vector", lambda h: h.tensor_tensor(s_bll[0:64, :], s_w2k[0:64, 0:D], s_blh[0:64, :], op=A.subtract),
           inc="V")
    p.drain("vector")
    w_bl = p.cnt["V"]

    t_g1 = tap("g1", [D, TOK])
    for m in range(CH):
        for kk in range(CH):
            p.emit("sync", lambda h, kk=kk, m=m: h.dma_start(
                out=s_w2k[:, kk * 128:(kk + 1) * 128],
                in_=wbT.ap()[kk * 128:(kk + 1) * 128, m * 128:(m + 1) * 128]),
                wait=([("V", p.count("V")), ("T", p.count("T"))] if kk == 0 else []),
                inc="L", inc_by=16)
        wLb = p.cnt["L"]
        p.emit("vector", lambda h: h.tensor_copy(s_wbh[:], s_w2k[:]), wait=[("L", wLb)])
        p.drain("vector")
        p.emit("vector", lambda h: h.tensor_tensor(s_wbl[:], s_w2k[:], s_wbh[:], op=A.subtract),
               inc="V")
        p.drain("vector")
        wsp = p.cnt["V"]
        for th in range(2):
            pd = ps_m0 if th == 0 else ps_m1
            sl = slice(th * 512, (th + 1) * 512)
            first = True
            for pi, (xs_, ws_) in enumerate(((0, 0), (0, 1), (1, 0))):
                for kk in range(CH):
                    st = (pi == 0 and kk == 0)
                    xop = (s_wbh if ws_ == 0 else s_wbl)[:, kk * 128:(kk + 1) * 128]
                    rop = (s_x1h if xs_ == 0 else s_x1l)[:, kk * TOK + th * 512: kk * TOK + th * 512 + 512]
                    wt = [("V", wsp), ("V", p.count("V"))] if first else []
                    first = False
                    p.emit("tensor", lambda h, xop=xop, rop=rop, st=st, pd=pd: h.matmul(
                        pd[:], xop, rop, start=st, stop=False),
                        wait=wt)
            for pi, (hs_, ws_) in enumerate(((0, 0), (0, 1), (1, 0))):
                sp = (pi == 2)
                bop = (s_blh if ws_ == 0 else s_bll)[0:64, m * 128:(m + 1) * 128]
                hop = (s_hsch if hs_ == 0 else s_hscl)[0:64, sl]
                p.emit("tensor", lambda h, bop=bop, hop=hop, sp=sp, pd=pd: h.matmul(
                    pd[:], bop, hop, start=False, stop=sp),
                    wait=[("V", w_hsc), ("V", w_bl)] if pi == 0 else [], inc="T" if sp else None)
            wTb = p.cnt["T"]
            p.emit("scalar", lambda h, m=m, sl=sl, pd=pd: h.activation(
                s_c2[:, sl], pd[:], ACT.Silu, bias=s_bb1[:, m:m + 1], scale=1.0),
                wait=[("T", wTb), ("L", w_bb1), ("G", p.count("G"))], inc="S")
        wSg = p.cnt["S"]
        p.emit("gpsimd", lambda h, m=m: h.dma_start(
            out=g1_spill.ap()[m * 128:(m + 1) * 128, :], in_=s_c2[:, 0:TOK]),
            wait=[("S", wSg)], inc="G", inc_by=16)
        if t_g1 is not None:
            p.emit("gpsimd", lambda h, m=m: h.dma_start(
                out=t_g1.ap()[m * 128:(m + 1) * 128, :], in_=s_c2[:, 0:TOK]),
                inc="G", inc_by=16)
        p.wait_only("scalar", [("G", p.cnt["G"])])
    w_g1 = p.cnt["G"]
    free_back_to("s_x1h")
    _bar5 = [("V", p.count("V")), ("S", p.count("S")), ("T", p.count("T")),
             ("G", p.count("G")), ("L", p.count("L"))]
    for _e in ENGINES:
        p.wait_only(_e, _bar5)

    p.emit("gpsimd", lambda h: h.dma_start(
        out=rq_scr.ap()[0:8, :].transpose([1, 0]), in_=s_rsx[:],
        allow_slow_non_contiguous=True),
        wait=[("V", w_ix)], inc="G", inc_by=16)
    p.wait_only("gpsimd", [("G", p.cnt["G"])])
    for c_ in range(8):
        p.emit("gpsimd", lambda h, c_=c_: h.dma_start(
            out=s_row[:, c_ * 128:(c_ + 1) * 128],
            in_=rq_scr.ap()[c_:c_ + 1, :].broadcast_to([128, 128])), inc="G", inc_by=16)
    wGrs = p.cnt["G"]
    p.emit("vector", lambda h: h.tensor_scalar(
        s_row2[:], s_row[:], s_swm[:, 0:1], 1.0 / math.sqrt(128.0), op0=A.mult, op1=A.mult),
        wait=[("G", wGrs)])
    p.drain("vector")
    p.emit("vector", lambda h: h.tensor_scalar(s_row3[:], s_row[:], s_swm[:, 1:2], None, op0=A.mult),
           inc="V")
    p.drain("vector")
    p.emit("vector", lambda h: h.tensor_scalar(s_rowv[:], s_row[:], s_swm[:, 2:3], None, op0=A.mult),
           inc="V")
    p.drain("vector")
    w_rows = p.cnt["V"]

    p.emit("sync", lambda h: h.dma_start(out=s_ropeC[:], in_=ropeC.ap()), inc="L", inc_by=16)
    p.emit("sync", lambda h: h.dma_start(out=s_ropeS[:], in_=ropeS.ap()), inc="L", inc_by=16)
    w_ropeL = p.cnt["L"]
    p.wait_only("vector", [("L", w_ropeL)])
    for wi, rowt in ((0, s_row2), (1, s_row3), (2, s_rowv)):
        for m in range(CH):
            for kk in range(CH):
                p.emit("sync", lambda h, wi=wi, kk=kk, m=m: h.dma_start(
                    out=s_wqkv[:, kk * 128:(kk + 1) * 128], in_=tern_tile(wi, kk, m)),
                    wait=([("C", wC3), ("T", p.count("T"))] if kk == 0 else []),
                    inc="L", inc_by=16)
            wLq = p.cnt["L"]
            for th in range(2):
                pd = ps_m0 if th == 0 else ps_m1
                for kk in range(CH):
                    p.emit("tensor", lambda h, kk=kk, th=th, pd=pd: h.matmul(
                        pd[:], s_wqkv[:, kk * 128:(kk + 1) * 128],
                        s_ifm[:, kk * TOK + th * 512: kk * TOK + th * 512 + 512],
                        start=(kk == 0), stop=(kk == CH - 1)),
                        wait=([("L", wLq), ("V", p.count("V"))] if kk == 0 else []),
                        inc="T" if kk == CH - 1 else None)
            wTq = p.cnt["T"]
            p.emit("vector", lambda h, rowt=rowt: h.tensor_tensor(
                s_pj[:, 0:512], ps_m0[:], rowt[:, 0:512], op=A.mult),
                wait=[("T", wTq), ("V", w_rows)])
            p.emit("vector", lambda h, rowt=rowt: h.tensor_tensor(
                s_pj[:, 512:1024], ps_m1[:], rowt[:, 512:1024], op=A.mult), inc="V")
            p.drain("vector")
            if wi < 2:
                p.emit("vector", lambda h: h.stream_shuffle(
                    s_pj2[:], s_pj[:], [i ^ 1 for i in range(32)]), inc="V")
                p.drain("vector")
                p.emit("vector", lambda h: h.tensor_tensor(s_pj[:], s_pj[:], s_ropeC[:], op=A.mult))
                p.drain("vector")
                p.emit("vector", lambda h: h.tensor_tensor(s_pj2[:], s_pj2[:], s_ropeS[:], op=A.mult))
                p.drain("vector")
                p.emit("vector", lambda h: h.tensor_tensor(s_pj[:], s_pj[:], s_pj2[:], op=A.add),
                       inc="V")
                p.drain("vector")
            p.emit("gpsimd", lambda h, wi=wi, m=m: h.dma_start(
                out=a2a6_in.ap()[m // 2, wi, m % 2, :, :], in_=s_pj[:]),
                wait=[("V", p.cnt["V"])], inc="G", inc_by=16)
            p.wait_only("vector", [("G", p.cnt["G"])])
    p.wait_only("gpsimd", [("G", p.cnt["G"])])
    t_qr = tap("qrope", [D, TOK])
    t_kr = tap("krope", [D, TOK])
    t_vp = tap("vpre", [D, TOK])
    for tt, wi in ((t_qr, 0), (t_kr, 1), (t_vp, 2)):
        if tt is not None:
            for m in range(CH):
                p.emit("gpsimd", lambda h, tt=tt, wi=wi, m=m: h.dma_start(
                    out=tt.ap()[m * 128:(m + 1) * 128, :],
                    in_=a2a6_in.ap()[m // 2, wi, m % 2, :, :]), inc="G", inc_by=16)
    p.wait_only("gpsimd", [("G", p.cnt["G"])])
    p.emit("gpsimd", lambda h: h.collective_compute(
        "AllToAll", A.bypass, replica_groups=[list(range(NC))],
        ins=[a2a6_in.ap().opt()], outs=[a2a6_out.ap().opt()]), inc="C")
    wC6 = p.cnt["C"]

    s_qh = sb("s_qh", [128, 2048], BF16)
    s_ql = sb("s_ql", [128, 2048], BF16)
    s_kh = sb("s_kh", [128, 2048], BF16)
    s_kl = sb("s_kl", [128, 2048], BF16)
    s_vth = sb("s_vth", [128, 2048], BF16)
    s_vtl = sb("s_vtl", [128, 2048], BF16)
    s_P = sb("s_P", [128, 2048])
    s_PTh = sb("s_PTh", [128, 128], BF16)
    s_PTl = sb("s_PTl", [128, 128], BF16)
    s_y = sb("s_y", [128, 2048])
    s_dm = sb("s_dm", [128, 4 * 512])
    s_rq16 = sb("s_rq16", [128, 16])
    s_rowq = sb("s_rowq", [128, 2048])
    p.emit("sync", lambda h: h.dma_start(out=s_dm[:], in_=dmask.ap()), inc="L", inc_by=16)
    w_dm = p.cnt["L"]

    for jb in range(NC):
        bi, hs = jb // 2, jb % 2
        for o in range(2):
            p.emit("sync", lambda h, bi=bi, o=o, hs=hs: h.dma_start(
                out=s_pj[:], in_=a2a6_out.ap()[2 * bi + o, 0, hs, :, :]),
                wait=[("C", wC6), ("V", p.count("V")), ("T", p.count("T"))], inc="L", inc_by=16)
            p.emit("sync", lambda h, bi=bi, o=o, hs=hs: h.dma_start(
                out=s_pj2[:], in_=a2a6_out.ap()[2 * bi + o, 1, hs, :, :]),
                inc="L", inc_by=16)
            wLj = p.cnt["L"]
            sl = slice(o * TOK, (o + 1) * TOK)
            p.emit("vector", lambda h, sl=sl: h.tensor_copy(s_qh[:, sl], s_pj[:]),
                   wait=[("L", wLj)])
            p.emit("vector", lambda h, sl=sl: h.tensor_copy(s_kh[:, sl], s_pj2[:]))
            p.drain("vector")
            p.emit("vector", lambda h, sl=sl: h.tensor_tensor(s_ql[:, sl], s_pj[:], s_qh[:, sl], op=A.subtract))
            p.emit("vector", lambda h, sl=sl: h.tensor_tensor(s_kl[:, sl], s_pj2[:], s_kh[:, sl], op=A.subtract),
                   inc="V")
            p.drain("vector")
            p.emit("sync", lambda h, bi=bi, o=o, hs=hs: h.dma_start(
                out=s_pj[:], in_=a2a6_out.ap()[2 * bi + o, 2, hs, :, :]),
                wait=[("V", p.count("V"))], inc="L", inc_by=16)
            wLv = p.cnt["L"]
            for kt in range(8):
                p.emit("tensor", lambda h, kt=kt: h.transpose(
                    ps_tr2[:], s_pj[:, kt * 128:(kt + 1) * 128], c_ident[:]),
                    wait=([("L", wLv)] if kt == 0 else []) + [("V", p.count("V"))], inc="T")
                col = (o * 8 + kt) * 128
                p.emit("vector", lambda h, col=col: h.tensor_copy(
                    s_vth[:, col:col + 128], ps_tr2[:]),
                    wait=[("T", p.cnt["T"])])
                p.drain("vector")
                p.emit("vector", lambda h, col=col: h.tensor_tensor(
                    s_vtl[:, col:col + 128], ps_tr2[:], s_vth[:, col:col + 128], op=A.subtract),
                    inc="V")
                p.drain("vector")
        w_qkv = p.cnt["V"]
        for qt in range(16):
            ng = qt // 4 + 1
            al = qt % 4
            for g in range(ng):
                ksl = slice(g * 512, (g + 1) * 512)
                pd = ps_m0 if g % 2 == 0 else ps_m1
                for pi, (qs_, ks_) in enumerate(((0, 0), (0, 1), (1, 0))):
                    qop = (s_qh if qs_ == 0 else s_ql)[:, qt * 128:(qt + 1) * 128]
                    kop = (s_kh if ks_ == 0 else s_kl)[:, ksl]
                    p.emit("tensor", lambda h, qop=qop, kop=kop, pi=pi, pd=pd: h.matmul(
                        pd[:], qop, kop, start=(pi == 0), stop=(pi == 2)),
                        wait=([("V", p.count("V"))] if (pi == 0) else []),
                        inc="T" if pi == 2 else None)
                wTs = p.cnt["T"]
                if g == ng - 1:
                    p.emit("vector", lambda h, ksl=ksl, al=al, pd=pd: h.tensor_tensor(
                        s_P[:, ksl], pd[:], s_dm[:, al * 512:(al + 1) * 512], op=A.add),
                        wait=[("T", wTs), ("L", w_dm)], inc="V")
                else:
                    p.emit("vector", lambda h, ksl=ksl, pd=pd: h.tensor_copy(s_P[:, ksl], pd[:]),
                           wait=[("T", wTs)], inc="V")
            p.drain("vector")
            wid = ng * 512
            p.emit("vector", lambda h, wid=wid: h.tensor_reduce(
                s_cols[:, 0:1], s_P[:, 0:wid], axis=AX.X, op=A.max), inc="V")
            p.drain("vector")
            p.emit("vector", lambda h: h.tensor_scalar(s_cols[:, 1:2], s_cols[:, 0:1], -1.0, None, op0=A.mult),
                   inc="V")
            p.drain("vector")
            p.emit("scalar", lambda h, wid=wid: h.activation(
                s_P[:, 0:wid], s_P[:, 0:wid], ACT.Exp, bias=s_cols[:, 1:2], scale=1.0,
                accum_out=s_cols[:, 2:3]),
                wait=[("V", p.cnt["V"])], inc="S")
            p.emit("vector", lambda h: h.reciprocal(s_cols[:, 3:4], s_cols[:, 2:3]),
                   wait=[("S", p.cnt["S"])])
            p.drain("vector")
            p.emit("vector", lambda h, qt=qt: h.tensor_copy(s_rq16[:, qt:qt + 1], s_cols[:, 3:4]),
                   inc="V")
            p.drain("vector")
            nkc = ng * 4
            for kc in range(nkc):
                p.emit("tensor", lambda h, kc=kc: h.transpose(
                    ps_tr2[:], s_P[:, kc * 128:(kc + 1) * 128], c_ident[:]),
                    wait=[("V", p.count("V")), ("S", p.count("S"))], inc="T")
                p.emit("vector", lambda h: h.tensor_copy(s_PTh[:], ps_tr2[:]),
                       wait=[("T", p.cnt["T"])])
                p.drain("vector")
                p.emit("vector", lambda h: h.tensor_tensor(s_PTl[:], ps_tr2[:], s_PTh[:], op=A.subtract),
                       inc="V")
                p.drain("vector")
                wVpt = p.cnt["V"]
                for pi, (vs_, ps_2) in enumerate(((0, 0), (0, 1), (1, 0))):
                    vop = (s_vth if vs_ == 0 else s_vtl)[:, kc * 128:(kc + 1) * 128]
                    pop = s_PTh if ps_2 == 0 else s_PTl
                    p.emit("tensor", lambda h, vop=vop, pop=pop, st=(kc == 0 and pi == 0),
                           sp=(kc == nkc - 1 and pi == 2): h.matmul(
                        ps_y[:], vop, pop[:], start=st, stop=sp),
                        wait=[("V", wVpt)] if pi == 0 else [],
                        inc="T" if pi == 2 else None)
            wTy = p.cnt["T"]
            p.emit("vector", lambda h, qt=qt: h.tensor_copy(s_y[:, qt * 128:(qt + 1) * 128],
                                                            ps_y[:]),
                   wait=[("T", wTy)], inc="V")
            p.drain("vector")
        p.emit("gpsimd", lambda h, jb=jb: h.dma_start(
            out=rq_scr.ap().transpose([1, 0]), in_=s_rq16[:],
            allow_slow_non_contiguous=True),
            wait=[("V", p.cnt["V"])], inc="G", inc_by=16)
        p.wait_only("gpsimd", [("G", p.cnt["G"])])
        for c_ in range(16):
            p.emit("gpsimd", lambda h, c_=c_: h.dma_start(
                out=s_rowq[:, c_ * 128:(c_ + 1) * 128],
                in_=rq_scr.ap()[c_:c_ + 1, :].broadcast_to([128, 128])), inc="G", inc_by=16)
        wGq = p.cnt["G"]
        p.emit("vector", lambda h: h.tensor_tensor(s_y[:], s_y[:], s_rowq[:], op=A.mult),
               wait=[("G", wGq)], inc="V")
        p.drain("vector")
        for o in range(2):
            p.emit("gpsimd", lambda h, bi=bi, o=o, hs=hs: h.dma_start(
                out=a2a7_in.ap()[2 * bi + o, hs, :, :], in_=s_y[:, o * TOK:(o + 1) * TOK]),
                wait=[("V", p.cnt["V"])], inc="G", inc_by=16)
        p.wait_only("vector", [("G", p.cnt["G"])])
        p.wait_only("tensor", [("G", p.cnt["G"])])
    p.wait_only("gpsimd", [("G", p.cnt["G"])])
    p.emit("gpsimd", lambda h: h.collective_compute(
        "AllToAll", A.bypass, replica_groups=[list(range(NC))],
        ins=[a2a7_in.ap().opt()], outs=[a2a7_out.ap().opt()]), inc="C")
    wC7 = p.cnt["C"]

    t_ya = tap("yatt", [D, TOK])
    if t_ya is not None:
        for kk in range(CH):
            p.emit("gpsimd", lambda h, kk=kk: h.dma_start(
                out=t_ya.ap()[kk * 128:(kk + 1) * 128, :],
                in_=a2a7_out.ap()[kk // 2, kk % 2, :, :]),
                wait=[("C", wC7)] if kk == 0 else [], inc="G", inc_by=16)

    s_rsy = s_rsx
    for tc in range(TOK // 128):
        for kk in range(CH):
            p.emit("sync", lambda h, kk=kk, tc=tc: h.dma_start(
                out=s_w2k[:, kk * 128:(kk + 1) * 128],
                in_=a2a7_out.ap()[kk // 2, kk % 2, :, tc * 128:(tc + 1) * 128]),
                wait=([("C", wC7), ("V", p.count("V")), ("T", p.count("T"))] if kk == 0 else []),
                inc="L", inc_by=16)
        wLy = p.cnt["L"]
        for kk in range(CH):
            p.emit("tensor", lambda h, kk=kk: h.transpose(
                ps_tr2[:], s_w2k[:, kk * 128:(kk + 1) * 128], c_ident[:]),
                wait=([("L", wLy)] if kk == 0 else []) + [("V", p.count("V"))], inc="T")
            p.emit("vector", lambda h, kk=kk: h.tensor_copy(s_t2k[:, kk * 128:(kk + 1) * 128], ps_tr2[:]),
                   wait=[("T", p.cnt["T"])], inc="V")
        p.drain("vector")
        emit_rms_quant(s_t2k[:], s_rsy[:, tc:tc + 1], [("V", p.cnt["V"])])
        for kk in range(CH):
            p.emit("tensor", lambda h, kk=kk: h.transpose(
                ps_tr2[:], s_t2k[:, kk * 128:(kk + 1) * 128], c_ident[:]),
                wait=[("V", p.count("V"))], inc="T")
            p.emit("vector", lambda h, kk=kk, tc=tc: h.tensor_copy(
                ifm(kk)[:, tc * 128:(tc + 1) * 128], ps_tr2[:]),
                wait=[("T", p.cnt["T"])], inc="V")
        p.drain("vector")
    w_iy = p.cnt["V"]

    p.emit("gpsimd", lambda h: h.dma_start(
        out=rq_scr.ap()[0:8, :].transpose([1, 0]), in_=s_rsy[:],
        allow_slow_non_contiguous=True),
        wait=[("V", w_iy)], inc="G", inc_by=16)
    p.wait_only("gpsimd", [("G", p.cnt["G"])])
    for c_ in range(8):
        p.emit("gpsimd", lambda h, c_=c_: h.dma_start(
            out=s_row[:, c_ * 128:(c_ + 1) * 128],
            in_=rq_scr.ap()[c_:c_ + 1, :].broadcast_to([128, 128])), inc="G", inc_by=16)
    wGro = p.cnt["G"]
    p.emit("vector", lambda h: h.tensor_scalar(s_row2[:], s_row[:], s_swm[:, 3:4], None, op0=A.mult),
           wait=[("G", wGro)], inc="V")
    p.drain("vector")
    w_rowo = p.cnt["V"]

    t_s1 = tap("s1", [D, TOK])
    t_x2 = tap("x2", [D, TOK])
    for m in range(CH):
        for kk in range(CH):
            p.emit("sync", lambda h, kk=kk, m=m: h.dma_start(
                out=s_wqkv[:, kk * 128:(kk + 1) * 128], in_=tern_tile(3, kk, m)),
                wait=([("T", p.count("T")), ("V", p.count("V"))] if kk == 0 else []),
                inc="L", inc_by=16)
        wLo = p.cnt["L"]
        for th in range(2):
            pd = ps_m0 if th == 0 else ps_m1
            for kk in range(CH):
                p.emit("tensor", lambda h, kk=kk, th=th, pd=pd: h.matmul(
                    pd[:], s_wqkv[:, kk * 128:(kk + 1) * 128],
                    s_ifm[:, kk * TOK + th * 512: kk * TOK + th * 512 + 512],
                    start=(kk == 0), stop=(kk == CH - 1)),
                    wait=([("L", wLo), ("V", p.count("V"))] if kk == 0 else []),
                    inc="T" if kk == CH - 1 else None)
        wTo = p.cnt["T"]
        p.emit("sync", lambda h, m=m: h.dma_start(
            out=s_g1t[:], in_=g1_spill.ap()[m * 128:(m + 1) * 128, :]),
            wait=[("G", w_g1), ("V", p.count("V"))], inc="L", inc_by=16)
        p.emit("sync", lambda h, m=m: h.dma_start(
            out=s_x1t[:], in_=x1_spill.ap()[m * 128:(m + 1) * 128, :]),
            inc="L", inc_by=16)
        wLg = p.cnt["L"]
        p.emit("vector", lambda h: h.tensor_tensor(s_pj[:, 0:512], ps_m0[:], s_row2[:, 0:512], op=A.mult),
               wait=[("T", wTo), ("V", w_rowo), ("L", wLg)])
        p.emit("vector", lambda h: h.tensor_tensor(s_pj[:, 512:1024], ps_m1[:], s_row2[:, 512:1024], op=A.mult),
               inc="V")
        p.drain("vector")
        if t_s1 is not None:
            p.emit("gpsimd", lambda h, m=m: h.dma_start(
                out=t_s1.ap()[m * 128:(m + 1) * 128, :], in_=s_pj[:]),
                wait=[("V", p.cnt["V"])], inc="G", inc_by=16)
            p.wait_only("vector", [("G", p.cnt["G"])])
        p.emit("vector", lambda h: h.tensor_tensor(s_pj[:], s_pj[:], s_g1t[:], op=A.mult))
        p.drain("vector")
        p.emit("vector", lambda h: h.tensor_tensor(s_pj[:], s_pj[:], s_x1t[:], op=A.add), inc="V")
        p.drain("vector")
        p.emit("gpsimd", lambda h, m=m: h.dma_start(
            out=out.ap()[m * 128:(m + 1) * 128, :], in_=s_pj[:]),
            wait=[("V", p.cnt["V"])], inc="G", inc_by=16)
        if t_x2 is not None:
            p.emit("gpsimd", lambda h, m=m: h.dma_start(
                out=t_x2.ap()[m * 128:(m + 1) * 128, :], in_=s_pj[:]),
                inc="G", inc_by=16)
        p.wait_only("vector", [("G", p.cnt["G"])])
    p.wait_only("gpsimd", [("G", p.cnt["G"])])

    p.realize()
    stack.close()
    return nc, ["out"] + ["tap_" + t for t in debug_taps]



def prep_inputs(kw):
    import ml_dtypes
    x = np.asarray(kw["L0_attn_hidden_states"], dtype=np.float32)
    ident = np.eye(128, dtype=np.float32)
    onesb = np.ones((128, 128), dtype=ml_dtypes.bfloat16)
    tribT = np.triu(np.ones((128, 128))).astype(ml_dtypes.bfloat16)
    identb = np.eye(128, dtype=ml_dtypes.bfloat16)
    wzT = np.ascontiguousarray(np.asarray(kw["L0_attn_W_z"], np.float32).T)
    whT = np.ascontiguousarray(np.asarray(kw["L0_attn_W_h"], np.float32).T)
    def cmj(v):
        return np.ascontiguousarray(np.asarray(v, np.float32).reshape(CH, 128).T)
    h0t = cmj(kw["L0_attn_h0"])
    gw_full = np.asarray(kw["L0_ffn_gate_w"], np.float32)
    up_full = np.asarray(kw["L0_ffn_up_w"], np.float32)
    dw_full = np.asarray(kw["L0_ffn_down_w"], np.float32)
    gw_s = [np.ascontiguousarray(gw_full[c * 1024:(c + 1) * 1024]) for c in range(NC)]
    up_s = [np.ascontiguousarray(up_full[c * 1024:(c + 1) * 1024]) for c in range(NC)]
    dw_s = [np.ascontiguousarray(dw_full[c * 256:(c + 1) * 256]) for c in range(NC)]
    wq_f = np.asarray(kw["L1_attn_Wq"], np.float32)
    wk_f = np.asarray(kw["L1_attn_Wk"], np.float32)
    wv_f = np.asarray(kw["L1_attn_Wv"], np.float32)
    wo_f = np.asarray(kw["L1_attn_Wo"], np.float32)
    wq_s = [np.ascontiguousarray(wq_f[c * 256:(c + 1) * 256]) for c in range(NC)]
    wk_s = [np.ascontiguousarray(wk_f[c * 256:(c + 1) * 256]) for c in range(NC)]
    wv_s = [np.ascontiguousarray(wv_f[c * 256:(c + 1) * 256]) for c in range(NC)]
    wo_s = [np.ascontiguousarray(wo_f[c * 256:(c + 1) * 256]) for c in range(NC)]
    wbT = np.ascontiguousarray(np.asarray(kw["L1_ffn_W_base"], np.float32).T)
    bb1 = cmj(kw["L1_ffn_b_base"])
    wrT = np.ascontiguousarray(np.asarray(kw["L1_ffn_W_router"], np.float32).T)
    aaT = np.ascontiguousarray(
        np.asarray(kw["L1_ffn_A_w"], np.float32).transpose(1, 0, 2).reshape(64, D).T)
    ball = np.ascontiguousarray(
        np.asarray(kw["L1_ffn_B_w"], np.float32).transpose(2, 0, 1).reshape(64, D))
    freqs = (1.0 / 10000.0 ** (np.arange(0, 128, 2, dtype=np.float64) / 128.0))
    ii = np.arange(128)[:, None]
    jj = np.arange(512)[None, :]
    dmask = np.concatenate(
        [np.where(jj <= a * 128 + ii, 0.0, NEG) for a in range(4)], axis=1).astype(np.float32)

    in_maps = []
    for c in range(NC):
        b, hf = c // 2, c % 2
        xs = np.ascontiguousarray(x[b, hf * TOK:(hf + 1) * TOK, :].T)
        pos = np.arange(hf * TOK, (hf + 1) * TOK, dtype=np.float64)
        ang = pos[None, :] * freqs[:, None]
        ropeC = np.repeat(np.cos(ang), 2, axis=0).astype(np.float32)
        sgn = np.where(np.arange(128) % 2 == 0, -1.0, 1.0)[:, None]
        ropeS = (np.repeat(np.sin(ang), 2, axis=0) * sgn).astype(np.float32)
        pm0 = np.full((128, CH), 0.0 if hf == 0 else NEG, np.float32)
        cm = np.full((128, NC), NEG, np.float32)
        if hf == 1:
            cm[:, c - 1] = 0.0
        wsel = np.full((128, 1), 1.0 if hf == 0 else 0.0, np.float32)
        m = {
            "x_fm": xs, "wzT": wzT, "whT": whT,
            "gwN": gw_s[c], "upN": up_s[c], "dwN": dw_s[c],
            "wqN": wq_s[c], "wkN": wk_s[c], "wvN": wv_s[c], "woN": wo_s[c],
            "wbT": wbT, "bb1": bb1, "wrT": wrT, "aaT": aaT, "ball": ball,
            "ropeC": ropeC, "ropeS": ropeS, "dmask": dmask,
            "bz": cmj(kw["L0_attn_b_z"]),
            "bh": cmj(kw["L0_attn_b_h"]),
            "lnw": cmj(kw["L0_attn_ln_w"]),
            "lnb": cmj(kw["L0_attn_ln_b"]),
            "h0t": h0t, "pm0": pm0, "cmask": cm, "wsel": wsel,
            "ident": ident, "onesb": onesb, "tribT": tribT, "identb": identb,
        }
        in_maps.append(m)
    return in_maps


def kernel(**kw):
    in_maps = prep_inputs(kw)
    nc, outs = build_nc()
    res = run_bass_kernel_spmd(nc, in_maps, core_ids=list(range(NC)))
    parts = []
    for c in range(NC):
        parts.append(np.asarray(res.results[c]["out"]).T)
    full = np.stack(parts, 0).reshape(B, T, D)
    return full.astype(np.float32)


# version 11
# speedup vs baseline: 1.0351x; 1.0210x over previous
import math
import os
os.environ.setdefault("TRN_ALLOW_ICI_ALL_TO_ALL", "1")
import numpy as np

import concourse.bass as bass
import concourse.mybir as mybir
from concourse.bass_utils import run_bass_kernel_spmd
from concourse.library_config import all_libraries

F32 = mybir.dt.float32
U32 = mybir.dt.uint32
BF16 = mybir.dt.bfloat16
A = mybir.AluOpType
ACT = mybir.ActivationFunctionType
AX = mybir.AxisListType
LIBS = {l.name: l for l in all_libraries}

B, T, D = 4, 2048, 2048
NC = 8
TOK = 1024
CH = D // 128
DFF = 8192
LN2F = float(np.float32(np.log(2.0)))
NEG = -1.0e30


ENGINES = ("sync", "tensor", "vector", "scalar", "gpsimd")


class Plan:
    def __init__(self, nc):
        self.nc = nc
        self.ops = {e: [] for e in ENGINES}
        self.cnt = {}
        self.sems = {}

    def sem(self, name):
        if name not in self.sems:
            h = self.nc.semaphore(name).__enter__()
            self.sems[name] = h
            self.cnt[name] = 0
        return self.sems[name]

    def count(self, name):
        self.sem(name)
        return self.cnt[name]

    def emit(self, eng, fn, wait=(), inc=None, inc_by=1):
        for s, v in wait:
            self.sem(s)
        if inc:
            self.sem(inc)
            self.cnt[inc] += inc_by
        waits = tuple(wait)
        inc_name = inc
        n = inc_by

        def op(h, _fn=fn, _waits=waits, _inc=inc_name, _n=n):
            for s, v in _waits:
                h.wait_ge(self.sems[s], v)
            inst = _fn(h)
            if _inc is not None:
                inst.then_inc(self.sems[_inc], _n)
        self.ops[eng].append(op)
        return self.cnt[inc] if inc else None

    def wait_only(self, eng, wait):
        def op(h, _waits=tuple(wait)):
            for s, v in _waits:
                h.wait_ge(self.sems[s], v)
        self.ops[eng].append(op)

    def drain(self, eng):
        self.ops[eng].append(lambda h: h.drain())

    def realize(self):
        nc = self.nc
        with nc.Block() as block:
            @block.sync
            def _(e):
                for op in self.ops["sync"]:
                    op(e)

            @block.tensor
            def _(e):
                for op in self.ops["tensor"]:
                    op(e)

            @block.vector
            def _(e):
                for op in self.ops["vector"]:
                    op(e)

            @block.scalar
            def _(e):
                for op in self.ops["scalar"]:
                    op(e)

            @block.gpsimd
            def _(e):
                for op in self.ops["gpsimd"]:
                    op(e)



def build_nc(debug_taps=()):
    nc = bass.Bass("TRN2", target_bir_lowering=False, debug=False)

    di = {}

    def inp(name, shape, dtype=F32):
        di[name] = nc.dram_tensor(name, shape, dtype, kind="ExternalInput")
        return di[name]

    x_fm = inp("x_fm", [D, TOK])
    wzT = inp("wzT", [D, D])
    whT = inp("whT", [D, D])
    bz = inp("bz", [128, CH])
    bh = inp("bh", [128, CH])
    lnw = inp("lnw", [128, CH])
    lnb = inp("lnb", [128, CH])
    h0t = inp("h0t", [128, CH])
    pm0 = inp("pm0", [128, CH])
    cmask = inp("cmask", [128, NC])
    wsel = inp("wsel", [128, 1])
    ident = inp("ident", [128, 128])
    onesb_i = inp("onesb", [128, 128], BF16)
    tribT_i = inp("tribT", [128, 128], BF16)
    identb_i = inp("identb", [128, 128], BF16)

    out = nc.dram_tensor("out", [D, TOK], F32, kind="ExternalOutput")

    taps = {}

    def tap(name, shape):
        if name in debug_taps:
            taps[name] = nc.dram_tensor("tap_" + name, shape, F32,
                                        kind="ExternalOutput")
            return taps[name]
        return None

    ag4_in = nc.dram_tensor("ag4_in", [D], F32)
    ag4_out = nc.dram_tensor("ag4_out", [NC, D], F32)
    ar5_in = nc.dram_tensor("ar5_in", [8], F32)
    ar5_out = nc.dram_tensor("ar5_out", [8], F32)
    bc_scr = nc.dram_tensor("bc_scr", [4, TOK], F32)
    xnh_sp = nc.dram_tensor("xnh_sp", [D, TOK], BF16)
    xnl_sp = nc.dram_tensor("xnl_sp", [D, TOK], BF16)

    p = Plan(nc)
    import contextlib
    stack = contextlib.ExitStack()
    frees = {}

    def sb(name, shape, dtype=F32):
        cm = nc.sbuf_tensor(name, shape, dtype)
        h = cm.__enter__()
        frees[name] = cm
        return h

    def sbfree(name):
        frees.pop(name).__exit__(None, None, None)

    def ps(name, shape, dtype=F32):
        cm = nc.psum_tensor(name, shape, dtype)
        h = cm.__enter__()
        frees["PS:" + name] = cm
        return h

    def psfree(name):
        frees.pop("PS:" + name).__exit__(None, None, None)

    def free_back_to(name):
        keys = list(frees.keys())
        idx = keys.index(name)
        for k in reversed(keys[idx:]):
            frees.pop(k).__exit__(None, None, None)

    c_ident = sb("c_ident", [128, 128])
    c_identb = sb("c_identb", [128, 128], BF16)
    c_onesb = sb("c_onesb", [128, 128], BF16)
    c_triTb = sb("c_triTb", [128, 128], BF16)
    c_two = sb("c_two", [128, 1])
    c_eps = sb("c_eps", [128, 1])
    s_h0 = sb("s_h0", [128, CH])
    s_pm0 = sb("s_pm0", [128, CH])
    s_cmask = sb("s_cmask", [128, NC])
    s_wsel = sb("s_wsel", [128, 1])
    s_lnw = sb("s_lnw", [128, CH])
    s_lnb = sb("s_lnb", [128, CH])
    s_bz = sb("s_bz", [128, CH])
    s_bh = sb("s_bh", [128, CH])

    def ld(dst, src_ap, inc="L"):
        return p.emit("sync", lambda h, d=dst, s=src_ap: h.dma_start(out=d, in_=s), inc=inc, inc_by=16)

    ld(c_ident[:], ident.ap())
    ld(c_identb[:], identb_i.ap())
    ld(c_onesb[:], onesb_i.ap())
    ld(c_triTb[:], tribT_i.ap())
    ld(s_h0[:], h0t.ap())
    ld(s_pm0[:], pm0.ap())
    ld(s_cmask[:], cmask.ap())
    ld(s_wsel[:], wsel.ap())
    for name, dst in ((lnw, s_lnw), (lnb, s_lnb), (bz, s_bz), (bh, s_bh)):
        ld(dst[:], name.ap())
    w_const = p.cnt["L"]

    p.emit("vector", lambda h: h.memset(c_two[:], 2.0))
    p.emit("vector", lambda h: h.memset(c_eps[:], 1e-38))
    p.emit("vector", lambda h: h.nop(), wait=[("L", w_const)], inc="V")
    w_cconst = p.cnt["V"]

    s_sqb = sb("s_sqb", [128, 2 * 512], BF16)
    s_xb = sb("s_xb", [128, 2 * 512], BF16)
    s_sum = sb("s_sum", [1, TOK])
    s_ssq = sb("s_ssq", [1, TOK])
    s_rstd = sb("s_rstd", [1, TOK])
    s_mub = sb("s_mub", [128, TOK])
    s_rstdb = sb("s_rstdb", [128, TOK])
    c_eps5 = sb("c_eps5", [1, 1])

    s_xnh = sb("s_xnh", [128, CH * TOK], BF16)
    s_xnl = sb("s_xnl", [128, CH * TOK], BF16)

    s_x = sb("s_x", [128, CH * 512])

    def xc(k):
        return s_x[:, k * 512:(k + 1) * 512]


    def xnc(k):
        return s_xn[:, k * TOK:(k + 1) * TOK]

    ps_sum = ps("ps_sum", [1, 512])
    ps_ssq = ps("ps_ssq", [1, 512])
    s_mu = s_sum

    def emit_ln_fm(src_c, dst_c, wait_src, eps_col, gam_col=None, beta_col=None,
                   ncols=TOK):
        t_base = p.count("T")
        j = 0
        for th in range(ncols // 512):
            sl = slice(th * 512, (th + 1) * 512)
            for k in range(CH):
                pp = slice((j % 2) * 512, (j % 2) * 512 + 512)
                w = [wait_src] if (th == 0 and k == 0) else []
                w = w + [("T", t_base + j - 1 if j >= 2 else t_base)]
                p.emit("vector", lambda h, k=k, sl=sl, pp=pp: h.tensor_copy(
                    s_xb[:, pp], src_c(k)[:, sl]), wait=w, inc="V")
                p.emit("scalar", lambda h, k=k, sl=sl, pp=pp: h.activation(
                    s_sqb[:, pp], src_c(k)[:, sl], ACT.Square), wait=w, inc="S")
                p.emit("tensor", lambda h, k=k, pp=pp: h.matmul(
                    ps_sum[:], c_onesb[:, 0:1], s_xb[:, pp], start=(k == 0), stop=False),
                    wait=[("V", p.cnt["V"]), ("S", p.cnt["S"])])
                p.emit("tensor", lambda h, k=k, pp=pp, last=(k == CH - 1): h.matmul(
                    ps_ssq[:], c_onesb[:, 0:1], s_sqb[:, pp], start=(k == 0), stop=last),
                    inc="T")
                j += 1
            p.emit("vector", lambda h, sl=sl: h.tensor_copy(s_sum[:, sl], ps_sum[:]),
                   wait=[("T", p.cnt["T"])])
            p.emit("vector", lambda h, sl=sl: h.tensor_copy(s_ssq[:, sl], ps_ssq[:]), inc="V")
        p.drain("vector")
        p.emit("vector", lambda h: h.tensor_scalar(s_mu[:, 0:ncols], s_sum[:, 0:ncols], 1.0 / D, None, op0=A.mult))
        p.drain("vector")
        p.emit("vector", lambda h: h.tensor_tensor(
            s_rstd[:, 0:ncols], s_mu[:, 0:ncols], s_mu[:, 0:ncols], op=A.mult), inc="V")
        p.drain("vector")
        p.emit("vector", lambda h: h.scalar_tensor_tensor(
            s_rstd[:, 0:ncols], s_ssq[:, 0:ncols], 1.0 / D, s_rstd[:, 0:ncols],
            op0=A.mult, op1=A.subtract), inc="V")
        p.drain("vector")
        p.emit("scalar", lambda h: h.activation(s_rstd[:, 0:ncols], s_rstd[:, 0:ncols], ACT.Sqrt,
                                                bias=eps_col, scale=1.0),
               wait=[("V", p.cnt["V"])], inc="S")
        p.emit("vector", lambda h: h.reciprocal(s_rstd[:, 0:ncols], s_rstd[:, 0:ncols]),
               wait=[("S", p.cnt["S"])], inc="V")
        p.drain("vector")
        p.emit("gpsimd", lambda h: h.dma_start(out=bc_scr.ap()[0:1, 0:ncols], in_=s_mu[:, 0:ncols]),
               wait=[("V", p.cnt["V"])], inc="P", inc_by=16)
        p.emit("gpsimd", lambda h: h.dma_start(out=bc_scr.ap()[1:2, 0:ncols], in_=s_rstd[:, 0:ncols]),
               inc="P", inc_by=16)
        wg = p.cnt["P"]
        p.emit("gpsimd", lambda h: h.dma_start(
            out=s_mub[:, 0:ncols], in_=bc_scr.ap()[0:1, 0:ncols].broadcast_to([128, ncols])),
            wait=[("P", wg)], inc="P", inc_by=16)
        p.emit("gpsimd", lambda h: h.dma_start(
            out=s_rstdb[:, 0:ncols], in_=bc_scr.ap()[1:2, 0:ncols].broadcast_to([128, ncols])),
            inc="P", inc_by=16)
        wP = p.cnt["P"]
        for k in range(CH):
            p.emit("vector", lambda h, k=k: h.tensor_tensor(dst_c(k), src_c(k), s_mub[:, 0:ncols], op=A.subtract),
                   wait=[("P", wP)] if k == 0 else [])
        p.drain("vector")
        for k in range(CH):
            p.emit("vector", lambda h, k=k: h.tensor_tensor(dst_c(k), dst_c(k), s_rstdb[:, 0:ncols], op=A.mult))
        p.drain("vector")
        if gam_col is not None:
            for k in range(CH):
                p.emit("vector", lambda h, k=k: h.tensor_scalar(
                    dst_c(k), dst_c(k), gam_col(k), beta_col(k), op0=A.mult, op1=A.add))
            p.drain("vector")
        p.emit("vector", lambda h: h.nop(), inc="V")
        return p.cnt["V"]

    p.emit("vector", lambda h: h.memset(c_eps5[:], 1e-5), inc="V")
    p.drain("vector")

    def xnh(k):
        return s_xnh[:, k * TOK:(k + 1) * TOK]

    def xnl(k):
        return s_xnl[:, k * TOK:(k + 1) * TOK]

    for hf in range(2):
        hsl = slice(hf * 512, hf * 512 + 512)
        for k in range(CH):
            ld(xc(k), x_fm.ap()[k * 128:(k + 1) * 128, hsl],
               )
        w_xload = p.cnt["L"]
        w_half = emit_ln_fm(xc, xc, ("L", w_xload), c_eps5[0:1, 0:1], ncols=512)
        for k in range(CH):
            p.emit("vector", lambda h, k=k, hsl=hsl: h.tensor_copy(xnh(k)[:, hsl], xc(k)),
                   wait=[("V", w_half)] if k == 0 else [])
        p.drain("vector")
        for k in range(CH):
            p.emit("vector", lambda h, k=k, hsl=hsl: h.tensor_tensor(
                xnl(k)[:, hsl], xc(k), xnh(k)[:, hsl], op=A.subtract),
                inc="V" if k == CH - 1 else None)
        p.drain("vector")
        p.wait_only("sync", [("V", p.cnt["V"])])
    w_xnsplit = p.cnt["V"]
    p.emit("gpsimd", lambda h: h.dma_start(out=xnh_sp.ap(), in_=s_xnh[:]),
           wait=[("V", w_xnsplit)], inc="G", inc_by=16)
    p.emit("gpsimd", lambda h: h.dma_start(out=xnl_sp.ap(), in_=s_xnl[:]),
           inc="G", inc_by=16)
    w_xnspill = p.cnt["G"]
    sbfree("s_x")

    a_spill = nc.dram_tensor("a_spill", [D, TOK], F32)
    s_spill = nc.dram_tensor("s_spill", [D, TOK], F32)
    NTOT = float(B * (T + 1) * D)

    c_half = sb("c_half", [128, 1])
    c_one = sb("c_one", [128, 1])
    p.emit("vector", lambda h: h.memset(c_half[:], 0.5))
    p.emit("vector", lambda h: h.memset(c_one[:], 1.0), inc="V")
    p.drain("vector")

    def emit_logsig(src, dst, tmp, wait=()):
        p.emit("scalar", lambda h: h.activation(tmp, src, ACT.Abs), wait=list(wait), inc="S")
        p.drain("scalar")
        p.emit("scalar", lambda h: h.activation(tmp, tmp, ACT.Exp, scale=-1.0), inc="S")
        p.drain("scalar")
        p.emit("scalar", lambda h: h.activation(tmp, tmp, ACT.Ln, bias=c_one[:, 0:1], scale=1.0), inc="S")
        wS = p.cnt["S"]
        p.emit("vector", lambda h: h.tensor_scalar(dst, src, 0.0, None, op0=A.min),
               wait=[("S", wS)])
        p.drain("vector")
        p.emit("vector", lambda h: h.tensor_tensor(dst, dst, tmp, op=A.subtract), inc="V")
        p.drain("vector")
        return p.cnt["V"]

    s_u0 = sb("s_u0", [128, CH])
    s_gt1 = sb("s_gt1", [128, CH])
    w0 = emit_logsig(s_h0[:], s_u0[:], s_gt1[:],
                     wait=[("L", w_const), ("V", w_xnsplit)])
    p.emit("vector", lambda h: h.tensor_scalar(s_gt1[:], s_h0[:], 0.0, None, op0=A.max), inc="V")
    p.drain("vector")
    p.emit("scalar", lambda h: h.activation(s_gt1[:], s_gt1[:], ACT.Ln,
                                            bias=c_half[:, 0:1], scale=1.0),
           wait=[("V", p.cnt["V"])], inc="S")
    s_msk0 = sb("s_msk0", [128, CH])
    p.emit("vector", lambda h: h.tensor_scalar(s_msk0[:], s_h0[:], 0.0, None, op0=A.is_ge),
           wait=[("S", p.cnt["S"])])
    p.drain("vector")
    p.emit("vector", lambda h: h.tensor_tensor(s_gt1[:], s_gt1[:], s_u0[:], op=A.subtract))
    p.drain("vector")
    p.emit("vector", lambda h: h.tensor_tensor(s_gt1[:], s_gt1[:], s_msk0[:], op=A.mult))
    p.drain("vector")
    p.emit("vector", lambda h: h.tensor_tensor(s_u0[:], s_u0[:], s_gt1[:], op=A.add))
    p.drain("vector")
    p.emit("vector", lambda h: h.tensor_tensor(s_u0[:], s_u0[:], s_pm0[:], op=A.add), inc="V")
    p.drain("vector")
    w_u0 = p.cnt["V"]

    t_u0 = tap("u0", [128, CH])
    if t_u0 is not None:
        p.emit("gpsimd", lambda h: h.dma_start(out=t_u0.ap(), in_=s_u0[:]),
               wait=[("V", w_u0)], inc="G", inc_by=16)

    TOK1 = TOK + 1
    s_cl = sb("s_cl", [128, CH * TOK1])

    def clc(k):
        return s_cl[:, k * TOK1:(k + 1) * TOK1]

    pool1 = sb("pool1", [128, TOK1])
    pool2 = sb("pool2", [128, TOK1])
    s_kt = pool1[:, 0:TOK]
    s_m = pool1[:]
    s_tht = pool2[:, 0:TOK]
    s_de = pool2[:]
    s_lz = s_mub
    s_lc = s_rstdb
    s_lth = sb("s_lth", [128, TOK])
    s_tmp = sb("s_tmp", [128, TOK])
    s_tmp2 = pool1[:, 0:TOK]
    s_t1a = sb("s_t1a", [128, 1])
    s_t1b = sb("s_t1b", [128, 1])
    s_lch = sb("s_lch", [128, TOK], BF16)
    s_lcl = sb("s_lcl", [128, TOK], BF16)
    s_rth = s_sum
    s_rth_h = sb("s_rtl_h", [1, TOK], BF16)
    s_rth_l = sb("s_rtl_l", [1, TOK], BF16)
    s_u = sb("s_u", [128, TOK1])
    s_tot = sb("s_tot", [128, CH])

    s_wst = sb("s_wst", [128, CH * 128])
    s_wzh = sb("s_wzh", [128, CH * 128], BF16)
    s_wzl = sb("s_wzl", [128, CH * 128], BF16)
    s_whh = sb("s_whh", [128, CH * 128], BF16)
    s_whl = sb("s_whl", [128, CH * 128], BF16)

    ps_z0 = ps("ps_z0", [128, 512])
    ps_z1 = ps("ps_z1", [128, 512])
    ps_h0 = ps("ps_h0", [128, 512])
    ps_h1 = ps("ps_h1", [128, 512])
    ps_a0 = ps("ps_a0", [128, 512])
    ps_a1 = ps("ps_a1", [128, 512])
    ps_cs = ps_sum

    p.emit("vector", lambda h: h.memset(s_rth[:], 0.0))
    p.emit("vector", lambda h: h.memset(s_rth_h[:], 0.0))
    p.emit("vector", lambda h: h.memset(s_rth_l[:], 0.0), inc="V")
    p.drain("vector")

    for m in range(CH):
        for kk in range(CH):
            p.emit("sync", lambda h, kk=kk, m=m: h.dma_start(
                out=s_wst[:, kk * 128:(kk + 1) * 128],
                in_=wzT.ap()[kk * 128:(kk + 1) * 128, m * 128:(m + 1) * 128]),
                wait=([("V", p.cnt["V"])] if kk == 0 else []), inc="L", inc_by=16)
        wL = p.cnt["L"]
        p.emit("vector", lambda h: h.tensor_copy(s_wzh[:], s_wst[:]),
               wait=[("L", wL), ("T", p.count("T"))])
        p.drain("vector")
        p.emit("vector", lambda h: h.tensor_tensor(s_wzl[:], s_wst[:], s_wzh[:], op=A.subtract),
               inc="V")
        p.drain("vector")
        for kk in range(CH):
            p.emit("sync", lambda h, kk=kk, m=m: h.dma_start(
                out=s_wst[:, kk * 128:(kk + 1) * 128],
                in_=whT.ap()[kk * 128:(kk + 1) * 128, m * 128:(m + 1) * 128]),
                wait=([("V", p.cnt["V"])] if kk == 0 else []), inc="L", inc_by=16)
        wL2 = p.cnt["L"]
        p.emit("vector", lambda h: h.tensor_copy(s_whh[:], s_wst[:]),
               wait=[("L", wL2)])
        p.drain("vector")
        p.emit("vector", lambda h: h.tensor_tensor(s_whl[:], s_wst[:], s_whh[:], op=A.subtract),
               inc="V")
        p.drain("vector")
        wsplit = p.cnt["V"]

        for th in range(2):
            pz = ps_z0 if th == 0 else ps_z1
            ph = ps_h0 if th == 0 else ps_h1
            sl = slice(th * 512, (th + 1) * 512)
            first = True
            for (wh_, wl_, rh_, rl_, pst) in (
                (s_wzh, s_wzl, s_xnh, s_xnl, pz),
                (s_whh, s_whl, s_xnh, s_xnl, ph),
            ):
                for pi, (wop, rop) in enumerate((( wh_, rh_), (wh_, rl_), (wl_, rh_))):
                    for kk in range(CH):
                        st = (pi == 0 and kk == 0)
                        sp = (pi == 2 and kk == CH - 1)
                        wt = [("V", wsplit)] if first else []
                        first = False
                        if wop in (s_wzh, s_wzl):
                            lw = wop[:, kk * 128:(kk + 1) * 128]
                        else:
                            lw = wop[:, kk * 128:(kk + 1) * 128]
                        rr = (s_xnh if rop is s_xnh else s_xnl)[:, kk * TOK + th * 512: kk * TOK + th * 512 + 512]
                        p.emit("tensor", lambda h, lw=lw, rr=rr, pst=pst, st=st, sp=sp: h.matmul(
                            pst[:], lw, rr, start=st, stop=sp),
                            wait=wt, inc="T" if sp else None)
        wTproj = p.cnt["T"]

        p.emit("vector", lambda h, m=m: h.tensor_scalar(
            s_kt[:, 0:512], ps_z0[:], s_bz[:, m:m + 1], None, op0=A.add),
            wait=[("T", wTproj)])
        p.emit("vector", lambda h, m=m: h.tensor_scalar(
            s_kt[:, 512:1024], ps_z1[:], s_bz[:, m:m + 1], None, op0=A.add))
        p.emit("vector", lambda h, m=m: h.tensor_scalar(
            s_tht[:, 0:512], ps_h0[:], s_bh[:, m:m + 1], None, op0=A.add))
        p.emit("vector", lambda h, m=m: h.tensor_scalar(
            s_tht[:, 512:1024], ps_h1[:], s_bh[:, m:m + 1], None, op0=A.add), inc="V")
        p.drain("vector")

        emit_logsig(s_kt[:], s_lz[:], s_tmp[:], wait=[("V", p.count("V"))])
        p.emit("vector", lambda h: h.tensor_tensor(s_lc[:], s_lz[:], s_kt[:], op=A.subtract), inc="V")
        p.drain("vector")

        emit_logsig(s_tht[:], s_lth[:], s_tmp[:], wait=[("V", p.count("V"))])
        p.emit("vector", lambda h: h.tensor_scalar(s_tmp[:], s_tht[:], 0.0, None, op0=A.max), inc="V")
        p.drain("vector")
        p.emit("scalar", lambda h: h.activation(s_tmp[:], s_tmp[:], ACT.Ln,
                                                bias=c_half[:, 0:1], scale=1.0),
               wait=[("V", p.cnt["V"])], inc="S")
        p.emit("vector", lambda h: h.tensor_scalar(s_tmp2[:], s_tht[:], 0.0, None, op0=A.is_ge),
               wait=[("S", p.cnt["S"])])
        p.drain("vector")
        p.emit("vector", lambda h: h.tensor_tensor(s_tmp[:], s_tmp[:], s_lth[:], op=A.subtract))
        p.drain("vector")
        p.emit("vector", lambda h: h.tensor_tensor(s_tmp[:], s_tmp[:], s_tmp2, op=A.mult))
        p.drain("vector")
        p.emit("vector", lambda h: h.tensor_tensor(s_lth[:], s_lth[:], s_tmp[:], op=A.add), inc="V")
        p.drain("vector")

        p.emit("vector", lambda h: h.tensor_copy(s_lch[:], s_lc[:]))
        p.drain("vector")
        p.emit("vector", lambda h: h.tensor_tensor(s_lcl[:], s_lc[:], s_lch[:], op=A.subtract), inc="V")
        p.drain("vector")
        wlc = p.cnt["V"]

        for th in range(2):
            pa = ps_a0 if th == 0 else ps_a1
            sl = slice(th * 512, (th + 1) * 512)
            p.emit("tensor", lambda h, sl=sl, pa=pa: h.matmul(
                pa[:], c_triTb[:], s_lch[:, sl], start=True, stop=False),
                wait=[("V", wlc)])
            p.emit("tensor", lambda h, sl=sl, pa=pa: h.matmul(
                pa[:], c_triTb[:], s_lcl[:, sl], start=False, stop=False))
            p.emit("tensor", lambda h, sl=sl, pa=pa: h.matmul(
                pa[:], c_onesb[0:1, :], s_rth_h[:, sl], start=False, stop=False))
            p.emit("tensor", lambda h, sl=sl, pa=pa: h.matmul(
                pa[:], c_onesb[0:1, :], s_rth_l[:, sl], start=False, stop=True),
                inc="T")
            p.emit("tensor", lambda h, sl=sl: h.matmul(
                ps_cs[:], c_onesb[:, 0:1], s_lch[:, sl], start=True, stop=False),
                wait=[("V", p.count("V"))])
            p.emit("tensor", lambda h, sl=sl: h.matmul(
                ps_cs[:], c_onesb[:, 0:1], s_lcl[:, sl], start=False, stop=True),
                inc="T")
            wTa = p.cnt["T"]
            p.emit("vector", lambda h, sl=sl: h.tensor_tensor(
                s_rth[:, sl], s_rth[:, sl], ps_cs[:], op=A.add),
                wait=[("T", wTa)], inc="V")
        p.drain("vector")
        p.emit("vector", lambda h: h.tensor_copy(s_rth_h[:], s_rth[:]),
               wait=[("T", p.count("T"))])
        p.drain("vector")
        p.emit("vector", lambda h: h.tensor_tensor(s_rth_l[:], s_rth[:], s_rth_h[:], op=A.subtract),
               inc="V")
        p.drain("vector")

        p.emit("vector", lambda h, m=m: h.tensor_copy(s_u[:, 0:1], s_u0[:, m:m + 1]),
               wait=[("V", w_u0)] if m == 0 else [])
        p.emit("vector", lambda h: h.tensor_tensor(s_u[:, 1:TOK1], s_lz[:], s_lth[:], op=A.add))
        p.drain("vector")
        p.emit("vector", lambda h: h.tensor_tensor(s_u[:, 1:513], s_u[:, 1:513], ps_a0[:], op=A.subtract))
        p.emit("vector", lambda h: h.tensor_tensor(s_u[:, 513:1025], s_u[:, 513:1025], ps_a1[:], op=A.subtract))
        p.emit("vector", lambda h: h.tensor_copy(s_tmp[:, 0:512], ps_a0[:]))
        p.emit("vector", lambda h: h.tensor_copy(s_tmp[:, 512:1024], ps_a1[:]), inc="V")
        p.drain("vector")
        wu = p.cnt["V"]
        p.emit("gpsimd", lambda h, m=m: h.dma_start(
            out=a_spill.ap()[m * 128:(m + 1) * 128, :], in_=s_tmp[:]),
            wait=[("V", wu)], inc="G", inc_by=16)
        wGa = p.cnt["G"]
        p.wait_only("tensor", [("G", wGa)])
        p.wait_only("vector", [("G", wGa)])
        p.wait_only("scalar", [("G", wGa)])

        p.emit("vector", lambda h: h.tensor_tensor_scan(
            s_m[:], s_u[:], s_u[:], -1e30, op0=A.max, op1=A.bypass), wait=[("V", wu)])
        p.drain("vector")
        p.emit("vector", lambda h: h.memset(s_de[:, 0:1], 0.0))
        p.emit("vector", lambda h: h.tensor_tensor(s_de[:, 1:TOK1], s_m[:, 0:TOK1 - 1], s_m[:, 1:TOK1], op=A.subtract))
        p.drain("vector")
        p.emit("vector", lambda h: h.tensor_tensor(s_u[:], s_u[:], s_m[:], op=A.subtract), inc="V")
        p.drain("vector")
        wsc = p.cnt["V"]
        p.emit("scalar", lambda h: h.activation(s_de[:], s_de[:], ACT.Exp), wait=[("V", wsc)])
        p.drain("scalar")
        p.emit("scalar", lambda h: h.activation(s_u[:], s_u[:], ACT.Exp), inc="S")
        p.drain("scalar")
        p.emit("vector", lambda h: h.tensor_tensor_scan(
            s_de[:], s_de[:], s_u[:], 0.0, op0=A.mult, op1=A.add),
            wait=[("S", p.cnt["S"])], inc="V")
        p.drain("vector")
        p.emit("scalar", lambda h: h.activation(s_de[:], s_de[:], ACT.Ln),
               wait=[("V", p.cnt["V"])], inc="S")
        p.drain("scalar")
        p.emit("vector", lambda h, m=m: h.tensor_tensor(clc(m), s_m[:], s_de[:], op=A.add),
               wait=[("S", p.cnt["S"])], inc="V")
        p.drain("vector")
        p.emit("vector", lambda h, m=m: h.tensor_copy(s_tot[:, m:m + 1], clc(m)[:, TOK1 - 1:TOK1]),
               inc="V")
        p.drain("vector")
    w_scan = p.cnt["V"]

    t_cl = tap("clocal", [D, TOK1])
    if t_cl is not None:
        for k in range(CH):
            p.emit("gpsimd", lambda h, k=k: h.dma_start(
                out=t_cl.ap()[k * 128:(k + 1) * 128, :], in_=clc(k)),
                wait=[("V", w_scan)] if k == 0 else [], inc="G", inc_by=16)

    ag4_in2 = nc.dram_tensor("ag4_in2", [128, CH], F32)
    ag4_out2 = nc.dram_tensor("ag4_out2", [NC, 128, CH], F32)
    p.emit("gpsimd", lambda h: h.dma_start(out=ag4_in2.ap(), in_=s_tot[:]),
           wait=[("V", w_scan)], inc="G", inc_by=16)
    p.wait_only("gpsimd", [("G", p.cnt["G"])])
    p.emit("gpsimd", lambda h: h.collective_compute(
        "AllGather", A.bypass, replica_groups=[list(range(NC))],
        ins=[ag4_in2.ap().opt()], outs=[ag4_out2.ap().opt()]), inc="C")
    wC4 = p.cnt["C"]
    s_ctall = sb("s_ctall", [128, CH * NC])
    p.emit("gpsimd", lambda h: h.dma_start(
        out=s_ctall[:],
        in_=ag4_out2.ap().rearrange("r p c -> p r c"),
        allow_slow_non_contiguous=True),
        wait=[("C", wC4)], inc="G", inc_by=16)
    wG4 = p.cnt["G"]

    s_carry = sb("s_carry", [128, CH])
    s_z8 = sb("s_z8", [128, NC])
    s_m8 = sb("s_m8", [128, 1])
    s_m8n = sb("s_m8n", [128, 1])
    s_se = sb("s_se", [128, 1])
    for k in range(CH):
        p.emit("vector", lambda h, k=k: h.tensor_tensor(
            s_z8[:], s_ctall[:].rearrange("p (r c) -> p r c", c=CH)[:, :, k],
            s_cmask[:], op=A.add),
            wait=[("G", wG4)] if k == 0 else [])
        p.drain("vector")
        p.emit("vector", lambda h: h.tensor_reduce(s_m8[:], s_z8[:], axis=AX.X, op=A.max))
        p.drain("vector")
        p.emit("vector", lambda h: h.tensor_scalar(s_m8n[:], s_m8[:], -1.0, None, op0=A.mult), inc="V")
        p.drain("vector")
        p.emit("scalar", lambda h: h.activation(s_z8[:], s_z8[:], ACT.Exp,
                                                bias=s_m8n[:, 0:1], scale=1.0,
                                                accum_out=s_se[:, 0:1]),
               wait=[("V", p.cnt["V"])], inc="S")
        p.drain("scalar")
        p.emit("scalar", lambda h: h.activation(s_se[:], s_se[:], ACT.Ln), inc="S")
        p.drain("scalar")
        p.emit("vector", lambda h, k=k: h.tensor_tensor(s_carry[:, k:k + 1], s_m8[:], s_se[:], op=A.add),
               wait=[("S", p.cnt["S"])], inc="V")
        p.drain("vector")
    w_carry = p.cnt["V"]
    t_ca = tap("carry", [128, CH])
    if t_ca is not None:
        p.emit("gpsimd", lambda h: h.dma_start(out=t_ca.ap(), in_=s_carry[:]),
               wait=[("V", w_carry)], inc="G", inc_by=16)
    t_as = tap("astar", [D, TOK])
    if t_as is not None:
        p.emit("gpsimd", lambda h: h.dma_start(out=t_as.ap(), in_=a_spill.ap()),
               wait=[("V", w_carry)], inc="G", inc_by=16)

    s_ar0 = s_lth[:, 0:512]
    s_ar1 = s_lth[:, 512:1024]
    s_sx = sb("s_sx2", [128, CH])
    s_sq2 = sb("s_sq2", [128, CH])
    s_c0 = sb("s_c0", [128, 1])
    s_c0q = sb("s_c0q", [128, 1])
    s_sqs = pool2

    for k in range(CH):
        wlda = [("V", w_carry)] if k == 0 else []
        la = p.emit("sync", lambda h, k=k: h.dma_start(
            out=s_ar0, in_=a_spill.ap()[k * 128:(k + 1) * 128, 0:512]),
            wait=wlda + [("V", p.count("V"))], inc="L", inc_by=16)
        p.emit("sync", lambda h, k=k: h.dma_start(
            out=s_ar1, in_=a_spill.ap()[k * 128:(k + 1) * 128, 512:1024]),
            inc="L", inc_by=16)
        wLa = p.cnt["L"]
        p.emit("vector", lambda h, k=k: h.tensor_scalar(
            s_u[:], clc(k), s_carry[:, k:k + 1], None, op0=A.subtract),
            wait=[("V", w_carry)] if k == 0 else [])
        p.emit("vector", lambda h, k=k: h.tensor_scalar(
            s_m[:], clc(k), s_carry[:, k:k + 1], None, op0=A.max), inc="V")
        p.drain("vector")
        p.emit("scalar", lambda h: h.activation(s_u[:], s_u[:], ACT.Abs),
               wait=[("V", p.cnt["V"])], inc="S")
        p.drain("scalar")
        p.emit("scalar", lambda h: h.activation(s_u[:], s_u[:], ACT.Exp, scale=-1.0), inc="S")
        p.drain("scalar")
        p.emit("scalar", lambda h: h.activation(s_u[:], s_u[:], ACT.Ln,
                                                bias=c_one[:, 0:1], scale=1.0), inc="S")
        wS = p.cnt["S"]
        p.emit("vector", lambda h, k=k: h.tensor_tensor(clc(k), s_m[:], s_u[:], op=A.add),
               wait=[("S", wS), ("L", wLa)])
        p.drain("vector")
        p.emit("vector", lambda h, k=k: h.tensor_tensor(
            clc(k)[:, 1:513], clc(k)[:, 1:513], s_ar0, op=A.add))
        p.emit("vector", lambda h, k=k: h.tensor_tensor(
            clc(k)[:, 513:1025], clc(k)[:, 513:1025], s_ar1, op=A.add), inc="V")
        p.drain("vector")
        p.emit("scalar", lambda h, k=k: h.activation(
            s_sqs[:, 0:TOK], clc(k)[:, 1:TOK1], ACT.Square),
            wait=[("V", p.cnt["V"])], inc="S")
        wq_ = p.cnt["S"]
        for w_ in (512, 256, 128, 64, 32, 16, 8, 4, 2, 1):
            p.emit("vector", lambda h, w_=w_: h.tensor_tensor(
                s_sqs[:, 0:w_], s_sqs[:, 0:w_], s_sqs[:, w_:2 * w_], op=A.add),
                wait=[("S", wq_)] if w_ == 512 else [])
            p.drain("vector")
        p.emit("vector", lambda h, k=k: h.tensor_copy(s_sq2[:, k:k + 1], s_sqs[:, 0:1]), inc="V")
        p.drain("vector")
        p.emit("vector", lambda h, k=k: h.tensor_copy(s_sqs[:, 0:TOK], clc(k)[:, 1:TOK1]))
        p.drain("vector")
        for w_ in (512, 256, 128, 64, 32, 16, 8, 4, 2, 1):
            p.emit("vector", lambda h, w_=w_: h.tensor_tensor(
                s_sqs[:, 0:w_], s_sqs[:, 0:w_], s_sqs[:, w_:2 * w_], op=A.add))
            p.drain("vector")
        p.emit("vector", lambda h, k=k: h.tensor_copy(s_sx[:, k:k + 1], s_sqs[:, 0:1]), inc="V")
        p.drain("vector")
        p.emit("vector", lambda h, k=k: h.tensor_scalar(
            s_tmp[:, 0:1], clc(k)[:, 0:1], -1e5, None, op0=A.max), inc="V")
        p.drain("vector")
        if k == 0:
            p.emit("vector", lambda h: h.tensor_copy(s_c0[:], s_tmp[:, 0:1]))
            p.emit("vector", lambda h: h.tensor_tensor(s_c0q[:], s_tmp[:, 0:1], s_tmp[:, 0:1], op=A.mult),
                   inc="V")
        else:
            p.emit("vector", lambda h: h.tensor_tensor(s_c0[:], s_c0[:], s_tmp[:, 0:1], op=A.add))
            p.drain("vector")
            p.emit("vector", lambda h: h.tensor_tensor(s_t1b[:], s_tmp[:, 0:1], s_tmp[:, 0:1], op=A.mult))
            p.drain("vector")
            p.emit("vector", lambda h: h.tensor_tensor(s_c0q[:], s_c0q[:], s_t1b[:], op=A.add),
                   inc="V")
        p.drain("vector")
    p.emit("vector", lambda h: h.tensor_reduce(s_tmp[:, 0:1], s_sx[:], axis=AX.X, op=A.add),
           wait=[("S", p.cnt["S"])])
    p.emit("vector", lambda h: h.tensor_reduce(s_t1b[:], s_sq2[:], axis=AX.X, op=A.add))
    p.drain("vector")
    p.emit("vector", lambda h: h.tensor_tensor(s_c0[:], s_c0[:], s_wsel[:], op=A.mult))
    p.emit("vector", lambda h: h.tensor_tensor(s_c0q[:], s_c0q[:], s_wsel[:], op=A.mult))
    p.drain("vector")
    p.emit("vector", lambda h: h.tensor_tensor(s_tmp[:, 0:1], s_tmp[:, 0:1], s_c0[:], op=A.add))
    p.emit("vector", lambda h: h.tensor_tensor(s_t1b[:], s_t1b[:], s_c0q[:], op=A.add))
    p.drain("vector")
    s_st2 = sb("s_st2", [128, 2])
    s_st2h = sb("s_st2h", [128, 2], BF16)
    s_st2l = sb("s_st2l", [128, 2], BF16)
    p.emit("vector", lambda h: h.tensor_copy(s_st2[:, 0:1], s_tmp[:, 0:1]))
    p.emit("vector", lambda h: h.tensor_copy(s_st2[:, 1:2], s_t1b[:]))
    p.drain("vector")
    p.emit("vector", lambda h: h.tensor_copy(s_st2h[:], s_st2[:]))
    p.drain("vector")
    p.emit("vector", lambda h: h.tensor_tensor(s_st2l[:], s_st2[:], s_st2h[:], op=A.subtract),
           inc="V")
    p.drain("vector")
    ps_g = ps_ssq
    p.emit("tensor", lambda h: h.matmul(ps_g[0:1, 0:2], c_onesb[:, 0:1], s_st2h[:], start=True, stop=False),
           wait=[("V", p.cnt["V"])])
    p.emit("tensor", lambda h: h.matmul(ps_g[0:1, 0:2], c_onesb[:, 0:1], s_st2l[:], start=False, stop=True),
           inc="T")
    s_g8 = sb("s_g8", [1, 8])
    p.emit("vector", lambda h: h.memset(s_g8[:], 0.0), wait=[("T", p.cnt["T"])])
    p.drain("vector")
    p.emit("vector", lambda h: h.tensor_copy(s_g8[:, 0:2], ps_g[0:1, 0:2]), inc="V")
    p.drain("vector")
    ar5b_in = nc.dram_tensor("ar5b_in", [1, 8], F32)
    ar5b_out = nc.dram_tensor("ar5b_out", [1, 8], F32)
    p.emit("gpsimd", lambda h: h.dma_start(out=ar5b_in.ap(), in_=s_g8[:]),
           wait=[("V", p.cnt["V"])], inc="G", inc_by=16)
    p.wait_only("gpsimd", [("G", p.cnt["G"])])
    p.emit("gpsimd", lambda h: h.collective_compute(
        "AllReduce", A.add, replica_groups=[list(range(NC))],
        ins=[ar5b_in.ap().opt()], outs=[ar5b_out.ap().opt()]), inc="C")
    p.emit("gpsimd", lambda h: h.dma_start(out=s_g8[:], in_=ar5b_out.ap()),
           wait=[("C", p.cnt["C"])], inc="G", inc_by=16)
    wG5 = p.cnt["G"]
    t_g8 = tap("stats8", [1, 8])
    if t_g8 is not None:
        p.emit("gpsimd", lambda h: h.dma_start(out=t_g8.ap(), in_=s_g8[:]),
               wait=[("G", wG5)], inc="G", inc_by=16)
    s_mean = sb("s_mean", [1, 1])
    s_rst = sb("s_rst", [1, 1])
    s_ngb = sb("s_ngb", [1, 1])
    p.emit("vector", lambda h: h.tensor_scalar(s_mean[:], s_g8[:, 0:1], 1.0 / NTOT, None, op0=A.mult),
           wait=[("G", wG5)])
    p.drain("vector")
    p.emit("vector", lambda h: h.tensor_tensor(s_rst[:], s_g8[:, 0:1], s_mean[:], op=A.mult))
    p.drain("vector")
    p.emit("vector", lambda h: h.tensor_tensor(s_rst[:], s_g8[:, 1:2], s_rst[:], op=A.subtract))
    p.drain("vector")
    p.emit("vector", lambda h: h.tensor_scalar(s_rst[:], s_rst[:], 1.0 / (NTOT - 1.0), None, op0=A.mult),
           inc="V")
    p.drain("vector")
    p.emit("scalar", lambda h: h.activation(s_rst[:], s_rst[:], ACT.Sqrt),
           wait=[("V", p.cnt["V"])], inc="S")
    p.emit("vector", lambda h: h.reciprocal(s_rst[:], s_rst[:]),
           wait=[("S", p.cnt["S"])], inc="V")
    p.drain("vector")
    p.emit("vector", lambda h: h.tensor_tensor(s_ngb[:], s_mean[:], s_rst[:], op=A.mult))
    p.drain("vector")
    p.emit("vector", lambda h: h.tensor_scalar(s_ngb[:], s_ngb[:], -1.0, None, op0=A.mult),
           inc="V")
    p.drain("vector")
    s_rstb = sb("s_rstb", [128, 1])
    s_ngbb = sb("s_ngbb", [128, 1])
    p.emit("gpsimd", lambda h: h.dma_start(out=bc_scr.ap()[2:3, 0:1], in_=s_rst[:]),
           wait=[("V", p.cnt["V"])], inc="G", inc_by=16)
    p.emit("gpsimd", lambda h: h.dma_start(out=bc_scr.ap()[3:4, 0:1], in_=s_ngb[:]),
           inc="G", inc_by=16)
    p.wait_only("gpsimd", [("G", p.cnt["G"])])
    p.emit("gpsimd", lambda h: h.dma_start(
        out=s_rstb[:], in_=bc_scr.ap()[2:3, 0:1].broadcast_to([128, 1])), inc="G", inc_by=16)
    p.emit("gpsimd", lambda h: h.dma_start(
        out=s_ngbb[:], in_=bc_scr.ap()[3:4, 0:1].broadcast_to([128, 1])), inc="G", inc_by=16)
    wGb = p.cnt["G"]

    for k in range(CH):
        p.emit("scalar", lambda h, k=k: h.activation(
            clc(k)[:, 1:TOK1], clc(k)[:, 1:TOK1], ACT.Exp,
            bias=s_ngbb[:, 0:1], scale=s_rstb[:, 0:1]),
            wait=[("G", wGb)] if k == 0 else [], inc="S")
    p.drain("scalar")
    wS_h = p.cnt["S"]
    for k in range(CH):
        p.emit("vector", lambda h, k=k: h.tensor_tensor(s_tmp[:], xnh(k), xnl(k), op=A.add),
               wait=[("S", wS_h)] if k == 0 else [])
        p.drain("vector")
        p.emit("vector", lambda h, k=k: h.tensor_tensor(
            clc(k)[:, 1:TOK1], clc(k)[:, 1:TOK1], s_tmp[:], op=A.add), inc="V")
        p.drain("vector")
    w_xa = p.cnt["V"]

    w_s = emit_ln_fm(lambda k: clc(k)[:, 1:TOK1], lambda k: clc(k)[:, 1:TOK1],
                     ("V", w_xa), c_eps5[0:1, 0:1],
                     gam_col=lambda k: s_lnw[:, k:k + 1],
                     beta_col=lambda k: s_lnb[:, k:k + 1])

    t_s0 = tap("s0", [D, TOK])
    for k in range(CH):
        p.emit("gpsimd", lambda h, k=k: h.dma_start(
            out=s_spill.ap()[k * 128:(k + 1) * 128, :], in_=clc(k)[:, 1:TOK1]),
            wait=[("V", w_s)] if k == 0 else [], inc="G", inc_by=16)
        if t_s0 is not None:
            p.emit("gpsimd", lambda h, k=k: h.dma_start(
                out=t_s0.ap()[k * 128:(k + 1) * 128, :], in_=clc(k)[:, 1:TOK1]),
                inc="G", inc_by=16)

    w_sspill = p.cnt["G"]

    gwN = inp("gwN", [1024, D])
    upN = inp("upN", [1024, D])
    dwN = inp("dwN", [256, DFF])
    x1_spill = nc.dram_tensor("x1_spill", [D, TOK], F32)
    GU = 2 * 1024 * D
    DW = 256 * DFF
    GOFF, UOFF = 0, 1024 * D
    ag2g_in = nc.dram_tensor("ag2g_in", [2 * GU], BF16)
    ag2g_out = nc.dram_tensor("ag2g_out", [NC, 2 * GU], BF16)
    ag2d_in = nc.dram_tensor("ag2d_in", [2 * DW], BF16)
    ag2d_out = nc.dram_tensor("ag2d_out", [NC, 2 * DW], BF16)

    free_back_to("s_h0")
    _bar = [("V", p.count("V")), ("S", p.count("S")), ("T", p.count("T")),
            ("G", p.count("G")), ("L", p.count("L"))]
    for _e in ENGINES:
        p.wait_only(_e, _bar)

    dqx_sp = nc.dram_tensor("dqx_sp", [2, D, TOK], BF16)
    s_dqxh = sb("s_dqxh", [128, CH * 512], BF16)
    s_dqxl = sb("s_dqxl", [128, CH * 512], BF16)
    s_dacc = sb("s_dacc", [128, CH * 512])

    def dqxh(kk):
        return s_dqxh[:, kk * 512:(kk + 1) * 512]

    def dqxl(kk):
        return s_dqxl[:, kk * 512:(kk + 1) * 512]

    def dacc(o):
        return s_dacc[:, o * 512:(o + 1) * 512]

    QW = 1024
    qs_src = sb("qs_src", [128, 1024])
    qs_t = sb("qs_t", [128, QW])
    qs_sq = sb("qs_sq", [128, QW])
    qs_r = sb("qs_r", [128, QW])
    qs_bm = sb("qs_bm", [128, QW // 2])
    qs_sc = sb("qs_sc", [128, 2 * (QW // 32)])
    qs_ev = sb("qs_ev", [128, 128])
    ps_tr = ps("ps_tr", [128, 128])
    ps_gu = ps("ps_gu", [128, 512])
    ps_d0 = ps("ps_d0", [128, 512])
    ps_d1 = ps("ps_d1", [128, 512])

    def emit_mxq(src, dst, W, wait=()):
        nb = W // 32
        p.emit("scalar", lambda h: h.activation(qs_sq[:, 0:W], src, ACT.Abs),
               wait=list(wait), inc="S")
        p.emit("scalar", lambda h: h.activation(qs_r[:, 0:W], src, ACT.Sign),
               inc="S")
        wS0 = p.cnt["S"]
        a3 = qs_sq[:, 0:W].rearrange("p (b q) -> p b q", q=32)
        b3 = qs_bm[:, 0:nb * 16].rearrange("p (b q) -> p b q", q=16)
        p.emit("vector", lambda h: h.tensor_tensor(b3[:, :, 0:16], a3[:, :, 0:16], a3[:, :, 16:32], op=A.max),
               wait=[("S", wS0)])
        p.drain("vector")
        p.emit("vector", lambda h: h.tensor_tensor(b3[:, :, 0:8], b3[:, :, 0:8], b3[:, :, 8:16], op=A.max))
        p.drain("vector")
        p.emit("vector", lambda h: h.tensor_tensor(b3[:, :, 0:4], b3[:, :, 0:4], b3[:, :, 4:8], op=A.max))
        p.drain("vector")
        p.emit("vector", lambda h: h.tensor_tensor(b3[:, :, 0:2], b3[:, :, 0:2], b3[:, :, 2:4], op=A.max))
        p.drain("vector")
        p.emit("vector", lambda h: h.tensor_tensor(b3[:, :, 0:1], b3[:, :, 0:1], b3[:, :, 1:2], op=A.max))
        p.drain("vector")
        p.emit("vector", lambda h: h.tensor_scalar(
            qs_sc[:, 0:nb], b3[:, :, 0:1], 1.0 / 448.0, 1e-30, op0=A.mult, op1=A.max))
        p.drain("vector")
        scb = qs_sc[:, 0:nb].rearrange("p (b o) -> p b o", o=1).broadcast_to([128, nb, 32])
        rscb = qs_sc[:, nb:2 * nb].rearrange("p (b o) -> p b o", o=1).broadcast_to([128, nb, 32])
        r3 = qs_r[:, 0:W].rearrange("p (b q) -> p b q", q=32)
        p.emit("vector", lambda h: h.reciprocal(qs_sc[:, nb:2 * nb], qs_sc[:, 0:nb]))
        p.emit("vector", lambda h: h.tensor_tensor(r3, r3, scb, op=A.mult))
        p.drain("vector")
        p.emit("vector", lambda h: h.tensor_tensor(
            qs_sq[:, 0:W].rearrange("p (b q) -> p b q", q=32), a3, rscb, op=A.mult))
        p.drain("vector")
        p.emit("vector", lambda h: h.tensor_scalar(
            qs_t[:, 0:W].bitcast(U32), qs_sq[:, 0:W].bitcast(U32), 0x7F800000, None,
            op0=A.bitwise_and))
        p.drain("vector")
        p.emit("vector", lambda h: h.tensor_scalar(qs_t[:, 0:W], qs_t[:, 0:W], 2.0 ** 20, 2.0 ** 14,
                                                   op0=A.mult, op1=A.max))
        p.drain("vector")
        p.emit("vector", lambda h: h.tensor_tensor(
            qs_sq[:, 0:W], qs_sq[:, 0:W], qs_t[:, 0:W], op=A.add))
        p.drain("vector")
        p.emit("vector", lambda h: h.tensor_tensor(
            qs_sq[:, 0:W], qs_sq[:, 0:W], qs_t[:, 0:W], op=A.subtract))
        p.drain("vector")
        p.emit("vector", lambda h: h.tensor_tensor(dst, qs_sq[:, 0:W], qs_r[:, 0:W], op=A.mult),
               inc="V")
        p.drain("vector")
        return p.cnt["V"]

    qs_evb = sb("qs_evb", [128, 128], BF16)
    qs_evlb = sb("qs_evlb", [128, 128], BF16)

    def stage_weight(src_dram, rows, width, base_off, out_rowlen, dst, half):
        ntile = rows // 128
        for ti in range(ntile):
            for sub in range(width // QW):
                p.emit("sync", lambda h, ti=ti, sub=sub: h.dma_start(
                    out=qs_src[:, 0:QW],
                    in_=src_dram.ap()[ti * 128:(ti + 1) * 128, sub * QW:(sub + 1) * QW]),
                    wait=[("V", p.count("V")), ("T", p.count("T"))], inc="L", inc_by=16)
                emit_mxq(qs_src[:, 0:QW], qs_src[:, 0:QW], QW,
                         wait=[("L", p.cnt["L"])])
                for tj in range(QW // 128):
                    p.emit("tensor", lambda h, tj=tj: h.transpose(
                        ps_tr[:], qs_src[:, tj * 128:(tj + 1) * 128], c_ident[:]),
                        wait=[("V", p.count("V")), ("S", p.count("S"))], inc="T")
                    p.emit("scalar", lambda h: h.copy(qs_evb[:], ps_tr[:]),
                           wait=[("T", p.cnt["T"]), ("G", p.count("G"))], inc="S")
                    p.emit("vector", lambda h: h.tensor_tensor(
                        qs_evlb[:], ps_tr[:], qs_evb[:], op=A.subtract),
                        wait=[("S", p.cnt["S"]), ("G", p.count("G"))], inc="V")
                    row = (base_off + (sub * QW + tj * 128) * out_rowlen) // out_rowlen
                    p.emit("gpsimd", lambda h, row=row, ti=ti, rl=out_rowlen, dst=dst, half=half: h.dma_start(
                        out=dst.ap()[0:half].rearrange("(a b) -> a b", b=rl)[
                            row:row + 128, ti * 128:ti * 128 + 128],
                        in_=qs_evb[:]),
                        wait=[("S", p.cnt["S"])], inc="G", inc_by=16)
                    p.emit("gpsimd", lambda h, row=row, ti=ti, rl=out_rowlen, dst=dst, half=half: h.dma_start(
                        out=dst.ap()[half:2 * half].rearrange("(a b) -> a b", b=rl)[
                            row:row + 128, ti * 128:ti * 128 + 128],
                        in_=qs_evlb[:]),
                        wait=[("V", p.cnt["V"])], inc="G", inc_by=16)

    stage_weight(gwN, 1024, D, GOFF, 1024, ag2g_in, GU)
    stage_weight(upN, 1024, D, UOFF, 1024, ag2g_in, GU)
    wG2a = p.cnt["G"]
    p.wait_only("gpsimd", [("G", wG2a)])
    p.emit("gpsimd", lambda h: h.collective_compute(
        "AllGather", A.bypass, replica_groups=[list(range(NC))],
        ins=[ag2g_in.ap().opt()], outs=[ag2g_out.ap().opt()]), inc="C")
    wC2g = p.cnt["C"]
    stage_weight(dwN, 256, DFF, 0, 256, ag2d_in, DW)
    wG2b = p.cnt["G"]
    p.wait_only("gpsimd", [("G", wG2b)])
    p.emit("gpsimd", lambda h: h.collective_compute(
        "AllGather", A.bypass, replica_groups=[list(range(NC))],
        ins=[ag2d_in.ap().opt()], outs=[ag2d_out.ap().opt()]), inc="C")
    wC2d = p.cnt["C"]

    s_xtm = sb("s_xtm", [128, D])
    s_xnhb = sb("s_xnhb", [128, 2048], BF16)
    s_xnlb = sb("s_xnlb", [128, 2048], BF16)
    for tc in range(TOK // 128):
        p.emit("sync", lambda h, tc=tc: h.dma_start(
            out=s_xnhb[:], in_=xnh_sp.ap().rearrange("(p c) t -> p c t", p=128)[
                :, :, tc * 128:(tc + 1) * 128]),
            wait=[("T", p.count("T")), ("V", p.count("V")), ("S", p.count("S")),
                  ("G", p.count("G"))],
            inc="L", inc_by=16)
        p.emit("sync", lambda h, tc=tc: h.dma_start(
            out=s_xnlb[:], in_=xnl_sp.ap().rearrange("(p c) t -> p c t", p=128)[
                :, :, tc * 128:(tc + 1) * 128]),
            inc="L", inc_by=16)
        wLx = p.cnt["L"]
        for hf2 in range(2):
            p.emit("vector", lambda h, hf2=hf2: h.tensor_tensor(
                qs_src[:], s_xnhb[:, hf2 * 1024:(hf2 + 1) * 1024],
                s_xnlb[:, hf2 * 1024:(hf2 + 1) * 1024], op=A.add),
                wait=[("L", wLx), ("T", p.count("T"))], inc="V")
            p.drain("vector")
            for kk2 in range(8):
                kk = hf2 * 8 + kk2
                p.emit("tensor", lambda h, kk2=kk2: h.transpose(
                    ps_tr[:], qs_src[:, kk2 * 128:(kk2 + 1) * 128], c_ident[:]),
                    wait=[("V", p.count("V"))], inc="T")
                p.emit("vector", lambda h, kk=kk: h.tensor_copy(s_xtm[:, kk * 128:(kk + 1) * 128], ps_tr[:]),
                       wait=[("T", p.cnt["T"])], inc="V")
        for sub in range(D // QW):
            emit_mxq(s_xtm[:, sub * QW:(sub + 1) * QW], s_xtm[:, sub * QW:(sub + 1) * QW], QW,
                     wait=[("V", p.count("V"))])
        wq = p.cnt["V"]
        for kk in range(CH):
            p.emit("tensor", lambda h, kk=kk: h.transpose(
                ps_tr[:], s_xtm[:, kk * 128:(kk + 1) * 128], c_ident[:]),
                wait=[("V", p.count("V")), ("S", p.count("S")), ("G", p.count("G"))], inc="T")
            p.emit("scalar", lambda h, kk=kk: h.copy(s_xnhb[:, 0:128], ps_tr[:]),
                   wait=[("T", p.cnt["T"]), ("G", p.count("G"))], inc="S")
            p.emit("vector", lambda h, kk=kk: h.tensor_tensor(
                s_xnlb[:, 0:128], ps_tr[:], s_xnhb[:, 0:128], op=A.subtract),
                wait=[("S", p.cnt["S"]), ("G", p.count("G"))], inc="V")
            p.emit("gpsimd", lambda h, kk=kk, tc=tc: h.dma_start(
                out=dqx_sp.ap()[0, kk * 128:(kk + 1) * 128, tc * 128:(tc + 1) * 128],
                in_=s_xnhb[:, 0:128]),
                wait=[("S", p.cnt["S"])], inc="G", inc_by=16)
            p.emit("gpsimd", lambda h, kk=kk, tc=tc: h.dma_start(
                out=dqx_sp.ap()[1, kk * 128:(kk + 1) * 128, tc * 128:(tc + 1) * 128],
                in_=s_xnlb[:, 0:128]),
                inc="G", inc_by=16)
    w_dqx = p.cnt["G"]
    t_dqx = tap("dqx", [D, TOK])
    if t_dqx is not None:
        for kk in range(CH):
            p.emit("sync", lambda h, kk=kk: h.dma_start(
                out=s_xnhb[:, 0:TOK], in_=dqx_sp.ap()[0, kk * 128:(kk + 1) * 128, :]),
                wait=[("G", w_dqx), ("V", p.count("V"))], inc="L", inc_by=16)
            p.emit("sync", lambda h, kk=kk: h.dma_start(
                out=s_xnlb[:, 0:TOK], in_=dqx_sp.ap()[1, kk * 128:(kk + 1) * 128, :]),
                inc="L", inc_by=16)
            p.emit("vector", lambda h: h.tensor_tensor(
                s_xtm[:, 0:TOK], s_xnhb[:, 0:TOK], s_xnlb[:, 0:TOK], op=A.add),
                wait=[("L", p.cnt["L"])], inc="V")
            p.drain("vector")
            p.emit("gpsimd", lambda h, kk=kk: h.dma_start(
                out=t_dqx.ap()[kk * 128:(kk + 1) * 128, :], in_=s_xtm[:, 0:TOK]),
                wait=[("V", p.cnt["V"])], inc="G", inc_by=16)
            p.wait_only("vector", [("G", p.cnt["G"])])
    t_gw = tap("dqgw0", [128, 1024])
    if t_gw is not None:
        p.emit("sync", lambda h: h.dma_start(
            out=s_xnhb[:, 0:1024],
            in_=ag2g_out.ap()[0, 0:128 * 1024].rearrange("(a b) -> a b", b=1024)),
            wait=[("C", wC2g), ("G", p.count("G")), ("S", p.count("S")),
                  ("V", p.count("V"))], inc="L", inc_by=16)
        p.emit("sync", lambda h: h.dma_start(
            out=s_xnlb[:, 0:1024],
            in_=ag2g_out.ap()[0, GU:GU + 128 * 1024].rearrange("(a b) -> a b", b=1024)),
            inc="L", inc_by=16)
        p.emit("vector", lambda h: h.tensor_tensor(
            s_xtm[:, 0:1024], s_xnhb[:, 0:1024], s_xnlb[:, 0:1024], op=A.add),
            wait=[("L", p.cnt["L"])], inc="V")
        p.drain("vector")
        p.emit("gpsimd", lambda h: h.dma_start(
            out=t_gw.ap(), in_=s_xtm[:, 0:1024]),
            wait=[("V", p.cnt["V"])], inc="G", inc_by=16)
        p.wait_only("vector", [("G", p.cnt["G"])])
    sbfree("s_xnlb")
    sbfree("s_xnhb")
    sbfree("s_xtm")
    _bar2 = [("V", p.count("V")), ("T", p.count("T")), ("L", p.count("L"))]
    for _e in ENGINES:
        p.wait_only(_e, _bar2)

    JG = 256
    NJG = DFF // JG
    s_wg = [sb("s_wg0", [128, CH * 2 * JG], BF16), sb("s_wg1", [128, CH * 2 * JG], BF16)]
    s_wgl = [sb("s_wgl0", [128, CH * 2 * JG], BF16), sb("s_wgl1", [128, CH * 2 * JG], BF16)]
    s_wd = [sb("s_wd0", [128, 2 * D], BF16), sb("s_wd1", [128, 2 * D], BF16)]
    s_wdl = [sb("s_wdl0", [128, 2 * D], BF16), sb("s_wdl1", [128, 2 * D], BF16)]
    s_h = qs_src
    s_dqfh = sb("s_dqfh", [128, 2 * 512], BF16)
    s_dqfl = sb("s_dqfl", [128, 2 * 512], BF16)
    ps_gu2 = ps("ps_gu2", [128, 512])
    t_g0 = tap("g0", [D, TOK])
    t_x1 = tap("x1", [D, TOK])
    s_xs = sb("s_xs", [128, 512])
    s_ss = sb("s_ss", [128, 512])

    def emit_wload(jg, buf, first_wait):
        slot = (jg * JG) // 1024
        col = (jg * JG) % 1024
        for hl, (wgt, wdt) in ((0, (s_wg, s_wd)), (1, (s_wgl, s_wdl))):
            hoff = hl * GU
            for gu, boff in ((0, GOFF), (1, UOFF)):
                src = ag2g_out.ap()[slot, hoff + boff:hoff + boff + CH * 128 * 1024].rearrange(
                    "(k q c) -> q k c", q=128, c=1024)[:, :, col:col + JG]
                dst = wgt[buf][:, :].rearrange("q (k c) -> q k c", c=2 * JG)[
                    :, :, gu * JG:(gu + 1) * JG]
                p.emit("sync", lambda h, src=src, dst=dst: h.dma_start(out=dst, in_=src),
                       wait=(list(first_wait) if (hl == 0 and gu == 0) else []),
                       inc="L", inc_by=16)
            for jc in range(2):
                roff = hl * DW + (jg * JG + jc * 128) * 256
                src = ag2d_out.ap()[:, roff:roff + 128 * 256].rearrange(
                    "s (q c) -> q s c", c=256)
                dst = wdt[buf][:, jc * D:(jc + 1) * D].rearrange("q (o c) -> q o c", c=256)
                p.emit("sync", lambda h, src=src, dst=dst: h.dma_start(out=dst, in_=src),
                       inc="L", inc_by=16)
        return p.cnt["L"]

    p.wait_only("sync", [("C", wC2d), ("G", w_dqx)])
    w_wld_map = {}
    for th in range(2):
        for kk in range(CH):
            p.emit("sync", lambda h, kk=kk, th=th: h.dma_start(
                out=dqxh(kk), in_=dqx_sp.ap()[0, kk * 128:(kk + 1) * 128, th * 512:(th + 1) * 512]),
                wait=[("T", p.count("T"))] if kk == 0 else [], inc="L", inc_by=16)
            p.emit("sync", lambda h, kk=kk, th=th: h.dma_start(
                out=dqxl(kk), in_=dqx_sp.ap()[1, kk * 128:(kk + 1) * 128, th * 512:(th + 1) * 512]),
                inc="L", inc_by=16)
        wLdq = p.cnt["L"]
        for o in range(CH):
            p.emit("vector", lambda h, o=o: h.memset(dacc(o), 0.0),
                   wait=[("S", p.count("S")), ("G", p.count("G"))] if o == 0 else [],
                   inc="V" if o == CH - 1 else None)
        p.drain("vector")

        for jg in range(NJG):
            buf = jg % 2
            if th == 0 and jg < 2:
                w_wld_map[(0, jg)] = emit_wload(jg, buf, [])
            w_wld = w_wld_map.get((th, jg), p.cnt["L"])
            for tci in range(4):
                pg = ps_gu if tci % 2 == 0 else ps_gu2
                first = True
                for pi, (xs_, ws_) in enumerate(((0, 0), (0, 1), (1, 0))):
                    for kk in range(CH):
                        st = (pi == 0 and kk == 0)
                        sp = (pi == 2 and kk == CH - 1)
                        xop = (dqxh if xs_ == 0 else dqxl)(kk)[:, tci * 128:(tci + 1) * 128]
                        wop = (s_wg[buf] if ws_ == 0 else s_wgl[buf])[:, kk * 2 * JG:(kk + 1) * 2 * JG]
                        wt = ([("L", w_wld), ("V", p.count("V")), ("S", p.count("S")), ("L", wLdq)]
                              if first else [])
                        first = False
                        p.emit("tensor", lambda h, xop=xop, wop=wop, st=st, sp=sp, pg=pg: h.matmul(
                            pg[:], xop, wop, start=st, stop=sp),
                            wait=wt, inc="T" if sp else None)
                wTgu = p.cnt["T"]
                hsl = slice(tci * JG, (tci + 1) * JG)
                p.emit("scalar", lambda h, hsl=hsl, pg=pg: h.activation(
                    s_h[:, hsl], pg[:, 0:JG], ACT.Silu),
                    wait=[("T", wTgu)], inc="S")
                p.emit("vector", lambda h, hsl=hsl, pg=pg: h.tensor_tensor(
                    s_h[:, hsl], s_h[:, hsl], pg[:, JG:2 * JG], op=A.mult),
                    wait=[("S", p.cnt["S"])], inc="V")
            emit_mxq(s_h[:], s_h[:], 4 * JG, wait=[("V", p.count("V"))])
            for tci in range(4):
                tcol = tci * 128
                for jc in range(2):
                    p.emit("tensor", lambda h, tci=tci, jc=jc: h.transpose(
                        ps_tr[:], s_h[:, tci * JG + jc * 128:tci * JG + (jc + 1) * 128], c_ident[:]),
                        wait=[("V", p.count("V")), ("S", p.count("S"))], inc="T")
                    p.emit("scalar", lambda h, jc=jc, tcol=tcol: h.copy(
                        s_dqfh[:, jc * 512 + tcol:jc * 512 + tcol + 128], ps_tr[:]),
                        wait=[("T", p.cnt["T"])], inc="S")
                    p.emit("vector", lambda h, jc=jc, tcol=tcol: h.tensor_tensor(
                        s_dqfl[:, jc * 512 + tcol:jc * 512 + tcol + 128], ps_tr[:],
                        s_dqfh[:, jc * 512 + tcol:jc * 512 + tcol + 128], op=A.subtract),
                        wait=[("S", p.cnt["S"])], inc="V")
            w_dqf = p.cnt["V"]
            w_dqfS = p.cnt["S"]
            for o in range(CH):
                pd = ps_d0 if o % 2 == 0 else ps_d1
                first = True
                for pi, (hs_, ws_) in enumerate(((0, 0), (0, 1), (1, 0))):
                    for jc in range(2):
                        st = (pi == 0 and jc == 0)
                        sp = (pi == 2 and jc == 1)
                        hop = (s_dqfh if hs_ == 0 else s_dqfl)[:, jc * 512:(jc + 1) * 512]
                        wop = (s_wd[buf] if ws_ == 0 else s_wdl[buf])[:, jc * D + o * 128:jc * D + (o + 1) * 128]
                        wt = ([("V", w_dqf), ("S", w_dqfS)] if first else [])
                        if o >= 2 and first:
                            wt.append(("V", p.count("V")))
                        first = False
                        p.emit("tensor", lambda h, hop=hop, wop=wop, st=st, sp=sp, pd=pd: h.matmul(
                            pd[:], wop, hop, start=st, stop=sp),
                            wait=wt, inc="T" if sp else None)
                wTd = p.cnt["T"]
                p.emit("vector", lambda h, o=o, pd=pd: h.tensor_tensor(
                    dacc(o), dacc(o), pd[:], op=A.add),
                    wait=[("T", wTd)], inc="V")
            p.drain("vector")
            nxt = jg + 2
            if th == 0 and nxt < NJG:
                w_wld_map[(0, nxt)] = emit_wload(nxt, nxt % 2, [("T", p.count("T"))])
            elif th == 0 and nxt - NJG < 2:
                w_wld_map[(1, nxt - NJG)] = emit_wload(nxt - NJG, (nxt - NJG) % 2,
                                                       [("T", p.count("T"))])
            elif th == 1 and nxt < NJG:
                w_wld_map[(1, nxt)] = emit_wload(nxt, nxt % 2, [("T", p.count("T"))])
        for k in range(CH):
            p.emit("scalar", lambda h, k=k: h.activation(dacc(k), dacc(k), ACT.Silu),
                   wait=[("V", p.count("V"))] if k == 0 else [], inc="S")
            if t_g0 is not None:
                p.emit("gpsimd", lambda h, k=k, th=th: h.dma_start(
                    out=t_g0.ap()[k * 128:(k + 1) * 128, th * 512:(th + 1) * 512], in_=dacc(k)),
                    wait=[("S", p.cnt["S"])], inc="G", inc_by=16)
            p.emit("sync", lambda h, k=k, th=th: h.dma_start(
                out=s_xs[:], in_=x_fm.ap()[k * 128:(k + 1) * 128, th * 512:(th + 1) * 512]),
                wait=[("V", p.count("V"))], inc="L", inc_by=16)
            p.emit("sync", lambda h, k=k, th=th: h.dma_start(
                out=s_ss[:], in_=s_spill.ap()[k * 128:(k + 1) * 128, th * 512:(th + 1) * 512]),
                wait=[("G", w_sspill), ("V", p.count("V"))], inc="L", inc_by=16)
            wLs = p.cnt["L"]
            p.emit("vector", lambda h, k=k: h.tensor_tensor(dacc(k), dacc(k), s_ss[:], op=A.mult),
                   wait=[("L", wLs), ("S", p.cnt["S"]), ("G", p.cnt["G"])])
            p.drain("vector")
            p.emit("vector", lambda h, k=k: h.tensor_tensor(dacc(k), dacc(k), s_xs[:], op=A.add),
                   inc="V")
            p.emit("gpsimd", lambda h, k=k, th=th: h.dma_start(
                out=x1_spill.ap()[k * 128:(k + 1) * 128, th * 512:(th + 1) * 512], in_=dacc(k)),
                wait=[("V", p.cnt["V"])], inc="G", inc_by=16)
            if t_x1 is not None:
                p.emit("gpsimd", lambda h, k=k, th=th: h.dma_start(
                    out=t_x1.ap()[k * 128:(k + 1) * 128, th * 512:(th + 1) * 512], in_=dacc(k)),
                    inc="G", inc_by=16)
    w_ffn = p.cnt["V"]

    free_back_to("s_wg0")
    _bar3 = [("V", p.count("V")), ("S", p.count("S")), ("T", p.count("T")),
             ("G", p.count("G")), ("L", p.count("L"))]
    for _e in ENGINES:
        p.wait_only(_e, _bar3)
    w_x1sp = p.cnt["G"]

    wqN = inp("wqN", [256, D])
    wkN = inp("wkN", [256, D])
    wvN = inp("wvN", [256, D])
    woN = inp("woN", [256, D])
    wbT = inp("wbT", [D, D])
    bb1 = inp("bb1", [128, CH])
    wrT = inp("wrT", [D, 8])
    aaT = inp("aaT", [D, 64])
    ball_i = inp("ball", [64, D])
    ropeC = inp("ropeC", [128, TOK])
    ropeS = inp("ropeS", [128, TOK])
    dmask = inp("dmask", [128, 4 * 512])
    g1_spill = nc.dram_tensor("g1_spill", [D, TOK], F32)
    NW = 2048 * 256
    ag3_in = nc.dram_tensor("ag3_in", [4 * NW], BF16)
    ag3_out = nc.dram_tensor("ag3_out", [NC, 4 * NW], BF16)
    ar1_in = nc.dram_tensor("ar1_in", [1, 8], F32)
    ar1_out = nc.dram_tensor("ar1_out", [1, 8], F32)
    a2a6_in = nc.dram_tensor("a2a6_in", [NC, 3, 2, 128, TOK], F32)
    a2a6_out = nc.dram_tensor("a2a6_out", [NC, 3, 2, 128, TOK], F32)
    a2a7_in = nc.dram_tensor("a2a7_in", [NC, 2, 128, TOK], F32)
    a2a7_out = nc.dram_tensor("a2a7_out", [NC, 2, 128, TOK], F32)
    rq_scr = nc.dram_tensor("rq_scr", [16, 128], F32)

    free_back_to("s_dqxh")
    _bar4 = [("V", p.count("V")), ("S", p.count("S")), ("T", p.count("T")),
             ("G", p.count("G")), ("L", p.count("L"))]
    for _e in ENGINES:
        p.wait_only(_e, _bar4)

    s_ifm = sb("s_ifm", [128, CH * TOK], BF16)

    def ifm(kk):
        return s_ifm[:, kk * TOK:(kk + 1) * TOK]

    s_t2k = sb("s_t2k", [128, D])
    s_w2k = sb("s_w2k", [128, D])
    s_a = sb("s_a", [128, D])
    s_b2 = sb("s_b2", [128, D])
    s_c2 = sb("s_c2", [128, D])
    s_cols = sb("s_cols", [128, 64])
    s_row = sb("s_row", [128, TOK])
    s_row2 = sb("s_row2", [128, TOK])
    s_row3 = sb("s_row3", [128, TOK])
    s_rowv = sb("s_rowv", [128, TOK])
    ps_tr2 = ps("ps_tr2", [128, 128])
    ps_m0 = ps("ps_m0", [128, 512])
    ps_m1 = ps("ps_m1", [128, 512])
    ps_sm = ps("ps_sm", [128, 64])
    ps_trb = ps("ps_trb", [128, 128], BF16)
    ps_y = ps("ps_y", [128, 128])
    c_e5 = sb("c_e5", [128, 1])
    s_swq = sb("s_swq", [128, 4])
    s_swm = sb("s_swm", [128, 4])
    s_rsx = sb("s_rsx", [128, 8])
    s_pj = sb("s_pj", [128, TOK])
    s_pj2 = sb("s_pj2", [128, TOK])
    s_wqkv = sb("s_wqkv", [128, CH * 128], BF16)
    s_ropeC = sb("s_ropeC", [128, TOK])
    s_ropeS = sb("s_ropeS", [128, TOK])
    s_g1t = s_pj2
    s_x1t = s_rowv

    s_x1h = sb("s_x1h", [128, CH * TOK], BF16)
    s_x1l = sb("s_x1l", [128, CH * TOK], BF16)

    def x1h(kk):
        return s_x1h[:, kk * TOK:(kk + 1) * TOK]

    def x1l(kk):
        return s_x1l[:, kk * TOK:(kk + 1) * TOK]

    s_evb = sb("s_evb", [128, 128], BF16)
    s_wbh = sb("s_wbh", [128, D], BF16)
    s_wbl = sb("s_wbl", [128, D], BF16)
    s_bb1 = sb("s_bb1", [128, CH])

    p.emit("vector", lambda h: h.memset(c_e5[:], 1e-5), inc="V")
    p.drain("vector")
    p.emit("sync", lambda h: h.dma_start(out=s_bb1[:], in_=bb1.ap()), inc="L", inc_by=16)
    w_bb1 = p.cnt["L"]

    for wi, wsrc in enumerate((wqN, wkN, wvN, woN)):
        for ti in range(2):
            p.emit("sync", lambda h, wsrc=wsrc, ti=ti: h.dma_start(
                out=s_w2k[:], in_=wsrc.ap()[ti * 128:(ti + 1) * 128, :]),
                wait=[("S", p.count("S")), ("V", p.count("V"))], inc="L", inc_by=16)
            p.emit("scalar", lambda h: h.activation(s_a[:], s_w2k[:], ACT.Abs),
                   wait=[("L", p.cnt["L"])], inc="S")
            wS_ = p.cnt["S"]
            for w_ in (1024, 512, 256, 128, 64, 32, 16, 8, 4, 2, 1):
                p.emit("vector", lambda h, w_=w_: h.tensor_tensor(
                    s_a[:, 0:w_], s_a[:, 0:w_], s_a[:, w_:2 * w_], op=A.add),
                    wait=[("S", wS_)] if w_ == 1024 else [])
                p.drain("vector")
            p.emit("vector", lambda h, wi=wi, ti=ti: h.tensor_copy(
                s_cols[:, (wi * 2 + ti):(wi * 2 + ti) + 1], s_a[:, 0:1]), inc="V")
            p.drain("vector")
    s_w8h = sb("s_w8h", [128, 8], BF16)
    s_w8l = sb("s_w8l", [128, 8], BF16)
    p.emit("vector", lambda h: h.tensor_copy(s_w8h[:], s_cols[:, 0:8]))
    p.drain("vector")
    p.emit("vector", lambda h: h.tensor_tensor(s_w8l[:], s_cols[:, 0:8], s_w8h[:], op=A.subtract),
           inc="V")
    p.drain("vector")
    p.emit("tensor", lambda h: h.matmul(ps_sm[0:1, 0:8], c_onesb[:, 0:1], s_w8h[:], start=True, stop=False),
           wait=[("V", p.cnt["V"])])
    p.emit("tensor", lambda h: h.matmul(ps_sm[0:1, 0:8], c_onesb[:, 0:1], s_w8l[:], start=False, stop=True),
           inc="T")
    p.emit("vector", lambda h: h.tensor_copy(s_a[0:1, 0:8], ps_sm[0:1, 0:8]),
           wait=[("T", p.cnt["T"])], inc="V")
    p.drain("vector")
    p.emit("vector", lambda h: h.tensor_tensor(
        s_a[0:1, 8:12],
        s_a[0:1, 0:8].rearrange("p (a b) -> p a b", b=2)[:, :, 0],
        s_a[0:1, 0:8].rearrange("p (a b) -> p a b", b=2)[:, :, 1], op=A.add), inc="V")
    p.drain("vector")
    p.emit("vector", lambda h: h.tensor_copy(s_a[0:1, 0:4], s_a[0:1, 8:12]), inc="V")
    p.drain("vector")
    p.emit("gpsimd", lambda h: h.dma_start(out=ar1_in.ap()[0:1, 0:4], in_=s_a[0:1, 0:4]),
           wait=[("V", p.cnt["V"])], inc="G", inc_by=16)
    p.emit("gpsimd", lambda h: h.dma_start(
        out=ar1_in.ap()[0:1, 4:8], in_=s_a[0:1, 0:4]), inc="G", inc_by=16)
    p.wait_only("gpsimd", [("G", p.cnt["G"])])
    p.emit("gpsimd", lambda h: h.collective_compute(
        "AllReduce", A.add, replica_groups=[list(range(NC))],
        ins=[ar1_in.ap().opt()], outs=[ar1_out.ap().opt()]), inc="C")
    p.emit("gpsimd", lambda h: h.dma_start(out=s_a[0:1, 0:8], in_=ar1_out.ap()),
           wait=[("C", p.cnt["C"])], inc="G", inc_by=16)
    wGr1 = p.cnt["G"]
    p.emit("vector", lambda h: h.tensor_scalar(
        s_a[0:1, 0:4], s_a[0:1, 0:4], 1.0 / (D * D), 1e-5, op0=A.mult, op1=A.max),
        wait=[("G", wGr1)], inc="V")
    p.drain("vector")
    p.emit("vector", lambda h: h.reciprocal(s_a[0:1, 4:8], s_a[0:1, 0:4]), inc="V")
    p.drain("vector")
    p.emit("gpsimd", lambda h: h.dma_start(out=bc_scr.ap()[0:1, 0:4], in_=s_a[0:1, 4:8]),
           wait=[("V", p.cnt["V"])], inc="G", inc_by=16)
    p.emit("gpsimd", lambda h: h.dma_start(out=bc_scr.ap()[1:2, 0:4], in_=s_a[0:1, 0:4]),
           inc="G", inc_by=16)
    p.wait_only("gpsimd", [("G", p.cnt["G"])])
    p.emit("gpsimd", lambda h: h.dma_start(
        out=s_swq[:], in_=bc_scr.ap()[0:1, 0:4].broadcast_to([128, 4])), inc="G", inc_by=16)
    p.emit("gpsimd", lambda h: h.dma_start(
        out=s_swm[:], in_=bc_scr.ap()[1:2, 0:4].broadcast_to([128, 4])), inc="G", inc_by=16)
    wGswq = p.cnt["G"]

    s_tq = sb("s_tq", [128, D], BF16)
    for wi, wsrc in enumerate((wqN, wkN, wvN, woN)):
        for ti in range(2):
            p.emit("sync", lambda h, wsrc=wsrc, ti=ti: h.dma_start(
                out=s_w2k[:], in_=wsrc.ap()[ti * 128:(ti + 1) * 128, :]),
                wait=[("V", p.count("V")), ("T", p.count("T")), ("S", p.count("S"))],
                inc="L", inc_by=16)
            p.emit("vector", lambda h, wi=wi: h.tensor_scalar(
                s_a[:], s_w2k[:], s_swq[:, wi:wi + 1], None, op0=A.mult),
                wait=[("L", p.cnt["L"]), ("G", wGswq)])
            p.drain("vector")
            p.emit("vector", lambda h: h.tensor_scalar(s_b2[:], s_a[:], 0.5, None, op0=A.is_gt))
            p.drain("vector")
            p.emit("vector", lambda h: h.tensor_scalar(s_c2[:], s_a[:], -0.5, None, op0=A.is_lt))
            p.drain("vector")
            p.emit("vector", lambda h: h.tensor_tensor(s_tq[:], s_b2[:], s_c2[:], op=A.subtract),
                   inc="V")
            p.drain("vector")
            for tj in range(CH):
                p.emit("tensor", lambda h, tj=tj: h.transpose(
                    ps_trb[:], s_tq[:, tj * 128:(tj + 1) * 128], c_identb[:]),
                    wait=[("V", p.count("V")), ("G", p.count("G"))], inc="T")
                p.emit("vector", lambda h: h.tensor_copy(s_evb[:], ps_trb[:]),
                       wait=[("T", p.cnt["T"])], inc="V")
                row = tj * 128
                p.emit("gpsimd", lambda h, wi=wi, row=row, ti=ti: h.dma_start(
                    out=ag3_in.ap()[wi * NW:(wi + 1) * NW].rearrange("(a b) -> a b", b=256)[
                        row:row + 128, ti * 128:ti * 128 + 128],
                    in_=s_evb[:]),
                    wait=[("V", p.cnt["V"])], inc="G", inc_by=16)
    wG3 = p.cnt["G"]
    p.wait_only("gpsimd", [("G", wG3)])
    p.emit("gpsimd", lambda h: h.collective_compute(
        "AllGather", A.bypass, replica_groups=[list(range(NC))],
        ins=[ag3_in.ap().opt()], outs=[ag3_out.ap().opt()]), inc="C")
    wC3 = p.cnt["C"]

    def tern_tile(wi, kk, m):
        slot = m // 2
        return ag3_out.ap()[slot, wi * NW:(wi + 1) * NW].rearrange(
            "(a b) -> a b", b=256)[kk * 128:(kk + 1) * 128,
                                   (m % 2) * 128:(m % 2) * 128 + 128]


    def emit_rms_quant(tm_ap, rs_col, wait0):
        p.emit("scalar", lambda h: h.activation(s_a[:], tm_ap, ACT.Square),
               wait=list(wait0), inc="S")
        p.emit("vector", lambda h: h.tensor_reduce(s_cols[:, 8:9], s_a[:], axis=AX.X, op=A.add),
               wait=[("S", p.cnt["S"])], inc="V")
        p.drain("vector")
        p.emit("vector", lambda h: h.tensor_scalar(
            s_cols[:, 8:9], s_cols[:, 8:9], 1.0 / D, 1e-6, op0=A.mult, op1=A.add), inc="V")
        p.drain("vector")
        p.emit("scalar", lambda h: h.activation(s_cols[:, 9:10], s_cols[:, 8:9], ACT.Sqrt),
               wait=[("V", p.cnt["V"])], inc="S")
        p.emit("vector", lambda h: h.reciprocal(s_cols[:, 8:9], s_cols[:, 9:10]),
               wait=[("S", p.cnt["S"])], inc="V")
        p.drain("vector")
        p.emit("vector", lambda h: h.tensor_scalar(tm_ap, tm_ap, s_cols[:, 8:9], None, op0=A.mult),
               inc="V")
        p.drain("vector")
        p.emit("scalar", lambda h: h.activation(s_a[:], tm_ap, ACT.Abs),
               wait=[("V", p.cnt["V"])], inc="S")
        p.emit("vector", lambda h: h.tensor_reduce(s_cols[:, 9:10], s_a[:], axis=AX.X, op=A.max),
               wait=[("S", p.cnt["S"])], inc="V")
        p.drain("vector")
        p.emit("vector", lambda h: h.tensor_scalar(
            s_cols[:, 9:10], s_cols[:, 9:10], 1e-5, 1.0 / 127.0, op0=A.max, op1=A.mult), inc="V")
        p.drain("vector")
        p.emit("vector", lambda h, rs_col=rs_col: h.tensor_copy(rs_col, s_cols[:, 9:10]))
        p.emit("vector", lambda h: h.reciprocal(s_cols[:, 10:11], s_cols[:, 9:10]), inc="V")
        p.drain("vector")
        p.emit("vector", lambda h: h.tensor_scalar(tm_ap, tm_ap, s_cols[:, 10:11], None, op0=A.mult),
               inc="V")
        p.drain("vector")
        p.emit("scalar", lambda h: h.activation(s_a[:], tm_ap, ACT.Abs),
               wait=[("V", p.cnt["V"])], inc="S")
        p.emit("scalar", lambda h: h.activation(s_b2[:], tm_ap, ACT.Sign), inc="S")
        wS2 = p.cnt["S"]
        p.emit("vector", lambda h: h.tensor_copy(s_c2[:], s_a[:]), wait=[("S", wS2)])
        p.drain("vector")
        for w_ in (64.0, 32.0, 16.0, 8.0, 4.0, 2.0, 1.0):
            p.emit("vector", lambda h, w_=w_: h.tensor_scalar(tm_ap, s_a[:], w_, None, op0=A.is_ge))
            p.drain("vector")
            p.emit("vector", lambda h, w_=w_: h.scalar_tensor_tensor(
                s_a[:], tm_ap, -w_, s_a[:], op0=A.mult, op1=A.add))
            p.drain("vector")
        p.emit("vector", lambda h: h.tensor_scalar(tm_ap, s_a[:], 0.5, None, op0=A.is_ge))
        p.drain("vector")
        p.emit("vector", lambda h: h.tensor_tensor(s_c2[:], s_c2[:], s_a[:], op=A.subtract))
        p.drain("vector")
        p.emit("vector", lambda h: h.tensor_tensor(s_c2[:], s_c2[:], tm_ap, op=A.add))
        p.drain("vector")
        p.emit("vector", lambda h: h.tensor_tensor(tm_ap, s_c2[:], s_b2[:], op=A.mult), inc="V")
        p.drain("vector")
        return p.cnt["V"]

    for tc in range(TOK // 128):
        p.emit("sync", lambda h, tc=tc: h.dma_start(
            out=s_w2k[:], in_=x1_spill.ap().rearrange("(c p) t -> p c t", p=128)[
                :, :, tc * 128:(tc + 1) * 128]),
            wait=[("G", w_x1sp), ("V", p.count("V")), ("T", p.count("T"))], inc="L", inc_by=16)
        wLx1 = p.cnt["L"]
        for kk in range(CH):
            p.emit("tensor", lambda h, kk=kk: h.transpose(
                ps_tr2[:], s_w2k[:, kk * 128:(kk + 1) * 128], c_ident[:]),
                wait=([("L", wLx1)] if kk == 0 else []) + [("V", p.count("V"))], inc="T")
            p.emit("vector", lambda h, kk=kk: h.tensor_copy(s_t2k[:, kk * 128:(kk + 1) * 128], ps_tr2[:]),
                   wait=[("T", p.cnt["T"])], inc="V")
        p.drain("vector")
        p.emit("vector", lambda h: h.tensor_reduce(s_cols[:, 0:1], s_t2k[:], axis=AX.X, op=A.add),
               wait=[("V", p.cnt["V"])], inc="V")
        p.drain("vector")
        p.emit("vector", lambda h: h.tensor_scalar(s_cols[:, 0:1], s_cols[:, 0:1], 1.0 / D, None, op0=A.mult),
               inc="V")
        p.drain("vector")
        p.emit("scalar", lambda h: h.activation(s_a[:], s_t2k[:], ACT.Square),
               wait=[("V", p.cnt["V"])], inc="S")
        p.emit("vector", lambda h: h.tensor_reduce(s_cols[:, 1:2], s_a[:], axis=AX.X, op=A.add),
               wait=[("S", p.cnt["S"])], inc="V")
        p.drain("vector")
        p.emit("vector", lambda h: h.tensor_tensor(s_cols[:, 2:3], s_cols[:, 0:1], s_cols[:, 0:1], op=A.mult))
        p.drain("vector")
        p.emit("vector", lambda h: h.scalar_tensor_tensor(
            s_cols[:, 1:2], s_cols[:, 1:2], 1.0 / D, s_cols[:, 2:3], op0=A.mult, op1=A.subtract),
            inc="V")
        p.drain("vector")
        p.emit("scalar", lambda h: h.activation(s_cols[:, 1:2], s_cols[:, 1:2], ACT.Sqrt,
                                                bias=c_e5[:, 0:1], scale=1.0),
               wait=[("V", p.cnt["V"])], inc="S")
        p.emit("vector", lambda h: h.reciprocal(s_cols[:, 1:2], s_cols[:, 1:2]),
               wait=[("S", p.cnt["S"])], inc="V")
        p.drain("vector")
        p.emit("vector", lambda h: h.tensor_scalar(
            s_t2k[:], s_t2k[:], s_cols[:, 0:1], None, op0=A.subtract))
        p.drain("vector")
        p.emit("vector", lambda h: h.tensor_scalar(
            s_t2k[:], s_t2k[:], s_cols[:, 1:2], None, op0=A.mult), inc="V")
        p.drain("vector")
        for kk in range(CH):
            p.emit("tensor", lambda h, kk=kk: h.transpose(
                ps_tr2[:], s_t2k[:, kk * 128:(kk + 1) * 128], c_ident[:]),
                wait=[("V", p.count("V"))], inc="T")
            p.emit("vector", lambda h, kk=kk, tc=tc: h.tensor_copy(
                x1h(kk)[:, tc * 128:(tc + 1) * 128], ps_tr2[:]),
                wait=[("T", p.cnt["T"])])
            p.drain("vector")
            p.emit("vector", lambda h, kk=kk, tc=tc: h.tensor_tensor(
                x1l(kk)[:, tc * 128:(tc + 1) * 128], ps_tr2[:],
                x1h(kk)[:, tc * 128:(tc + 1) * 128], op=A.subtract), inc="V")
            p.drain("vector")
        emit_rms_quant(s_t2k[:], s_rsx[:, tc:tc + 1], [("V", p.cnt["V"])])
        for kk in range(CH):
            p.emit("tensor", lambda h, kk=kk: h.transpose(
                ps_tr2[:], s_t2k[:, kk * 128:(kk + 1) * 128], c_ident[:]),
                wait=[("V", p.count("V"))], inc="T")
            p.emit("vector", lambda h, kk=kk, tc=tc: h.tensor_copy(
                ifm(kk)[:, tc * 128:(tc + 1) * 128], ps_tr2[:]),
                wait=[("T", p.cnt["T"])], inc="V")
        p.drain("vector")
    w_ix = p.cnt["V"]

    s_wr8h = sb("s_wr8h", [128, CH * 8], BF16)
    s_wr8l = sb("s_wr8l", [128, CH * 8], BF16)
    s_aah = sb("s_aah", [128, CH * 64], BF16)
    s_aal = sb("s_aal", [128, CH * 64], BF16)
    for kk in range(CH):
        p.emit("sync", lambda h, kk=kk: h.dma_start(
            out=s_w2k[:, 0:8], in_=wrT.ap()[kk * 128:(kk + 1) * 128, :]),
            wait=[("V", p.count("V")), ("T", p.count("T"))], inc="L", inc_by=16)
        p.emit("sync", lambda h, kk=kk: h.dma_start(
            out=s_w2k[:, 64:128], in_=aaT.ap()[kk * 128:(kk + 1) * 128, :]),
            inc="L", inc_by=16)
        wLr = p.cnt["L"]
        p.emit("vector", lambda h, kk=kk: h.tensor_copy(s_wr8h[:, kk * 8:(kk + 1) * 8], s_w2k[:, 0:8]),
               wait=[("L", wLr)])
        p.emit("vector", lambda h, kk=kk: h.tensor_copy(s_aah[:, kk * 64:(kk + 1) * 64], s_w2k[:, 64:128]))
        p.drain("vector")
        p.emit("vector", lambda h, kk=kk: h.tensor_tensor(
            s_wr8l[:, kk * 8:(kk + 1) * 8], s_w2k[:, 0:8], s_wr8h[:, kk * 8:(kk + 1) * 8], op=A.subtract))
        p.emit("vector", lambda h, kk=kk: h.tensor_tensor(
            s_aal[:, kk * 64:(kk + 1) * 64], s_w2k[:, 64:128], s_aah[:, kk * 64:(kk + 1) * 64], op=A.subtract),
            inc="V")
        p.drain("vector")
    w_wra = p.cnt["V"]

    s_lg = sb("s_lg", [128, 8 * 8])
    s_ha = sb("s_ha", [128, 8 * 64])
    s_hsch = sb("s_hsch", [128, TOK], BF16)
    s_hscl = sb("s_hscl", [128, TOK], BF16)
    for tc in range(8):
        first = True
        for pi, (xs_, ws_) in enumerate(((0, 0), (0, 1), (1, 0))):
            for kk in range(CH):
                st = (pi == 0 and kk == 0)
                sp = (pi == 2 and kk == CH - 1)
                xop = (s_x1h if xs_ == 0 else s_x1l)[:, kk * TOK + tc * 128: kk * TOK + tc * 128 + 128]
                wop = (s_wr8h if ws_ == 0 else s_wr8l)[:, kk * 8:(kk + 1) * 8]
                aop = (s_aah if ws_ == 0 else s_aal)[:, kk * 64:(kk + 1) * 64]
                wt = [("V", p.count("V"))] if first else []
                first = False
                p.emit("tensor", lambda h, xop=xop, wop=wop, st=st, sp=sp: h.matmul(
                    ps_sm[:, 0:8], xop, wop, start=st, stop=sp),
                    wait=wt, inc="T" if sp else None)
                p.emit("tensor", lambda h, xop=xop, aop=aop, st=st, sp=sp: h.matmul(
                    ps_m0[:, 0:64], xop, aop, start=st, stop=sp),
                    inc="T" if sp else None)
        wTr = p.cnt["T"]
        p.emit("vector", lambda h, tc=tc: h.tensor_copy(s_lg[:, tc * 8:(tc + 1) * 8], ps_sm[:, 0:8]),
               wait=[("T", wTr)])
        p.emit("vector", lambda h, tc=tc: h.tensor_copy(s_ha[:, tc * 64:(tc + 1) * 64], ps_m0[:, 0:64]),
               inc="V")
        p.drain("vector")
    w_lg = p.cnt["V"]

    for tc in range(8):
        lg = s_lg[:, tc * 8:(tc + 1) * 8]
        p.emit("vector", lambda h, lg=lg: h.tensor_reduce(s_cols[:, 0:1], lg, axis=AX.X, op=A.max),
               wait=[("V", w_lg)] if tc == 0 else [])
        p.drain("vector")
        p.emit("vector", lambda h, lg=lg: h.tensor_scalar(s_a[:, 0:8], lg, s_cols[:, 0:1], None, op0=A.is_ge))
        p.drain("vector")
        p.emit("vector", lambda h: h.tensor_reduce(s_cols[:, 1:2], s_a[:, 0:8], axis=AX.X, op=A.add))
        p.drain("vector")
        p.emit("vector", lambda h: h.tensor_scalar(s_a[:, 0:8], s_a[:, 0:8], 1e30, None, op0=A.mult))
        p.drain("vector")
        p.emit("vector", lambda h, lg=lg: h.tensor_tensor(s_a[:, 8:16], lg, s_a[:, 0:8], op=A.subtract))
        p.drain("vector")
        p.emit("vector", lambda h: h.tensor_reduce(s_cols[:, 2:3], s_a[:, 8:16], axis=AX.X, op=A.max))
        p.drain("vector")
        p.emit("vector", lambda h: h.tensor_scalar(s_cols[:, 3:4], s_cols[:, 1:2], 2.0, None, op0=A.is_ge))
        p.drain("vector")
        p.emit("vector", lambda h: h.tensor_tensor(s_cols[:, 4:5], s_cols[:, 0:1], s_cols[:, 2:3], op=A.subtract))
        p.drain("vector")
        p.emit("vector", lambda h: h.tensor_tensor(s_cols[:, 4:5], s_cols[:, 4:5], s_cols[:, 3:4], op=A.mult))
        p.drain("vector")
        p.emit("vector", lambda h: h.tensor_tensor(s_cols[:, 2:3], s_cols[:, 2:3], s_cols[:, 4:5], op=A.add),
               inc="V")
        p.drain("vector")
        p.emit("vector", lambda h: h.tensor_tensor(s_cols[:, 5:6], s_cols[:, 2:3], s_cols[:, 0:1], op=A.subtract),
               inc="V")
        p.drain("vector")
        p.emit("scalar", lambda h: h.activation(s_cols[:, 5:6], s_cols[:, 5:6], ACT.Exp),
               wait=[("V", p.cnt["V"])], inc="S")
        p.emit("vector", lambda h: h.tensor_scalar(s_cols[:, 5:6], s_cols[:, 5:6], 1.0, None, op0=A.add),
               wait=[("S", p.cnt["S"])], inc="V")
        p.drain("vector")
        p.emit("vector", lambda h: h.reciprocal(s_cols[:, 6:7], s_cols[:, 5:6]), inc="V")
        p.drain("vector")
        p.emit("vector", lambda h: h.tensor_scalar(s_cols[:, 6:7], s_cols[:, 6:7], 2.0, None, op0=A.mult))
        p.drain("vector")
        p.emit("vector", lambda h, lg=lg: h.tensor_scalar(s_a[:, 0:8], lg, s_cols[:, 2:3], None, op0=A.is_ge))
        p.drain("vector")
        p.emit("vector", lambda h: h.tensor_scalar(s_cols[:, 7:8], s_cols[:, 0:1], -1.0, None, op0=A.mult),
               inc="V")
        p.drain("vector")
        p.emit("scalar", lambda h, lg=lg: h.activation(s_a[:, 8:16], lg, ACT.Exp,
                                                       bias=s_cols[:, 7:8], scale=1.0),
               wait=[("V", p.cnt["V"])], inc="S")
        p.emit("vector", lambda h: h.tensor_tensor(s_a[:, 0:8], s_a[:, 0:8], s_a[:, 8:16], op=A.mult),
               wait=[("S", p.cnt["S"])])
        p.drain("vector")
        p.emit("vector", lambda h: h.tensor_scalar(s_a[:, 0:8], s_a[:, 0:8], s_cols[:, 6:7], None, op0=A.mult),
               inc="V")
        p.drain("vector")
        p.emit("vector", lambda h, tc=tc: h.tensor_tensor(
            s_a[:, 64:128].rearrange("p (r e) -> p r e", e=8),
            s_ha[:, tc * 64:(tc + 1) * 64].rearrange("p (r e) -> p r e", e=8),
            s_a[:, 0:8].unsqueeze(1).broadcast_to([128, 8, 8]), op=A.mult), inc="V")
        p.drain("vector")
        p.emit("tensor", lambda h: h.transpose(ps_tr2[0:64, :], s_a[:, 64:128], c_ident[:]),
               wait=[("V", p.cnt["V"])], inc="T")
        p.emit("vector", lambda h, tc=tc: h.tensor_copy(
            s_hsch[0:64, tc * 128:(tc + 1) * 128], ps_tr2[0:64, :]),
            wait=[("T", p.cnt["T"])])
        p.drain("vector")
        p.emit("vector", lambda h, tc=tc: h.tensor_tensor(
            s_hscl[0:64, tc * 128:(tc + 1) * 128], ps_tr2[0:64, :],
            s_hsch[0:64, tc * 128:(tc + 1) * 128], op=A.subtract), inc="V")
        p.drain("vector")
    w_hsc = p.cnt["V"]

    s_blh = sb("s_blh", [128, D], BF16)
    s_bll = sb("s_bll", [128, D], BF16)
    p.emit("sync", lambda h: h.dma_start(out=s_w2k[0:64, 0:D], in_=ball_i.ap()),
           wait=[("V", p.count("V")), ("T", p.count("T"))], inc="L", inc_by=16)
    p.emit("vector", lambda h: h.tensor_copy(s_blh[0:64, :], s_w2k[0:64, 0:D]),
           wait=[("L", p.cnt["L"])])
    p.drain("vector")
    p.emit("vector", lambda h: h.tensor_tensor(s_bll[0:64, :], s_w2k[0:64, 0:D], s_blh[0:64, :], op=A.subtract),
           inc="V")
    p.drain("vector")
    w_bl = p.cnt["V"]

    t_g1 = tap("g1", [D, TOK])
    for m in range(CH):
        for kk in range(CH):
            p.emit("sync", lambda h, kk=kk, m=m: h.dma_start(
                out=s_w2k[:, kk * 128:(kk + 1) * 128],
                in_=wbT.ap()[kk * 128:(kk + 1) * 128, m * 128:(m + 1) * 128]),
                wait=([("V", p.count("V")), ("T", p.count("T"))] if kk == 0 else []),
                inc="L", inc_by=16)
        wLb = p.cnt["L"]
        p.emit("vector", lambda h: h.tensor_copy(s_wbh[:], s_w2k[:]), wait=[("L", wLb)])
        p.drain("vector")
        p.emit("vector", lambda h: h.tensor_tensor(s_wbl[:], s_w2k[:], s_wbh[:], op=A.subtract),
               inc="V")
        p.drain("vector")
        wsp = p.cnt["V"]
        for th in range(2):
            pd = ps_m0 if th == 0 else ps_m1
            sl = slice(th * 512, (th + 1) * 512)
            first = True
            for pi, (xs_, ws_) in enumerate(((0, 0), (0, 1), (1, 0))):
                for kk in range(CH):
                    st = (pi == 0 and kk == 0)
                    xop = (s_wbh if ws_ == 0 else s_wbl)[:, kk * 128:(kk + 1) * 128]
                    rop = (s_x1h if xs_ == 0 else s_x1l)[:, kk * TOK + th * 512: kk * TOK + th * 512 + 512]
                    wt = [("V", wsp), ("V", p.count("V"))] if first else []
                    first = False
                    p.emit("tensor", lambda h, xop=xop, rop=rop, st=st, pd=pd: h.matmul(
                        pd[:], xop, rop, start=st, stop=False),
                        wait=wt)
            for pi, (hs_, ws_) in enumerate(((0, 0), (0, 1), (1, 0))):
                sp = (pi == 2)
                bop = (s_blh if ws_ == 0 else s_bll)[0:64, m * 128:(m + 1) * 128]
                hop = (s_hsch if hs_ == 0 else s_hscl)[0:64, sl]
                p.emit("tensor", lambda h, bop=bop, hop=hop, sp=sp, pd=pd: h.matmul(
                    pd[:], bop, hop, start=False, stop=sp),
                    wait=[("V", w_hsc), ("V", w_bl)] if pi == 0 else [], inc="T" if sp else None)
            wTb = p.cnt["T"]
            p.emit("scalar", lambda h, m=m, sl=sl, pd=pd: h.activation(
                s_c2[:, sl], pd[:], ACT.Silu, bias=s_bb1[:, m:m + 1], scale=1.0),
                wait=[("T", wTb), ("L", w_bb1), ("G", p.count("G"))], inc="S")
        wSg = p.cnt["S"]
        p.emit("gpsimd", lambda h, m=m: h.dma_start(
            out=g1_spill.ap()[m * 128:(m + 1) * 128, :], in_=s_c2[:, 0:TOK]),
            wait=[("S", wSg)], inc="G", inc_by=16)
        if t_g1 is not None:
            p.emit("gpsimd", lambda h, m=m: h.dma_start(
                out=t_g1.ap()[m * 128:(m + 1) * 128, :], in_=s_c2[:, 0:TOK]),
                inc="G", inc_by=16)
        p.wait_only("scalar", [("G", p.cnt["G"])])
    w_g1 = p.cnt["G"]
    free_back_to("s_x1h")
    _bar5 = [("V", p.count("V")), ("S", p.count("S")), ("T", p.count("T")),
             ("G", p.count("G")), ("L", p.count("L"))]
    for _e in ENGINES:
        p.wait_only(_e, _bar5)

    p.emit("gpsimd", lambda h: h.dma_start(
        out=rq_scr.ap()[0:8, :].transpose([1, 0]), in_=s_rsx[:],
        allow_slow_non_contiguous=True),
        wait=[("V", w_ix)], inc="G", inc_by=16)
    p.wait_only("gpsimd", [("G", p.cnt["G"])])
    for c_ in range(8):
        p.emit("gpsimd", lambda h, c_=c_: h.dma_start(
            out=s_row[:, c_ * 128:(c_ + 1) * 128],
            in_=rq_scr.ap()[c_:c_ + 1, :].broadcast_to([128, 128])), inc="G", inc_by=16)
    wGrs = p.cnt["G"]
    p.emit("vector", lambda h: h.tensor_scalar(
        s_row2[:], s_row[:], s_swm[:, 0:1], 1.0 / math.sqrt(128.0), op0=A.mult, op1=A.mult),
        wait=[("G", wGrs)])
    p.drain("vector")
    p.emit("vector", lambda h: h.tensor_scalar(s_row3[:], s_row[:], s_swm[:, 1:2], None, op0=A.mult),
           inc="V")
    p.drain("vector")
    p.emit("vector", lambda h: h.tensor_scalar(s_rowv[:], s_row[:], s_swm[:, 2:3], None, op0=A.mult),
           inc="V")
    p.drain("vector")
    w_rows = p.cnt["V"]

    p.emit("sync", lambda h: h.dma_start(out=s_ropeC[:], in_=ropeC.ap()), inc="L", inc_by=16)
    p.emit("sync", lambda h: h.dma_start(out=s_ropeS[:], in_=ropeS.ap()), inc="L", inc_by=16)
    w_ropeL = p.cnt["L"]
    p.wait_only("vector", [("L", w_ropeL)])
    for wi, rowt in ((0, s_row2), (1, s_row3), (2, s_rowv)):
        for m in range(CH):
            for kk in range(CH):
                p.emit("sync", lambda h, wi=wi, kk=kk, m=m: h.dma_start(
                    out=s_wqkv[:, kk * 128:(kk + 1) * 128], in_=tern_tile(wi, kk, m)),
                    wait=([("C", wC3), ("T", p.count("T"))] if kk == 0 else []),
                    inc="L", inc_by=16)
            wLq = p.cnt["L"]
            for th in range(2):
                pd = ps_m0 if th == 0 else ps_m1
                for kk in range(CH):
                    p.emit("tensor", lambda h, kk=kk, th=th, pd=pd: h.matmul(
                        pd[:], s_wqkv[:, kk * 128:(kk + 1) * 128],
                        s_ifm[:, kk * TOK + th * 512: kk * TOK + th * 512 + 512],
                        start=(kk == 0), stop=(kk == CH - 1)),
                        wait=([("L", wLq), ("V", p.count("V"))] if kk == 0 else []),
                        inc="T" if kk == CH - 1 else None)
            wTq = p.cnt["T"]
            p.emit("vector", lambda h, rowt=rowt: h.tensor_tensor(
                s_pj[:, 0:512], ps_m0[:], rowt[:, 0:512], op=A.mult),
                wait=[("T", wTq), ("V", w_rows)])
            p.emit("vector", lambda h, rowt=rowt: h.tensor_tensor(
                s_pj[:, 512:1024], ps_m1[:], rowt[:, 512:1024], op=A.mult), inc="V")
            p.drain("vector")
            if wi < 2:
                p.emit("vector", lambda h: h.stream_shuffle(
                    s_pj2[:], s_pj[:], [i ^ 1 for i in range(32)]), inc="V")
                p.drain("vector")
                p.emit("vector", lambda h: h.tensor_tensor(s_pj[:], s_pj[:], s_ropeC[:], op=A.mult))
                p.drain("vector")
                p.emit("vector", lambda h: h.tensor_tensor(s_pj2[:], s_pj2[:], s_ropeS[:], op=A.mult))
                p.drain("vector")
                p.emit("vector", lambda h: h.tensor_tensor(s_pj[:], s_pj[:], s_pj2[:], op=A.add),
                       inc="V")
                p.drain("vector")
            p.emit("gpsimd", lambda h, wi=wi, m=m: h.dma_start(
                out=a2a6_in.ap()[m // 2, wi, m % 2, :, :], in_=s_pj[:]),
                wait=[("V", p.cnt["V"])], inc="G", inc_by=16)
            p.wait_only("vector", [("G", p.cnt["G"])])
    p.wait_only("gpsimd", [("G", p.cnt["G"])])
    t_qr = tap("qrope", [D, TOK])
    t_kr = tap("krope", [D, TOK])
    t_vp = tap("vpre", [D, TOK])
    for tt, wi in ((t_qr, 0), (t_kr, 1), (t_vp, 2)):
        if tt is not None:
            for m in range(CH):
                p.emit("gpsimd", lambda h, tt=tt, wi=wi, m=m: h.dma_start(
                    out=tt.ap()[m * 128:(m + 1) * 128, :],
                    in_=a2a6_in.ap()[m // 2, wi, m % 2, :, :]), inc="G", inc_by=16)
    p.wait_only("gpsimd", [("G", p.cnt["G"])])
    p.emit("gpsimd", lambda h: h.collective_compute(
        "AllToAll", A.bypass, replica_groups=[list(range(NC))],
        ins=[a2a6_in.ap().opt()], outs=[a2a6_out.ap().opt()]), inc="C")
    wC6 = p.cnt["C"]

    s_qh = sb("s_qh", [128, 2048], BF16)
    s_ql = sb("s_ql", [128, 2048], BF16)
    s_kh = sb("s_kh", [128, 2048], BF16)
    s_kl = sb("s_kl", [128, 2048], BF16)
    s_vth = sb("s_vth", [128, 2048], BF16)
    s_vtl = sb("s_vtl", [128, 2048], BF16)
    s_P = sb("s_P", [128, 2048])
    s_PTh = sb("s_PTh", [128, 128], BF16)
    s_PTl = sb("s_PTl", [128, 128], BF16)
    s_y = sb("s_y", [128, 2048])
    s_dm = sb("s_dm", [128, 4 * 512])
    s_rq16 = sb("s_rq16", [128, 16])
    s_rowq = sb("s_rowq", [128, 2048])
    p.emit("sync", lambda h: h.dma_start(out=s_dm[:], in_=dmask.ap()), inc="L", inc_by=16)
    w_dm = p.cnt["L"]

    for jb in range(NC):
        bi, hs = jb // 2, jb % 2
        for o in range(2):
            p.emit("sync", lambda h, bi=bi, o=o, hs=hs: h.dma_start(
                out=s_pj[:], in_=a2a6_out.ap()[2 * bi + o, 0, hs, :, :]),
                wait=[("C", wC6), ("V", p.count("V")), ("T", p.count("T"))], inc="L", inc_by=16)
            p.emit("sync", lambda h, bi=bi, o=o, hs=hs: h.dma_start(
                out=s_pj2[:], in_=a2a6_out.ap()[2 * bi + o, 1, hs, :, :]),
                inc="L", inc_by=16)
            wLj = p.cnt["L"]
            sl = slice(o * TOK, (o + 1) * TOK)
            p.emit("vector", lambda h, sl=sl: h.tensor_copy(s_qh[:, sl], s_pj[:]),
                   wait=[("L", wLj)])
            p.emit("vector", lambda h, sl=sl: h.tensor_copy(s_kh[:, sl], s_pj2[:]))
            p.drain("vector")
            p.emit("vector", lambda h, sl=sl: h.tensor_tensor(s_ql[:, sl], s_pj[:], s_qh[:, sl], op=A.subtract))
            p.emit("vector", lambda h, sl=sl: h.tensor_tensor(s_kl[:, sl], s_pj2[:], s_kh[:, sl], op=A.subtract),
                   inc="V")
            p.drain("vector")
            p.emit("sync", lambda h, bi=bi, o=o, hs=hs: h.dma_start(
                out=s_pj[:], in_=a2a6_out.ap()[2 * bi + o, 2, hs, :, :]),
                wait=[("V", p.count("V"))], inc="L", inc_by=16)
            wLv = p.cnt["L"]
            for kt in range(8):
                p.emit("tensor", lambda h, kt=kt: h.transpose(
                    ps_tr2[:], s_pj[:, kt * 128:(kt + 1) * 128], c_ident[:]),
                    wait=([("L", wLv)] if kt == 0 else []) + [("V", p.count("V"))], inc="T")
                col = (o * 8 + kt) * 128
                p.emit("vector", lambda h, col=col: h.tensor_copy(
                    s_vth[:, col:col + 128], ps_tr2[:]),
                    wait=[("T", p.cnt["T"])])
                p.drain("vector")
                p.emit("vector", lambda h, col=col: h.tensor_tensor(
                    s_vtl[:, col:col + 128], ps_tr2[:], s_vth[:, col:col + 128], op=A.subtract),
                    inc="V")
                p.drain("vector")
        w_qkv = p.cnt["V"]
        for qt in range(16):
            ng = qt // 4 + 1
            al = qt % 4
            for g in range(ng):
                ksl = slice(g * 512, (g + 1) * 512)
                pd = ps_m0 if g % 2 == 0 else ps_m1
                for pi, (qs_, ks_) in enumerate(((0, 0), (0, 1), (1, 0))):
                    qop = (s_qh if qs_ == 0 else s_ql)[:, qt * 128:(qt + 1) * 128]
                    kop = (s_kh if ks_ == 0 else s_kl)[:, ksl]
                    p.emit("tensor", lambda h, qop=qop, kop=kop, pi=pi, pd=pd: h.matmul(
                        pd[:], qop, kop, start=(pi == 0), stop=(pi == 2)),
                        wait=([("V", p.count("V"))] if (pi == 0) else []),
                        inc="T" if pi == 2 else None)
                wTs = p.cnt["T"]
                if g == ng - 1:
                    p.emit("vector", lambda h, ksl=ksl, al=al, pd=pd: h.tensor_tensor(
                        s_P[:, ksl], pd[:], s_dm[:, al * 512:(al + 1) * 512], op=A.add),
                        wait=[("T", wTs), ("L", w_dm)], inc="V")
                else:
                    p.emit("vector", lambda h, ksl=ksl, pd=pd: h.tensor_copy(s_P[:, ksl], pd[:]),
                           wait=[("T", wTs)], inc="V")
            p.drain("vector")
            wid = ng * 512
            p.emit("vector", lambda h, wid=wid: h.tensor_reduce(
                s_cols[:, 0:1], s_P[:, 0:wid], axis=AX.X, op=A.max), inc="V")
            p.drain("vector")
            p.emit("vector", lambda h: h.tensor_scalar(s_cols[:, 1:2], s_cols[:, 0:1], -1.0, None, op0=A.mult),
                   inc="V")
            p.drain("vector")
            p.emit("scalar", lambda h, wid=wid: h.activation(
                s_P[:, 0:wid], s_P[:, 0:wid], ACT.Exp, bias=s_cols[:, 1:2], scale=1.0,
                accum_out=s_cols[:, 2:3]),
                wait=[("V", p.cnt["V"])], inc="S")
            p.emit("vector", lambda h: h.reciprocal(s_cols[:, 3:4], s_cols[:, 2:3]),
                   wait=[("S", p.cnt["S"])])
            p.drain("vector")
            p.emit("vector", lambda h, qt=qt: h.tensor_copy(s_rq16[:, qt:qt + 1], s_cols[:, 3:4]),
                   inc="V")
            p.drain("vector")
            nkc = ng * 4
            for kc in range(nkc):
                p.emit("tensor", lambda h, kc=kc: h.transpose(
                    ps_tr2[:], s_P[:, kc * 128:(kc + 1) * 128], c_ident[:]),
                    wait=[("V", p.count("V")), ("S", p.count("S"))], inc="T")
                p.emit("vector", lambda h: h.tensor_copy(s_PTh[:], ps_tr2[:]),
                       wait=[("T", p.cnt["T"])])
                p.drain("vector")
                p.emit("vector", lambda h: h.tensor_tensor(s_PTl[:], ps_tr2[:], s_PTh[:], op=A.subtract),
                       inc="V")
                p.drain("vector")
                wVpt = p.cnt["V"]
                for pi, (vs_, ps_2) in enumerate(((0, 0), (0, 1), (1, 0))):
                    vop = (s_vth if vs_ == 0 else s_vtl)[:, kc * 128:(kc + 1) * 128]
                    pop = s_PTh if ps_2 == 0 else s_PTl
                    p.emit("tensor", lambda h, vop=vop, pop=pop, st=(kc == 0 and pi == 0),
                           sp=(kc == nkc - 1 and pi == 2): h.matmul(
                        ps_y[:], vop, pop[:], start=st, stop=sp),
                        wait=[("V", wVpt)] if pi == 0 else [],
                        inc="T" if pi == 2 else None)
            wTy = p.cnt["T"]
            p.emit("vector", lambda h, qt=qt: h.tensor_copy(s_y[:, qt * 128:(qt + 1) * 128],
                                                            ps_y[:]),
                   wait=[("T", wTy)], inc="V")
            p.drain("vector")
        p.emit("gpsimd", lambda h, jb=jb: h.dma_start(
            out=rq_scr.ap().transpose([1, 0]), in_=s_rq16[:],
            allow_slow_non_contiguous=True),
            wait=[("V", p.cnt["V"])], inc="G", inc_by=16)
        p.wait_only("gpsimd", [("G", p.cnt["G"])])
        for c_ in range(16):
            p.emit("gpsimd", lambda h, c_=c_: h.dma_start(
                out=s_rowq[:, c_ * 128:(c_ + 1) * 128],
                in_=rq_scr.ap()[c_:c_ + 1, :].broadcast_to([128, 128])), inc="G", inc_by=16)
        wGq = p.cnt["G"]
        p.emit("vector", lambda h: h.tensor_tensor(s_y[:], s_y[:], s_rowq[:], op=A.mult),
               wait=[("G", wGq)], inc="V")
        p.drain("vector")
        for o in range(2):
            p.emit("gpsimd", lambda h, bi=bi, o=o, hs=hs: h.dma_start(
                out=a2a7_in.ap()[2 * bi + o, hs, :, :], in_=s_y[:, o * TOK:(o + 1) * TOK]),
                wait=[("V", p.cnt["V"])], inc="G", inc_by=16)
        p.wait_only("vector", [("G", p.cnt["G"])])
        p.wait_only("tensor", [("G", p.cnt["G"])])
    p.wait_only("gpsimd", [("G", p.cnt["G"])])
    p.emit("gpsimd", lambda h: h.collective_compute(
        "AllToAll", A.bypass, replica_groups=[list(range(NC))],
        ins=[a2a7_in.ap().opt()], outs=[a2a7_out.ap().opt()]), inc="C")
    wC7 = p.cnt["C"]

    t_ya = tap("yatt", [D, TOK])
    if t_ya is not None:
        for kk in range(CH):
            p.emit("gpsimd", lambda h, kk=kk: h.dma_start(
                out=t_ya.ap()[kk * 128:(kk + 1) * 128, :],
                in_=a2a7_out.ap()[kk // 2, kk % 2, :, :]),
                wait=[("C", wC7)] if kk == 0 else [], inc="G", inc_by=16)

    s_rsy = s_rsx
    for tc in range(TOK // 128):
        for kk in range(CH):
            p.emit("sync", lambda h, kk=kk, tc=tc: h.dma_start(
                out=s_w2k[:, kk * 128:(kk + 1) * 128],
                in_=a2a7_out.ap()[kk // 2, kk % 2, :, tc * 128:(tc + 1) * 128]),
                wait=([("C", wC7), ("V", p.count("V")), ("T", p.count("T"))] if kk == 0 else []),
                inc="L", inc_by=16)
        wLy = p.cnt["L"]
        for kk in range(CH):
            p.emit("tensor", lambda h, kk=kk: h.transpose(
                ps_tr2[:], s_w2k[:, kk * 128:(kk + 1) * 128], c_ident[:]),
                wait=([("L", wLy)] if kk == 0 else []) + [("V", p.count("V"))], inc="T")
            p.emit("vector", lambda h, kk=kk: h.tensor_copy(s_t2k[:, kk * 128:(kk + 1) * 128], ps_tr2[:]),
                   wait=[("T", p.cnt["T"])], inc="V")
        p.drain("vector")
        emit_rms_quant(s_t2k[:], s_rsy[:, tc:tc + 1], [("V", p.cnt["V"])])
        for kk in range(CH):
            p.emit("tensor", lambda h, kk=kk: h.transpose(
                ps_tr2[:], s_t2k[:, kk * 128:(kk + 1) * 128], c_ident[:]),
                wait=[("V", p.count("V"))], inc="T")
            p.emit("vector", lambda h, kk=kk, tc=tc: h.tensor_copy(
                ifm(kk)[:, tc * 128:(tc + 1) * 128], ps_tr2[:]),
                wait=[("T", p.cnt["T"])], inc="V")
        p.drain("vector")
    w_iy = p.cnt["V"]

    p.emit("gpsimd", lambda h: h.dma_start(
        out=rq_scr.ap()[0:8, :].transpose([1, 0]), in_=s_rsy[:],
        allow_slow_non_contiguous=True),
        wait=[("V", w_iy)], inc="G", inc_by=16)
    p.wait_only("gpsimd", [("G", p.cnt["G"])])
    for c_ in range(8):
        p.emit("gpsimd", lambda h, c_=c_: h.dma_start(
            out=s_row[:, c_ * 128:(c_ + 1) * 128],
            in_=rq_scr.ap()[c_:c_ + 1, :].broadcast_to([128, 128])), inc="G", inc_by=16)
    wGro = p.cnt["G"]
    p.emit("vector", lambda h: h.tensor_scalar(s_row2[:], s_row[:], s_swm[:, 3:4], None, op0=A.mult),
           wait=[("G", wGro)], inc="V")
    p.drain("vector")
    w_rowo = p.cnt["V"]

    t_s1 = tap("s1", [D, TOK])
    t_x2 = tap("x2", [D, TOK])
    for m in range(CH):
        for kk in range(CH):
            p.emit("sync", lambda h, kk=kk, m=m: h.dma_start(
                out=s_wqkv[:, kk * 128:(kk + 1) * 128], in_=tern_tile(3, kk, m)),
                wait=([("T", p.count("T")), ("V", p.count("V"))] if kk == 0 else []),
                inc="L", inc_by=16)
        wLo = p.cnt["L"]
        for th in range(2):
            pd = ps_m0 if th == 0 else ps_m1
            for kk in range(CH):
                p.emit("tensor", lambda h, kk=kk, th=th, pd=pd: h.matmul(
                    pd[:], s_wqkv[:, kk * 128:(kk + 1) * 128],
                    s_ifm[:, kk * TOK + th * 512: kk * TOK + th * 512 + 512],
                    start=(kk == 0), stop=(kk == CH - 1)),
                    wait=([("L", wLo), ("V", p.count("V"))] if kk == 0 else []),
                    inc="T" if kk == CH - 1 else None)
        wTo = p.cnt["T"]
        p.emit("sync", lambda h, m=m: h.dma_start(
            out=s_g1t[:], in_=g1_spill.ap()[m * 128:(m + 1) * 128, :]),
            wait=[("G", w_g1), ("V", p.count("V"))], inc="L", inc_by=16)
        p.emit("sync", lambda h, m=m: h.dma_start(
            out=s_x1t[:], in_=x1_spill.ap()[m * 128:(m + 1) * 128, :]),
            inc="L", inc_by=16)
        wLg = p.cnt["L"]
        p.emit("vector", lambda h: h.tensor_tensor(s_pj[:, 0:512], ps_m0[:], s_row2[:, 0:512], op=A.mult),
               wait=[("T", wTo), ("V", w_rowo), ("L", wLg)])
        p.emit("vector", lambda h: h.tensor_tensor(s_pj[:, 512:1024], ps_m1[:], s_row2[:, 512:1024], op=A.mult),
               inc="V")
        p.drain("vector")
        if t_s1 is not None:
            p.emit("gpsimd", lambda h, m=m: h.dma_start(
                out=t_s1.ap()[m * 128:(m + 1) * 128, :], in_=s_pj[:]),
                wait=[("V", p.cnt["V"])], inc="G", inc_by=16)
            p.wait_only("vector", [("G", p.cnt["G"])])
        p.emit("vector", lambda h: h.tensor_tensor(s_pj[:], s_pj[:], s_g1t[:], op=A.mult))
        p.drain("vector")
        p.emit("vector", lambda h: h.tensor_tensor(s_pj[:], s_pj[:], s_x1t[:], op=A.add), inc="V")
        p.drain("vector")
        p.emit("gpsimd", lambda h, m=m: h.dma_start(
            out=out.ap()[m * 128:(m + 1) * 128, :], in_=s_pj[:]),
            wait=[("V", p.cnt["V"])], inc="G", inc_by=16)
        if t_x2 is not None:
            p.emit("gpsimd", lambda h, m=m: h.dma_start(
                out=t_x2.ap()[m * 128:(m + 1) * 128, :], in_=s_pj[:]),
                inc="G", inc_by=16)
        p.wait_only("vector", [("G", p.cnt["G"])])
    p.wait_only("gpsimd", [("G", p.cnt["G"])])

    p.realize()
    stack.close()
    return nc, ["out"] + ["tap_" + t for t in debug_taps]



def prep_inputs(kw):
    import ml_dtypes
    x = np.asarray(kw["L0_attn_hidden_states"], dtype=np.float32)
    ident = np.eye(128, dtype=np.float32)
    onesb = np.ones((128, 128), dtype=ml_dtypes.bfloat16)
    tribT = np.triu(np.ones((128, 128))).astype(ml_dtypes.bfloat16)
    identb = np.eye(128, dtype=ml_dtypes.bfloat16)
    wzT = np.ascontiguousarray(np.asarray(kw["L0_attn_W_z"], np.float32).T)
    whT = np.ascontiguousarray(np.asarray(kw["L0_attn_W_h"], np.float32).T)
    def cmj(v):
        return np.ascontiguousarray(np.asarray(v, np.float32).reshape(CH, 128).T)
    h0t = cmj(kw["L0_attn_h0"])
    gw_full = np.asarray(kw["L0_ffn_gate_w"], np.float32)
    up_full = np.asarray(kw["L0_ffn_up_w"], np.float32)
    dw_full = np.asarray(kw["L0_ffn_down_w"], np.float32)
    gw_s = [np.ascontiguousarray(gw_full[c * 1024:(c + 1) * 1024]) for c in range(NC)]
    up_s = [np.ascontiguousarray(up_full[c * 1024:(c + 1) * 1024]) for c in range(NC)]
    dw_s = [np.ascontiguousarray(dw_full[c * 256:(c + 1) * 256]) for c in range(NC)]
    wq_f = np.asarray(kw["L1_attn_Wq"], np.float32)
    wk_f = np.asarray(kw["L1_attn_Wk"], np.float32)
    wv_f = np.asarray(kw["L1_attn_Wv"], np.float32)
    wo_f = np.asarray(kw["L1_attn_Wo"], np.float32)
    wq_s = [np.ascontiguousarray(wq_f[c * 256:(c + 1) * 256]) for c in range(NC)]
    wk_s = [np.ascontiguousarray(wk_f[c * 256:(c + 1) * 256]) for c in range(NC)]
    wv_s = [np.ascontiguousarray(wv_f[c * 256:(c + 1) * 256]) for c in range(NC)]
    wo_s = [np.ascontiguousarray(wo_f[c * 256:(c + 1) * 256]) for c in range(NC)]
    wbT = np.ascontiguousarray(np.asarray(kw["L1_ffn_W_base"], np.float32).T)
    bb1 = cmj(kw["L1_ffn_b_base"])
    wrT = np.ascontiguousarray(np.asarray(kw["L1_ffn_W_router"], np.float32).T)
    aaT = np.ascontiguousarray(
        np.asarray(kw["L1_ffn_A_w"], np.float32).transpose(1, 0, 2).reshape(64, D).T)
    ball = np.ascontiguousarray(
        np.asarray(kw["L1_ffn_B_w"], np.float32).transpose(2, 0, 1).reshape(64, D))
    freqs = (1.0 / 10000.0 ** (np.arange(0, 128, 2, dtype=np.float64) / 128.0))
    ii = np.arange(128)[:, None]
    jj = np.arange(512)[None, :]
    dmask = np.concatenate(
        [np.where(jj <= a * 128 + ii, 0.0, NEG) for a in range(4)], axis=1).astype(np.float32)

    in_maps = []
    for c in range(NC):
        b, hf = c // 2, c % 2
        xs = np.ascontiguousarray(x[b, hf * TOK:(hf + 1) * TOK, :].T)
        pos = np.arange(hf * TOK, (hf + 1) * TOK, dtype=np.float64)
        ang = pos[None, :] * freqs[:, None]
        ropeC = np.repeat(np.cos(ang), 2, axis=0).astype(np.float32)
        sgn = np.where(np.arange(128) % 2 == 0, -1.0, 1.0)[:, None]
        ropeS = (np.repeat(np.sin(ang), 2, axis=0) * sgn).astype(np.float32)
        pm0 = np.full((128, CH), 0.0 if hf == 0 else NEG, np.float32)
        cm = np.full((128, NC), NEG, np.float32)
        if hf == 1:
            cm[:, c - 1] = 0.0
        wsel = np.full((128, 1), 1.0 if hf == 0 else 0.0, np.float32)
        m = {
            "x_fm": xs, "wzT": wzT, "whT": whT,
            "gwN": gw_s[c], "upN": up_s[c], "dwN": dw_s[c],
            "wqN": wq_s[c], "wkN": wk_s[c], "wvN": wv_s[c], "woN": wo_s[c],
            "wbT": wbT, "bb1": bb1, "wrT": wrT, "aaT": aaT, "ball": ball,
            "ropeC": ropeC, "ropeS": ropeS, "dmask": dmask,
            "bz": cmj(kw["L0_attn_b_z"]),
            "bh": cmj(kw["L0_attn_b_h"]),
            "lnw": cmj(kw["L0_attn_ln_w"]),
            "lnb": cmj(kw["L0_attn_ln_b"]),
            "h0t": h0t, "pm0": pm0, "cmask": cm, "wsel": wsel,
            "ident": ident, "onesb": onesb, "tribT": tribT, "identb": identb,
        }
        in_maps.append(m)
    return in_maps


def kernel(**kw):
    in_maps = prep_inputs(kw)
    nc, outs = build_nc()
    res = run_bass_kernel_spmd(nc, in_maps, core_ids=list(range(NC)))
    parts = []
    for c in range(NC):
        parts.append(np.asarray(res.results[c]["out"]).T)
    full = np.stack(parts, 0).reshape(B, T, D)
    return full.astype(np.float32)
